# version 04
# baseline (speedup 1.0000x reference)
import math
from contextlib import ExitStack

import numpy as np
import ml_dtypes

import concourse.bass as bass
import concourse.mybir as mybir
from concourse.bass_utils import run_bass_kernel_spmd

F32 = mybir.dt.float32
BF16 = mybir.dt.bfloat16
I32 = mybir.dt.int32
AF = mybir.ActivationFunctionType
ALU = mybir.AluOpType
AX = mybir.AxisListType

BF = ml_dtypes.bfloat16

T = 4096
D = 2048
H = 16
HD = 128
QC = 512
NBLK = 256
MBLK = 16
CI = 64
NIH = 4
WIN = 256
HID = 5632
NEG = -1.0e30
SENT = -2.0e30
SCALE = 1.0 / math.sqrt(128.0)

NT = T // 128
NSUP = T // 512
HT = HID // 128
DT = D // 128

ENGINES = ("pe", "dve", "act", "pool", "sp")
DISABLE = set()
ROPE_CAP = 6


class Res:
    __slots__ = ("w", "rs")

    def __init__(self):
        self.w = None
        self.rs = []

    def for_read(self):
        return [self.w] if self.w is not None else []

    def for_write(self):
        out = list(self.rs)
        if self.w is not None:
            out.append(self.w)
        return out


class Prog:

    def __init__(self, nc, ctx):
        self.nc = nc
        self.ctx = ctx
        self.ops = {e: [] for e in ENGINES}
        self.sems = {}
        self.cnt = {}
        for name in ENGINES:
            self.sems[name] = ctx.enter_context(nc.semaphore("sem_" + name))
            self.cnt[name] = 0
        self.dma_sems = []
        self.last_tok = {e: None for e in ENGINES}
        self.floor = {e: {} for e in ENGINES}

    def _dma_sem(self, name):
        key = "dma_" + name
        if key not in self.sems:
            self.sems[key] = self.ctx.enter_context(
                self.nc.semaphore("sem_" + key))
            self.cnt[key] = 0
            self.dma_sems.append(key)
        return key

    def op(self, eng, fn, reads=(), writes=(), waits=(), dma=None, strict=False):
        wl = {}
        fl = self.floor[eng]

        def add(tok):
            if tok is None:
                return
            sem, val, teng = tok
            if fl.get(sem, -1) >= val:
                return
            if wl.get(sem, -1) < val:
                wl[sem] = val
        for rres in reads:
            for tok in rres.for_read():
                add(tok)
        for wres in writes:
            for tok in wres.for_write():
                add(tok)
        for tok in waits:
            add(tok)
        for sem, val in wl.items():
            assert val <= self.cnt[sem], (sem, val, self.cnt[sem])
            fl[sem] = val
        if dma is None:
            sem, k = eng, 1
        else:
            sem, k = self._dma_sem(dma), 16
        self.cnt[sem] += k
        tok = (sem, self.cnt[sem], None if dma is not None else eng)
        self.ops[eng].append((fn, sorted(wl.items()), sem, k))
        if dma is None:
            self.last_tok[eng] = tok
        for rres in reads:
            rres.rs.append(tok)
        for wres in writes:
            wres.w = tok
            wres.rs = []
        return tok

    def barrier(self):
        toks = [t for t in self.last_tok.values() if t is not None]
        for key in self.dma_sems:
            if self.cnt[key]:
                toks.append((key, self.cnt[key], None))
        for e in ENGINES:
            self.op(e, lambda h: h.nop(), waits=toks)

    def replay(self, block):
        sems = self.sems

        def mk_body(eng_name):
            def run(h):
                for fn, waits, sem, k in self.ops[eng_name]:
                    for wsem, val in waits:
                        h.wait_ge(sems[wsem], val)
                    instr = fn(h)
                    if instr is not None:
                        instr.then_inc(sems[sem], k)
                    else:
                        h.nop().then_inc(sems[sem], k)
            return run
        block.tensor(mk_body("pe"))
        block.vector(mk_body("dve"))
        block.scalar(mk_body("act"))
        block.gpsimd(mk_body("pool"))
        block.sync(mk_body("sp"))


def host_constants():
    c = {}
    c["ident"] = np.eye(128, dtype=BF)
    inv = 1.0 / 10000.0 ** (np.arange(0, 64, 2, dtype=np.float32) / 64.0)
    ang = np.arange(T, dtype=np.float32)[:, None] * inv
    cos = np.cos(ang).astype(np.float32)
    sin = np.sin(ang).astype(np.float32)
    c["cosp"] = np.ascontiguousarray(
        cos.reshape(NT, 128, 32).transpose(1, 0, 2).reshape(128, NT * 32))
    c["sinp"] = np.ascontiguousarray(
        sin.reshape(NT, 128, 32).transpose(1, 0, 2).reshape(128, NT * 32))
    tvec = np.arange(T)
    nvec = np.arange(NBLK)
    causal = nvec[None, :] < (tvec[:, None] // MBLK)
    ca = np.where(causal, 0.0, NEG).astype(np.float32)
    c["causal"] = np.ascontiguousarray(
        ca.reshape(NT, 128, NBLK).transpose(1, 0, 2).reshape(128, NT * NBLK)
    ).astype(BF)
    diff = np.arange(2 * WIN)[None, :] - np.arange(WIN)[:, None]
    win_ok = (diff > 0) & (diff <= WIN)
    first_ok = np.arange(2 * WIN)[None, :] >= WIN
    mask_n = np.where(win_ok, 0.0, NEG).astype(np.float32)
    mask_0 = np.where(win_ok & first_ok, 0.0, NEG).astype(np.float32)
    wm = np.concatenate([mask_n[:128], mask_n[128:], mask_0[:128], mask_0[128:]],
                        axis=1)
    c["winmask"] = np.ascontiguousarray(wm).astype(BF)
    return c


def host_prep(inputs):
    g = lambda k: np.ascontiguousarray(np.asarray(inputs[k]))
    shared = dict(host_constants())
    shared["w_qc"] = g("L0_attn_w_qc")
    shared["w_qup"] = g("L0_attn_w_qup")
    shared["W1"] = np.ascontiguousarray(np.concatenate(
        [g("L0_attn_k_proj_w"), g("L0_attn_v_proj_w"), g("L0_attn_w_w")], axis=1))
    shared["W2"] = np.ascontiguousarray(np.concatenate(
        [g("L0_attn_kvc_w"), g("L0_attn_kvc_wz"), g("L0_attn_idx_c_w"),
         g("L0_attn_idx_c_wz"), g("L0_attn_w_dq")], axis=1))
    shared["w_iuq"] = g("L0_attn_w_iuq")
    shared["biasTkv"] = np.ascontiguousarray(g("L0_attn_kvc_bias").T)
    shared["biasTidx"] = np.ascontiguousarray(g("L0_attn_idx_c_bias").T)
    qn = g("L0_attn_q_norm_w").reshape(-1)
    kn = g("L0_attn_k_norm_w").reshape(-1)
    shared["qn_col"] = np.ascontiguousarray(qn[:, None])
    shared["kn_col"] = np.ascontiguousarray(kn[:, None])
    shared["qn_rep"] = np.ascontiguousarray(np.broadcast_to(qn[None, :], (128, 128)))
    shared["kn_rep"] = np.ascontiguousarray(np.broadcast_to(kn[None, :], (128, 128)))
    shared["group_w"] = g("L0_attn_group_w")
    shared["final_w"] = g("L0_attn_final_w")
    shared["wq_gate"] = g("L0_ffn_wq_gate")
    shared["wq_up"] = g("L0_ffn_wq_up")
    shared["wq_down"] = g("L0_ffn_wq_down")
    shared["sgate"] = np.ascontiguousarray(g("L0_ffn_s_gate").reshape(HT, 128).T)
    shared["sup"] = np.ascontiguousarray(g("L0_ffn_s_up").reshape(HT, 128).T)
    shared["sdown_rep"] = np.ascontiguousarray(
        np.broadcast_to(g("L0_ffn_s_down").reshape(-1)[None, :], (128, D)))
    x = np.asarray(inputs["L0_attn_h"])
    per_core = [{"x": np.ascontiguousarray(x[b])} for b in range(8)]
    return shared, per_core


class Builder:
    def __init__(self, dbg_names=()):
        self.nc = bass.Bass("TRN2", target_bir_lowering=False, debug=False)
        self.ctx = ExitStack()
        self.p = Prog(self.nc, self.ctx)
        self.dbg_names = set(dbg_names)
        self.dram = {}
        self.sb = {}
        self.res = {}

    def din(self, name, shape, dtype):
        self.dram[name] = self.nc.dram_tensor(name, list(shape), dtype,
                                              kind="ExternalInput").ap()
        return self.dram[name]

    def dout(self, name, shape, dtype):
        self.dram[name] = self.nc.dram_tensor(name, list(shape), dtype,
                                              kind="ExternalOutput").ap()
        return self.dram[name]

    def dscr(self, name, shape, dtype):
        self.dram[name] = self.nc.dram_tensor(name, list(shape), dtype,
                                              kind="Internal").ap()
        return self.dram[name]

    def sbuf(self, name, shape, dtype, ctx=None):
        h = (ctx or self.ctx).enter_context(
            self.nc.sbuf_tensor("sb_" + name, list(shape), dtype))
        self.sb[name] = h
        self.res[name] = Res()
        return h

    def psum(self, name, ctx=None):
        h = (ctx or self.ctx).enter_context(
            self.nc.psum_tensor("ps_" + name, [128, 512], F32))
        self.sb[name] = h
        self.res[name] = Res()
        return h

    def r(self, name):
        return self.res[name]

    def dbg(self, name, ap_fn, shape, dtype, reads):
        if name not in self.dbg_names:
            return
        d = self.dout("dbg_" + name, shape, dtype)
        self.p.op("pool", lambda h, d=d, f=ap_fn: h.dma_start(out=d, in_=f()),
                  reads=[self.r(n) for n in reads], dma="dbg")

    def p0_globals(self):
        p, dr = self.p, self.dram
        for i in range(8):
            self.psum("pb%d" % i)
        self.din("x", (T, D), F32)
        for nm, sh, dt in [
            ("ident", (128, 128), BF16), ("cosp", (128, NT * 32), F32),
            ("sinp", (128, NT * 32), F32), ("causal", (128, NT * NBLK), BF16),
            ("winmask", (128, 4 * 512), BF16),
            ("w_qc", (D, QC), F32), ("w_qup", (QC, D), F32),
            ("W1", (D, 260), F32), ("W2", (D, 448), F32),
            ("w_iuq", (64, 256), F32),
            ("biasTkv", (128, 16), F32), ("biasTidx", (64, 16), F32),
            ("qn_col", (128, 1), F32), ("kn_col", (128, 1), F32),
            ("qn_rep", (128, 128), F32), ("kn_rep", (128, 128), F32),
            ("group_w", (4, 512, 512), F32), ("final_w", (D, D), F32),
            ("wq_gate", (HID, D), I32), ("wq_up", (HID, D), I32),
            ("wq_down", (D, HID), I32),
            ("sgate", (128, HT), F32), ("sup", (128, HT), F32),
            ("sdown_rep", (128, D), F32),
        ]:
            self.din(nm, sh, dt)
        self.dout("y", (T, D), F32)
        self.dscr("xnT_scr", (DT, 128, T), BF16)
        self.dscr("s_scr", (NT, 128, D), BF16)
        self.dscr("wgT_scr", (HT, DT, 128, 128), BF16)
        self.dscr("wuT_scr", (HT, DT, 128, 128), BF16)
        self.dscr("wdT_scr", (HT, 128, D), BF16)
        self.dscr("fwT_scr", (DT, 128, D), BF16)

        sb = self.sbuf
        sb("ident", (128, 128), BF16)
        sb("constp", (128, 8), F32)
        sb("qn_col", (128, 1), F32)
        sb("kn_col", (128, 1), F32)
        sb("gains_col", (128, 1), F32)
        sb("ones_col", (128, 1), F32)
        sb("ones_row", (1, 128), F32)

        for nm in ("ident", "qn_col", "kn_col"):
            p.op("sp", lambda h, d=self.sb[nm], s=dr[nm]: h.dma_start(
                out=d[:], in_=s[:, :]), writes=[self.r(nm)], dma=nm)
        cp = self.sb["constp"]
        for slot, val in [(0, 1e-6), (1, 1e-5), (2, -0.5), (3, 1.0 / D),
                          (4, 1.0 / 128.0), (5, 0.0), (6, 1e30), (7, -1e30)]:
            p.op("dve", lambda h, s=slot, v=val: h.memset(cp[:, s:s + 1], v),
                 writes=[self.r("constp")])
        p.op("dve", lambda h: h.memset(self.sb["ones_col"][:], 1.0),
             writes=[self.r("ones_col")])
        p.op("dve", lambda h: h.memset(self.sb["ones_row"][:], 1.0),
             writes=[self.r("ones_row")])
        p.op("dve", lambda h: h.tensor_tensor(
            out=self.sb["gains_col"][:], in0=self.sb["qn_col"][:],
            in1=self.sb["kn_col"][:], op=ALU.mult),
            reads=[self.r("qn_col"), self.r("kn_col")],
            writes=[self.r("gains_col")])
        p.op("dve", lambda h: h.tensor_scalar(
            out=self.sb["gains_col"][:], in0=self.sb["gains_col"][:],
            scalar1=SCALE, scalar2=None, op0=ALU.mult),
            writes=[self.r("gains_col")], strict=True)

    def p0_allocA(self, ctx):
        p, dr = self.p, self.dram
        sb = self.sbuf
        sb("cos", (128, NT * 32), F32, ctx)
        sb("sin", (128, NT * 32), F32, ctx)
        sb("q_cT", (128, 4 * T), BF16, ctx)
        sb("khatT", (128, T), BF16, ctx)
        sb("v_sb", (128, NT * 128), BF16, ctx)
        sb("qiT", (128, 2 * T), BF16, ctx)
        sb("wih", (128, NT * 4), F32, ctx)
        sb("ckvT", (128, NBLK), F32, ctx)
        sb("kcompT", (64, NBLK), F32, ctx)
        sb("kcomp_bf", (128, NBLK), BF16, ctx)
        sb("skv_ss", (128, NBLK), BF16, ctx)
        sb("skv_pv", (128, 2 * 128), BF16, ctx)
        sb("mask_sb", (128, NT * NBLK), BF16, ctx)
        for nm, srcn in (("cos", "cosp"), ("sin", "sinp")):
            p.op("sp", lambda h, d=self.sb[nm], s=dr[srcn]: h.dma_start(
                out=d[:], in_=s[:, :]), writes=[self.r(nm)], dma=nm)

    def p0_weights(self, ctx):
        p, dr = self.p, self.dram
        sb = self.sbuf
        sb("wqs_rep", (128, 128), F32, ctx)
        sb("wk_rep", (128, 128), F32, ctx)
        sb("biasTkv", (128, 16), F32, ctx)
        sb("biasTidx", (64, 16), F32, ctx)
        sb("w_iuq", (64, 256), BF16, ctx)
        sb("wstage", (128, 512), F32, ctx)
        sb("w_qc_bf", (128, DT * QC), BF16, ctx)
        sb("W1_bf", (128, DT * 260), BF16, ctx)
        sb("W2_bf", (128, DT * 448), BF16, ctx)

        for nm in ("biasTkv", "biasTidx"):
            p.op("sp", lambda h, d=self.sb[nm], s=dr[nm]: h.dma_start(
                out=d[:], in_=s[:, :]), writes=[self.r(nm)], dma=nm)

        def conv(dstname, col, src_ap, ncols):
            st = self.sb["wstage"]
            p.op("sp", lambda h, s=src_ap, n=ncols: h.dma_start(
                out=st[:, :n], in_=s), writes=[self.r("wstage")], dma="wstage")
            p.op("dve", lambda h, d=self.sb[dstname], c=col, n=ncols:
                 h.tensor_copy(out=d[:, c:c + n], in_=st[:, :n]),
                 reads=[self.r("wstage")], writes=[self.r(dstname)])

        for i in range(DT):
            conv("w_qc_bf", i * QC, dr["w_qc"][i * 128:(i + 1) * 128, :], QC)
            conv("W1_bf", i * 260, dr["W1"][i * 128:(i + 1) * 128, :], 260)
            conv("W2_bf", i * 448, dr["W2"][i * 128:(i + 1) * 128, :], 448)
        st = self.sb["wstage"]
        p.op("sp", lambda h: h.dma_start(out=st[:64, :256], in_=dr["w_iuq"][:, :]),
             writes=[self.r("wstage")], dma="wstage")
        p.op("dve", lambda h: h.tensor_copy(out=self.sb["w_iuq"][:, :],
                                            in_=st[:64, :256]),
             reads=[self.r("wstage")], writes=[self.r("w_iuq")])
        p.op("sp", lambda h: h.dma_start(out=self.sb["wqs_rep"][:],
                                         in_=dr["qn_rep"][:, :]),
             writes=[self.r("wqs_rep")], dma="wqs_rep")
        p.op("dve", lambda h: h.tensor_scalar(
            out=self.sb["wqs_rep"][:], in0=self.sb["wqs_rep"][:],
            scalar1=SCALE, scalar2=None, op0=ALU.mult),
            reads=[self.r("wqs_rep")], writes=[self.r("wqs_rep")])
        p.op("sp", lambda h: h.dma_start(out=self.sb["wk_rep"][:],
                                         in_=dr["kn_rep"][:, :]),
             writes=[self.r("wk_rep")], dma="wk_rep")

    def p1_alloc(self, ctx):
        sb = self.sbuf
        sb("xstage", (128, D), F32, ctx)
        sb("lnstat", (128, 16), F32, ctx)
        sb("xn_tm", (128, 4 * D), BF16, ctx)
        sb("xnT", (128, DT * 512), BF16, ctx)
        sb("kstage", (128, 4 * 128), F32, ctx)
        sb("khat_tm", (128, 4 * 128), BF16, ctx)
        sb("dq_bf", (64, 512), BF16, ctx)
        sb("czkv", (128, 1024), F32, ctx)
        sb("czidx", (64, 1024), F32, ctx)
        sb("cmp_scr", (128, 512), F32, ctx)
        sb("cmp_red", (128, 64), F32, ctx)
        sb("rope_scr", (128, 128), F32, ctx)
        sb("skv_stg", (128, 256), BF16, ctx)

    def p1(self, u):
        p, sb, dr, r = self.p, self.sb, self.dram, self.r
        t0 = u * 512
        cp = sb["constp"]
        xst = sb["xstage"]

        for s in range(4):
            p.op("pool", lambda h, s=s: h.dma_start(
                out=xst[:],
                in_=dr["x"][t0 + s * 128:t0 + (s + 1) * 128, :]),
                writes=[r("xstage")], dma="xstage")
            xs = xst[:, 0:D]
            c_mean = sb["lnstat"][:, 4 * s:4 * s + 1]
            c_rv = sb["lnstat"][:, 4 * s + 1:4 * s + 2]
            c_t = sb["lnstat"][:, 4 * s + 2:4 * s + 3]
            p.op("dve", lambda h, xs=xs, c=c_mean: h.reduce_sum(
                out=c, in_=xs, axis=AX.X),
                reads=[r("xstage")], writes=[r("lnstat")])
            p.op("act", lambda h, xs=xs, s=s, c=c_rv: h.activation(
                out=sb["xn_tm"][:, s * D:(s + 1) * D], in_=xs, func=AF.Square,
                accum_out=c),
                reads=[r("xstage")], writes=[r("xn_tm"), r("lnstat")])
            p.op("dve", lambda h, c=c_mean: h.tensor_scalar(
                out=c, in0=c, scalar1=1.0 / D, scalar2=None, op0=ALU.mult),
                reads=[r("lnstat")], writes=[r("lnstat")], strict=True)
            p.op("dve", lambda h, c=c_t, m=c_mean: h.tensor_tensor(
                out=c, in0=m, in1=m, op=ALU.mult),
                reads=[r("lnstat")], writes=[r("lnstat")], strict=True)
            p.op("dve", lambda h, c=c_rv, t_=c_t: h.scalar_tensor_tensor(
                out=c, in0=c, scalar=1.0 / D, in1=t_, op0=ALU.mult,
                op1=ALU.subtract),
                reads=[r("lnstat")], writes=[r("lnstat")], strict=True)
            p.op("act", lambda h, c=c_rv: h.activation(
                out=c, in_=c, func=AF.Ln, bias=cp[:, 1:2], scale=1.0),
                reads=[r("lnstat"), r("constp")], writes=[r("lnstat")])
            p.op("act", lambda h, c=c_rv: h.activation(
                out=c, in_=c, func=AF.Exp, bias=0.0, scale=cp[:, 2:3]),
                reads=[r("lnstat"), r("constp")], writes=[r("lnstat")],
                strict=True)
            p.op("dve", lambda h, c=c_t, m=c_mean, v=c_rv: h.scalar_tensor_tensor(
                out=c, in0=m, scalar=-1.0, in1=v, op0=ALU.mult, op1=ALU.mult),
                reads=[r("lnstat")], writes=[r("lnstat")])
            p.op("act", lambda h, xs=xs, s=s, v=c_rv, b=c_t: h.activation(
                out=sb["xn_tm"][:, s * D:(s + 1) * D], in_=xs, func=AF.Identity,
                bias=b, scale=v),
                reads=[r("lnstat"), r("xstage")], writes=[r("xn_tm")])

        for dt in range(DT):
            bn = "pb%d" % (dt % 2)
            bank = sb[bn]

            def tmm(h, dt=dt, bank=bank):
                last = None
                for s in range(4):
                    last = h.matmul(bank[:, s * 128:(s + 1) * 128],
                                    sb["xn_tm"][:, s * D + dt * 128:
                                                s * D + (dt + 1) * 128],
                                    sb["ident"][:], start=True, stop=True)
                return last
            p.op("pe", tmm, reads=[r("xn_tm"), r("ident")], writes=[r(bn)])
            p.op("dve", lambda h, dt=dt, bank=bank: h.tensor_copy(
                out=sb["xnT"][:, dt * 512:(dt + 1) * 512], in_=bank[:]),
                reads=[r(bn)], writes=[r("xnT")])
        p.op("pool", lambda h: h.dma_start(
            out=dr["xnT_scr"][:, :, t0:t0 + 512].rearrange("dt p t -> p dt t"),
            in_=sb["xnT"][:].rearrange("p (dt t) -> p dt t", dt=DT)),
            reads=[r("xnT")], dma="xnT_sp")

        for mt in range(4):
            bank = sb["pb2"]

            def qcmm(h, mt=mt, bank=bank):
                last = None
                for kt in range(DT):
                    last = h.matmul(
                        bank[:],
                        sb["w_qc_bf"][:, kt * QC + mt * 128:
                                      kt * QC + (mt + 1) * 128],
                        sb["xnT"][:, kt * 512:(kt + 1) * 512],
                        start=(kt == 0), stop=(kt == DT - 1))
                return last
            p.op("pe", qcmm, reads=[r("w_qc_bf"), r("xnT")], writes=[r("pb2")])
            p.op("dve", lambda h, mt=mt, bank=bank: h.tensor_copy(
                out=sb["q_cT"][:, mt * T + t0: mt * T + t0 + 512], in_=bank[:]),
                reads=[r("pb2")], writes=[r("q_cT")])

        for s in range(4):
            bank = sb["pb3"]
            st_idx = u * 4 + s

            def w1mm(h, s=s, bank=bank):
                last = None
                for kt in range(DT):
                    last = h.matmul(
                        bank[:, :260],
                        sb["xnT"][:, kt * 512 + s * 128:kt * 512 + (s + 1) * 128],
                        sb["W1_bf"][:, kt * 260:(kt + 1) * 260],
                        start=(kt == 0), stop=(kt == DT - 1))
                return last
            p.op("pe", w1mm, reads=[r("W1_bf"), r("xnT")], writes=[r("pb3")])
            p.op("act", lambda h, s=s, bank=bank: h.copy(
                out=sb["kstage"][:, s * 128:(s + 1) * 128], in_=bank[:, 0:128]),
                reads=[r("pb3")], writes=[r("kstage")])
            p.op("act", lambda h, bank=bank, st=st_idx: h.copy(
                out=sb["v_sb"][:, st * 128:(st + 1) * 128], in_=bank[:, 128:256]),
                reads=[r("pb3")], writes=[r("v_sb")])
            p.op("act", lambda h, bank=bank, st=st_idx: h.copy(
                out=sb["wih"][:, st * 4:(st + 1) * 4], in_=bank[:, 256:260]),
                reads=[r("pb3")], writes=[r("wih")])

        for s in range(4):
            st_idx = u * 4 + s
            ks = sb["kstage"][:, s * 128:(s + 1) * 128]
            c_rs = sb["lnstat"][:, s:s + 1]
            p.op("act", lambda h, ks=ks, c=c_rs: h.activation(
                out=sb["xn_tm"][:, 0:128], in_=ks, func=AF.Square, accum_out=c),
                reads=[r("kstage")], writes=[r("xn_tm"), r("lnstat")])
            p.op("act", lambda h, c=c_rs: h.activation(
                out=c, in_=c, func=AF.Ln, bias=cp[:, 0:1], scale=cp[:, 4:5]),
                reads=[r("lnstat"), r("constp")], writes=[r("lnstat")],
                strict=True)
            p.op("act", lambda h, c=c_rs: h.activation(
                out=c, in_=c, func=AF.Exp, bias=0.0, scale=cp[:, 2:3]),
                reads=[r("lnstat"), r("constp")], writes=[r("lnstat")],
                strict=True)
            p.op("act", lambda h, ks=ks, c=c_rs: h.activation(
                out=ks, in_=ks, func=AF.Identity, bias=0.0, scale=c),
                reads=[r("lnstat")], writes=[r("kstage")], strict=True)
            p.op("dve", lambda h, ks=ks: h.tensor_tensor(
                out=ks, in0=ks, in1=sb["wk_rep"][:], op=ALU.mult),
                reads=[r("wk_rep"), r("kstage")], writes=[r("kstage")],
                strict=True)
            co = sb["cos"][:, st_idx * 32:(st_idx + 1) * 32]
            si = sb["sin"][:, st_idx * 32:(st_idx + 1) * 32]
            x1 = ks[:, 0:32]
            x2 = ks[:, 32:64]
            rs_ = sb["rope_scr"]
            p.op("dve", lambda h, x1=x1, co=co: h.tensor_tensor(
                out=rs_[:, 0:32], in0=x1, in1=co, op=ALU.mult),
                reads=[r("kstage"), r("cos")], writes=[r("rope_scr")],
                strict=True)
            p.op("dve", lambda h, x2=x2, co=co: h.tensor_tensor(
                out=rs_[:, 32:64], in0=x2, in1=co, op=ALU.mult),
                reads=[r("kstage"), r("cos")], writes=[r("rope_scr")],
                strict=True)
            p.op("dve", lambda h, x2=x2, si=si: h.scalar_tensor_tensor(
                out=rs_[:, 64:96], in0=x2, scalar=-1.0, in1=si,
                op0=ALU.mult, op1=ALU.mult),
                reads=[r("kstage"), r("sin")], writes=[r("rope_scr")],
                strict=True)
            p.op("dve", lambda h, x1=x1, si=si: h.tensor_tensor(
                out=rs_[:, 96:128], in0=x1, in1=si, op=ALU.mult),
                reads=[r("kstage"), r("sin")], writes=[r("rope_scr")],
                strict=True)
            p.op("dve", lambda h, x1=x1: h.tensor_tensor(
                out=x1, in0=rs_[:, 0:32], in1=rs_[:, 64:96], op=ALU.add),
                reads=[r("rope_scr")], writes=[r("kstage")], strict=True)
            p.op("dve", lambda h, x2=x2: h.tensor_tensor(
                out=x2, in0=rs_[:, 96:128], in1=rs_[:, 32:64], op=ALU.add),
                reads=[r("rope_scr")], writes=[r("kstage")], strict=True)
            p.op("dve", lambda h, ks=ks, s=s: h.tensor_tensor(
                out=sb["khat_tm"][:, s * 128:(s + 1) * 128], in0=ks,
                in1=sb["wqs_rep"][:], op=ALU.mult),
                reads=[r("kstage"), r("wqs_rep")], writes=[r("khat_tm")],
                strict=True)
        bank = sb["pb4"]

        def ktr(h, bank=bank):
            last = None
            for s in range(4):
                last = h.matmul(bank[:, s * 128:(s + 1) * 128],
                                sb["khat_tm"][:, s * 128:(s + 1) * 128],
                                sb["ident"][:], start=True, stop=True)
            return last
        p.op("pe", ktr, reads=[r("khat_tm"), r("ident")], writes=[r("pb4")])
        p.op("dve", lambda h, bank=bank: h.tensor_copy(
            out=sb["khatT"][:, t0:t0 + 512], in_=bank[:]),
            reads=[r("pb4")], writes=[r("khatT")])

        for mt, (c0, cn) in enumerate([(0, 128), (128, 128), (256, 128),
                                       (384, 64)]):
            bn = "pb%d" % (5 + mt % 2)
            bank = sb[bn]

            def w2mm(h, c0=c0, cn=cn, bank=bank):
                last = None
                for kt in range(DT):
                    last = h.matmul(
                        bank[:cn, :512],
                        sb["W2_bf"][:, kt * 448 + c0:kt * 448 + c0 + cn],
                        sb["xnT"][:, kt * 512:(kt + 1) * 512],
                        start=(kt == 0), stop=(kt == DT - 1))
                return last
            p.op("pe", w2mm, reads=[r("W2_bf"), r("xnT")], writes=[r(bn)])
            if mt == 0:
                p.op("act", lambda h, bank=bank: h.copy(
                    out=sb["czkv"][:, 0:512], in_=bank[:, :512]),
                    reads=[r(bn)], writes=[r("czkv")])
            elif mt == 1:
                p.op("dve", lambda h, bank=bank: h.tensor_tensor(
                    out=sb["czkv"][:, 512:1024].rearrange("p (b m) -> p b m",
                                                          m=16),
                    in0=bank[:, :512].rearrange("p (b m) -> p b m", m=16),
                    in1=sb["biasTkv"][:].unsqueeze(1).to_broadcast([128, 32, 16]),
                    op=ALU.add),
                    reads=[r(bn), r("biasTkv")], writes=[r("czkv")])
            elif mt == 2:
                p.op("dve", lambda h, bank=bank: h.tensor_copy(
                    out=sb["czidx"][:, 0:512], in_=bank[0:64, :512]),
                    reads=[r(bn)], writes=[r("czidx")])
                p.op("dve", lambda h, bank=bank: h.tensor_tensor(
                    out=sb["czidx"][:, 512:1024].rearrange("p (b m) -> p b m",
                                                           m=16),
                    in0=bank[64:128, :512].rearrange("p (b m) -> p b m", m=16),
                    in1=sb["biasTidx"][:].unsqueeze(1).to_broadcast([64, 32, 16]),
                    op=ALU.add),
                    reads=[r(bn), r("biasTidx")], writes=[r("czidx")])
            else:
                p.op("dve", lambda h, bank=bank: h.tensor_copy(
                    out=sb["dq_bf"][:], in_=bank[0:64, :512]),
                    reads=[r(bn)], writes=[r("dq_bf")])
                for mt2 in range(2):
                    bank2 = sb["pb7"]
                    p.op("pe", lambda h, mt2=mt2, bank2=bank2: h.matmul(
                        bank2[:, :512],
                        sb["w_iuq"][:, mt2 * 128:(mt2 + 1) * 128],
                        sb["dq_bf"][:], start=True, stop=True),
                        reads=[r("w_iuq"), r("dq_bf")], writes=[r("pb7")])
                    p.op("dve", lambda h, mt2=mt2, bank2=bank2: h.tensor_copy(
                        out=sb["qiT"][:, mt2 * T + t0:mt2 * T + t0 + 512],
                        in_=bank2[:, :512]),
                        reads=[r("pb7")], writes=[r("qiT")])

        def compress(cz_name, parts, dstname):
            cz = sb[cz_name]
            czr = r(cz_name)
            scr = sb["cmp_scr"]
            red = sb["cmp_red"]
            c_ap = cz[:parts, 0:512]
            z_ap = cz[:parts, 512:1024]
            p.op("dve", lambda h: h.reduce_max(
                out=red[:parts, 0:32],
                in_=z_ap.rearrange("p (b m) -> p b m", m=16),
                axis=AX.X, negate=True),
                reads=[czr], writes=[r("cmp_red")])
            p.op("dve", lambda h: h.tensor_tensor(
                out=scr[:parts, 0:512].rearrange("p (b m) -> p b m", m=16),
                in0=z_ap.rearrange("p (b m) -> p b m", m=16),
                in1=red[:parts, 0:32].unsqueeze(2).to_broadcast([parts, 32, 16]),
                op=ALU.add),
                reads=[czr, r("cmp_red")], writes=[r("cmp_scr")], strict=True)
            p.op("act", lambda h: h.activation(
                out=scr[:parts, 0:512], in_=scr[:parts, 0:512], func=AF.Exp),
                reads=[r("cmp_scr")], writes=[r("cmp_scr")])
            p.op("dve", lambda h: h.reduce_sum(
                out=red[:parts, 32:64],
                in_=scr[:parts, 0:512].rearrange("p (b m) -> p b m", m=16),
                axis=AX.X),
                reads=[r("cmp_scr")], writes=[r("cmp_red")])
            p.op("dve", lambda h: h.tensor_tensor(
                out=scr[:parts, 0:512], in0=scr[:parts, 0:512], in1=c_ap,
                op=ALU.mult),
                reads=[r("cmp_scr"), czr], writes=[r("cmp_scr")], strict=True)
            p.op("dve", lambda h: h.reduce_sum(
                out=red[:parts, 0:32],
                in_=scr[:parts, 0:512].rearrange("p (b m) -> p b m", m=16),
                axis=AX.X),
                reads=[r("cmp_scr")], writes=[r("cmp_red")], strict=True)
            p.op("dve", lambda h: h.reciprocal(
                out=red[:parts, 32:64], in_=red[:parts, 32:64]),
                reads=[r("cmp_red")], writes=[r("cmp_red")], strict=True)
            p.op("dve", lambda h: h.tensor_tensor(
                out=sb[dstname][:parts, u * 32:(u + 1) * 32],
                in0=red[:parts, 0:32], in1=red[:parts, 32:64], op=ALU.mult),
                reads=[r("cmp_red")], writes=[r(dstname)], strict=True)
        compress("czkv", 128, "ckvT")
        compress("czidx", 64, "kcompT")

    def p1_post(self):
        p, sb, r = self.p, self.sb, self.r
        cp = sb["constp"]
        scr = sb["cmp_scr"]
        p.op("dve", lambda h: h.tensor_tensor(
            out=scr[:, 0:256], in0=sb["ckvT"][:], in1=sb["ckvT"][:],
            op=ALU.mult),
            reads=[r("ckvT")], writes=[r("cmp_scr")])
        bank = sb["pb0"]
        p.op("pe", lambda h: h.matmul(bank[:1, 0:256], sb["ones_col"][:],
                                      scr[:, 0:256], start=True, stop=True),
             reads=[r("ones_col"), r("cmp_scr")], writes=[r("pb0")])
        p.op("act", lambda h: h.activation(
            out=scr[:1, 256:512], in_=bank[:1, 0:256], func=AF.Ln,
            bias=cp[:1, 0:1], scale=cp[:1, 4:5]),
            reads=[r("pb0"), r("constp")], writes=[r("cmp_scr")])
        p.op("act", lambda h: h.activation(
            out=scr[:1, 256:512], in_=scr[:1, 256:512], func=AF.Exp,
            bias=0.0, scale=cp[:1, 2:3]),
            reads=[r("cmp_scr"), r("constp")], writes=[r("cmp_scr")],
            strict=True)
        bank1 = sb["pb1"]
        p.op("pe", lambda h: h.matmul(bank1[:, 0:256], sb["ones_row"][:],
                                      scr[:1, 256:512], start=True, stop=True),
             reads=[r("ones_row"), r("cmp_scr")], writes=[r("pb1")])
        p.op("dve", lambda h: h.tensor_tensor(
            out=scr[:, 0:256], in0=sb["ckvT"][:], in1=bank1[:, 0:256],
            op=ALU.mult),
            reads=[r("ckvT"), r("pb1")], writes=[r("cmp_scr")])
        p.op("dve", lambda h: h.tensor_scalar(
            out=sb["skv_ss"][:], in0=scr[:, 0:256],
            scalar1=sb["gains_col"][:, 0:1], scalar2=None, op0=ALU.mult),
            reads=[r("cmp_scr"), r("gains_col")], writes=[r("skv_ss")])
        p.op("dve", lambda h: h.tensor_copy(
            out=sb["skv_stg"][:], in_=scr[:, 0:256]),
            reads=[r("cmp_scr")], writes=[r("skv_stg")])
        bank2 = sb["pb2"]

        def tr2(h, bank2=bank2):
            last = None
            for i in range(2):
                last = h.matmul(
                    bank2[:, i * 128:(i + 1) * 128],
                    sb["skv_stg"][:, i * 128:(i + 1) * 128],
                    sb["ident"][:], start=True, stop=True)
            return last
        p.op("pe", tr2, reads=[r("skv_stg"), r("ident")], writes=[r("pb2")])
        p.op("dve", lambda h, bank2=bank2: h.tensor_copy(
            out=sb["skv_pv"][:], in_=bank2[:, 0:256]),
            reads=[r("pb2")], writes=[r("skv_pv")])
        p.op("dve", lambda h: h.tensor_copy(out=sb["kcomp_bf"][0:64, :],
                                            in_=sb["kcompT"][:]),
             reads=[r("kcompT")], writes=[r("kcomp_bf")])
        p.op("pool", lambda h: h.dma_start(out=sb["kcomp_bf"][64:128, :],
                                           in_=sb["kcomp_bf"][0:64, :]),
             reads=[r("kcomp_bf")], writes=[r("kcomp_bf")], dma="kcompdup")

    def p2_alloc(self, ctx):
        p, dr = self.p, self.dram
        sb = self.sbuf
        sb("causal", (128, NT * NBLK), BF16, ctx)
        sb("fw_f32", (128, D), F32, ctx)
        sb("fw_bf", (128, D), BF16, ctx)
        sb("iscc", (128, NBLK), F32, ctx)
        sb("isc2", (128, NBLK), F32, ctx)
        sb("m8", (128, 8), F32, ctx)
        p.op("sp", lambda h: h.dma_start(out=self.sb["causal"][:],
                                         in_=dr["causal"][:, :]),
             writes=[self.r("causal")], dma="causal")

    def p2_fwconv(self, kt):
        p, sb, dr, r = self.p, self.sb, self.dram, self.r
        p.op("sp", lambda h, kt=kt: h.dma_start(
            out=sb["fw_f32"][:], in_=dr["final_w"][kt * 128:(kt + 1) * 128, :]),
            writes=[r("fw_f32")], dma="fw_f32")
        p.op("dve", lambda h: h.tensor_copy(out=sb["fw_bf"][:],
                                            in_=sb["fw_f32"][:]),
             reads=[r("fw_f32")], writes=[r("fw_bf")])
        p.op("pool", lambda h, kt=kt: h.dma_start(
            out=self.dram["fwT_scr"][kt, :, :], in_=sb["fw_bf"][:]),
            reads=[r("fw_bf")], dma="fwbf_st")

    def p2(self, st):
        p, sb, r = self.p, self.sb, self.r
        t0 = st * 128
        cp = sb["constp"]
        for ih in range(NIH):
            bn = "pb%d" % (ih % 2)
            bank = sb[bn]
            ptile = ih // 2
            row0 = (ih % 2) * 64
            p.op("pe", lambda h, bank=bank, ptile=ptile, row0=row0: h.matmul(
                bank[:, 0:NBLK],
                sb["qiT"][row0:row0 + 64, ptile * T + t0:ptile * T + t0 + 128],
                sb["kcomp_bf"][row0:row0 + 64, :], start=True, stop=True),
                reads=[r("qiT"), r("kcomp_bf")], writes=[r(bn)])
            wcol = sb["wih"][:, st * 4 + ih:st * 4 + ih + 1]
            if ih == 0:
                p.op("dve", lambda h, bank=bank, wcol=wcol: h.scalar_tensor_tensor(
                    out=sb["iscc"][:], in0=bank[:, 0:NBLK], scalar=0.0,
                    in1=wcol.to_broadcast([128, NBLK]), op0=ALU.max, op1=ALU.mult),
                    reads=[r(bn), r("wih")], writes=[r("iscc")])
            else:
                p.op("dve", lambda h, bank=bank, wcol=wcol: h.scalar_tensor_tensor(
                    out=sb["isc2"][:], in0=bank[:, 0:NBLK], scalar=0.0,
                    in1=wcol.to_broadcast([128, NBLK]), op0=ALU.max, op1=ALU.mult),
                    reads=[r(bn), r("wih")], writes=[r("isc2")])
                p.op("dve", lambda h: h.tensor_tensor(
                    out=sb["iscc"][:], in0=sb["iscc"][:], in1=sb["isc2"][:],
                    op=ALU.add),
                    reads=[r("isc2")], writes=[r("iscc")], strict=True)
        p.op("dve", lambda h, st=st: h.tensor_tensor(
            out=sb["iscc"][:], in0=sb["iscc"][:],
            in1=sb["causal"][:, st * NBLK:(st + 1) * NBLK], op=ALU.add),
            reads=[r("causal")], writes=[r("iscc")], strict=True)
        for rnd in range(8):
            p.op("dve", lambda h: h.max(out=sb["m8"][:], in_=sb["iscc"][:]),
                 reads=[r("iscc")], writes=[r("m8")], strict=True)
            p.op("dve", lambda h: h.match_replace(
                out=sb["iscc"][:], in_to_replace=sb["m8"][:],
                in_values=sb["iscc"][:], imm_value=SENT),
                reads=[r("m8")], writes=[r("iscc")], strict=True)
        p.op("dve", lambda h: h.tensor_scalar(
            out=sb["isc2"][:], in0=sb["iscc"][:], scalar1=SENT, scalar2=None,
            op0=ALU.is_equal),
            reads=[r("iscc")], writes=[r("isc2")], strict=True)
        p.op("act", lambda h: h.activation(
            out=sb["isc2"][:], in_=sb["isc2"][:], func=AF.Identity,
            bias=cp[:, 7:8], scale=cp[:, 6:7]),
            reads=[r("isc2"), r("constp")], writes=[r("isc2")])
        p.op("dve", lambda h, st=st: h.tensor_tensor(
            out=sb["mask_sb"][:, st * NBLK:(st + 1) * NBLK], in0=sb["isc2"][:],
            in1=sb["causal"][:, st * NBLK:(st + 1) * NBLK], op=ALU.add),
            reads=[r("isc2"), r("causal")], writes=[r("mask_sb")])

    def p3_alloc(self, ctx):
        p, dr = self.p, self.dram
        sb = self.sbuf
        sb("winmask", (128, 4 * 512), BF16, ctx)
        sb("w_qup_bf", (128, 4 * D), BF16, ctx)
        sb("groupw_bf", (128, 16 * 512), BF16, ctx)
        sb("wstage3", (128, 512), F32, ctx)
        sb("qpre", (128, 512), F32, ctx)
        sb("qrt", (128, 512), F32, ctx)
        sb("qhat_tm", (128, 512), BF16, ctx)
        sb("qstat", (128, 8), F32, ctx)
        sb("qhatT", (128, 16 * 512), BF16, ctx)
        sb("attn_bf", (128, 16 * 512), BF16, ctx)
        sb("goT", (128, 16 * 512), BF16, ctx)
        sb("ssb", (128, 256), F32, ctx)
        sb("wsb", (128, 512), F32, ctx)
        sb("nmax", (128, 2), F32, ctx)
        sb("den", (128, 2), F32, ctx)
        sb("pbf", (128, 256), BF16, ctx)
        sb("pwb", (128, 512), BF16, ctx)
        sb("ptb", (128, 256), BF16, ctx)
        sb("ptw", (128, 512), BF16, ctx)
        sb("tmps", (128, 128), F32, ctx)
        sb("sstage", (128, D), BF16, ctx)
        sb("fwstage", (128, 2 * D), BF16, ctx)
        self.res["fwstage0"] = Res()
        self.res["fwstage1"] = Res()
        p.op("sp", lambda h: h.dma_start(out=self.sb["winmask"][:],
                                         in_=dr["winmask"][:, :]),
             writes=[self.r("winmask")], dma="winmask")

        def conv(dstname, col, src_ap, ncols):
            st = self.sb["wstage3"]
            p.op("sp", lambda h, s=src_ap, n=ncols: h.dma_start(
                out=st[:, :n], in_=s), writes=[self.r("wstage3")], dma="wstage3")
            p.op("dve", lambda h, d=self.sb[dstname], c=col, n=ncols:
                 h.tensor_copy(out=d[:, c:c + n], in_=st[:, :n]),
                 reads=[self.r("wstage3")], writes=[self.r(dstname)])
        for i in range(4):
            for c4 in range(4):
                conv("w_qup_bf", i * D + c4 * 512,
                     dr["w_qup"][i * 128:(i + 1) * 128, c4 * 512:(c4 + 1) * 512],
                     512)
        for gi in range(4):
            for isub in range(4):
                conv("groupw_bf", (gi * 4 + isub) * 512,
                     dr["group_w"][gi, isub * 128:(isub + 1) * 128, :], 512)

    def p3(self, u):
        p, sb, dr, r = self.p, self.sb, self.dram, self.r
        t0 = u * 512
        cp = sb["constp"]

        for tsub in range(4) if "qhat" not in DISABLE else []:
            gt0 = t0 + tsub * 128
            for hg in range(4):
                bank = sb["pb0"]

                def qmm(h, gt0=gt0, hg=hg, bank=bank):
                    last = None
                    for mt in range(4):
                        last = h.matmul(
                            bank[:],
                            sb["q_cT"][:, mt * T + gt0:mt * T + gt0 + 128],
                            sb["w_qup_bf"][:, mt * D + hg * 512:
                                           mt * D + (hg + 1) * 512],
                            start=(mt == 0), stop=(mt == 3))
                    return last
                p.op("pe", qmm, reads=[r("q_cT"), r("w_qup_bf")],
                     writes=[r("pb0")])
                p.op("act", lambda h, bank=bank: h.copy(
                    out=sb["qpre"][:], in_=bank[:]),
                    reads=[r("pb0")], writes=[r("qpre")])
                st_idx = gt0 // 128
                co = sb["cos"][:, st_idx * 32:(st_idx + 1) * 32]
                si = sb["sin"][:, st_idx * 32:(st_idx + 1) * 32]
                cob = co.unsqueeze(1).to_broadcast([128, 4, 32])
                sib = si.unsqueeze(1).to_broadcast([128, 4, 32])
                x1 = sb["qpre"][:].rearrange("p (h c) -> p h c", h=4)[:, :, 0:32]
                x2 = sb["qpre"][:].rearrange("p (h c) -> p h c", h=4)[:, :, 32:64]
                qr = sb["qrt"]
                a_ = qr[:, 0:128].rearrange("p (h c) -> p h c", h=4)
                b_ = qr[:, 128:256].rearrange("p (h c) -> p h c", h=4)
                c_ = qr[:, 256:384].rearrange("p (h c) -> p h c", h=4)
                d_ = qr[:, 384:512].rearrange("p (h c) -> p h c", h=4)
                o1 = x1
                o2 = x2
                if "qrope" not in DISABLE:
                    for dst, s0, s1 in ((a_, x1, cob), (b_, x2, sib),
                                        (c_, x1, sib), (d_, x2, cob)):
                        p.op("dve", lambda h, dst=dst, s0=s0, s1=s1:
                             h.tensor_tensor(out=dst, in0=s0, in1=s1,
                                             op=ALU.mult),
                             reads=[r("qpre"), r("cos"), r("sin")],
                             writes=[r("qrt")], strict=True)
                    p.op("dve", lambda h, a_=a_, b_=b_, o1=o1: h.tensor_tensor(
                        out=o1, in0=a_, in1=b_, op=ALU.subtract),
                        reads=[r("qrt")], writes=[r("qpre")], strict=True)
                    p.op("dve", lambda h, c_=c_, d_=d_, o2=o2: h.tensor_tensor(
                        out=o2, in0=c_, in1=d_, op=ALU.add),
                        reads=[r("qrt")], writes=[r("qpre")], strict=True)
                for hh in range(4) if "qrms" not in DISABLE else []:
                    p.op("act", lambda h, hh=hh: h.activation(
                        out=sb["qrt"][:, 0:128],
                        in_=sb["qpre"][:, hh * 128:(hh + 1) * 128],
                        func=AF.Square, accum_out=sb["qstat"][:, hh:hh + 1]),
                        reads=[r("qpre")], writes=[r("qrt"), r("qstat")],
                        strict=True)
                if "qrms" not in DISABLE:
                    p.op("act", lambda h: h.activation(
                        out=sb["qstat"][:, 0:4], in_=sb["qstat"][:, 0:4],
                        func=AF.Ln, bias=cp[:, 0:1], scale=cp[:, 4:5]),
                        reads=[r("qstat"), r("constp")], writes=[r("qstat")],
                        strict=True)
                    p.op("act", lambda h: h.activation(
                        out=sb["qstat"][:, 0:4], in_=sb["qstat"][:, 0:4],
                        func=AF.Exp, bias=0.0, scale=cp[:, 2:3]),
                        reads=[r("qstat"), r("constp")], writes=[r("qstat")],
                        strict=True)
                if "qmul" in DISABLE:
                    continue
                p.op("dve", lambda h: h.tensor_tensor(
                    out=sb["qhat_tm"][:],
                    in0=sb["qpre"][:].rearrange("p (h c) -> p h c", h=4),
                    in1=sb["qstat"][:, 0:4].unsqueeze(2).to_broadcast([128, 4, 128]),
                    op=ALU.mult),
                    reads=[r("qstat"), r("qpre")], writes=[r("qhat_tm")])
                if "qtr" in DISABLE:
                    continue
                bank1 = sb["pb1"]

                def qtr(h, bank1=bank1):
                    last = None
                    for hh in range(4):
                        last = h.matmul(
                            bank1[:, hh * 128:(hh + 1) * 128],
                            sb["qhat_tm"][:, hh * 128:(hh + 1) * 128],
                            sb["ident"][:], start=True, stop=True)
                    return last
                p.op("pe", qtr, reads=[r("qhat_tm"), r("ident")],
                     writes=[r("pb1")])
                p.op("dve", lambda h, bank1=bank1, hg=hg, tsub=tsub:
                     h.tensor_copy(
                         out=sb["qhatT"][:].rearrange(
                             "p (hd t) -> p hd t", hd=16)[:, hg * 4:hg * 4 + 4,
                                                          tsub * 128:(tsub + 1) * 128],
                         in_=bank1[:].rearrange("p (hh t) -> p hh t", hh=4)),
                     reads=[r("pb1")], writes=[r("qhatT")])

        for hh in range(16) if "attn" not in DISABLE else []:
            for tsub in range(4):
                st_idx = (t0 + tsub * 128) // 128
                qsl = sb["qhatT"][:, hh * 512 + tsub * 128:hh * 512 + (tsub + 1) * 128]
                if "sparse" in DISABLE:
                    continue
                p.op("pe", lambda h, qsl=qsl: h.matmul(
                    sb["pb2"][:, 0:NBLK], qsl, sb["skv_ss"][:, :],
                    start=True, stop=True),
                    reads=[r("qhatT"), r("skv_ss")], writes=[r("pb2")])
                p.op("dve", lambda h, st=st_idx: h.tensor_tensor(
                    out=sb["ssb"][:], in0=sb["pb2"][:, 0:NBLK],
                    in1=sb["mask_sb"][:, st * NBLK:(st + 1) * NBLK], op=ALU.add),
                    reads=[r("pb2"), r("mask_sb")], writes=[r("ssb")])
                p.op("dve", lambda h: h.reduce_max(
                    out=sb["nmax"][:, 0:1], in_=sb["ssb"][:], axis=AX.X,
                    negate=True),
                    reads=[r("ssb")], writes=[r("nmax")], strict=True)
                p.op("act", lambda h: h.activation(
                    out=sb["ssb"][:], in_=sb["ssb"][:], func=AF.Exp,
                    bias=sb["nmax"][:, 0:1], scale=1.0,
                    accum_out=sb["den"][:, 0:1]),
                    reads=[r("ssb"), r("nmax")], writes=[r("ssb"), r("den")])
                p.op("dve", lambda h: h.reciprocal(
                    out=sb["den"][:, 0:1], in_=sb["den"][:, 0:1]),
                    reads=[r("den")], writes=[r("den")], strict=True)
                p.op("dve", lambda h: h.tensor_scalar(
                    out=sb["pbf"][:], in0=sb["ssb"][:],
                    scalar1=sb["den"][:, 0:1], scalar2=None, op0=ALU.mult),
                    reads=[r("ssb"), r("den")], writes=[r("pbf")])
                if st_idx == 0:
                    p.op("dve", lambda h: h.memset(sb["pbf"][0:16, :], 0.0),
                         writes=[r("pbf")], strict=True)
                bank3 = sb["pb3"]

                def ptr(h, bank3=bank3):
                    last = None
                    for i in range(2):
                        last = h.matmul(bank3[:, i * 128:(i + 1) * 128],
                                        sb["pbf"][:, i * 128:(i + 1) * 128],
                                        sb["ident"][:], start=True, stop=True)
                    return last
                p.op("pe", ptr, reads=[r("pbf"), r("ident")], writes=[r("pb3")])
                p.op("act", lambda h, bank3=bank3: h.copy(
                    out=sb["ptb"][:], in_=bank3[:, 0:256]),
                    reads=[r("pb3")], writes=[r("ptb")])
                bank5 = sb["pb5"]

                def pvs(h, bank5=bank5):
                    last = None
                    for i in range(2):
                        last = h.matmul(bank5[:, 0:128],
                                        sb["skv_pv"][:, i * 128:(i + 1) * 128],
                                        sb["ptb"][:, i * 128:(i + 1) * 128],
                                        start=(i == 0), stop=(i == 1))
                    return last
                p.op("pe", pvs, reads=[r("skv_pv"), r("ptb")], writes=[r("pb5")])
                p.op("act", lambda h, bank5=bank5: h.activation(
                    out=sb["tmps"][:], in_=bank5[:, 0:128], func=AF.Identity,
                    bias=0.0, scale=sb["kn_col"][:, 0:1]),
                    reads=[r("pb5"), r("kn_col")], writes=[r("tmps")])

                nblk = st_idx // 2
                parity = st_idx % 2
                bank6 = sb["pb6"]
                if nblk == 0:
                    def ws0(h, qsl=qsl, bank6=bank6):
                        h.matmul(bank6[:, 0:256], qsl, sb["khatT"][:, 0:256],
                                 start=True, stop=True)
                        return h.matmul(bank6[:, 256:512], qsl,
                                        sb["khatT"][:, 0:256],
                                        start=True, stop=True)
                    p.op("pe", ws0, reads=[r("qhatT"), r("khatT")],
                         writes=[r("pb6")])
                else:
                    klo = (nblk - 1) * 256
                    p.op("pe", lambda h, qsl=qsl, bank6=bank6, klo=klo:
                         h.matmul(bank6[:], qsl, sb["khatT"][:, klo:klo + 512],
                                  start=True, stop=True),
                         reads=[r("qhatT"), r("khatT")], writes=[r("pb6")])
                sel = parity + (2 if nblk == 0 else 0)
                p.op("dve", lambda h, bank6=bank6, sel=sel: h.tensor_tensor(
                    out=sb["wsb"][:], in0=bank6[:],
                    in1=sb["winmask"][:, sel * 512:(sel + 1) * 512], op=ALU.add),
                    reads=[r("pb6"), r("winmask")], writes=[r("wsb")])
                p.op("dve", lambda h: h.reduce_max(
                    out=sb["nmax"][:, 1:2], in_=sb["wsb"][:], axis=AX.X,
                    negate=True),
                    reads=[r("wsb")], writes=[r("nmax")], strict=True)
                p.op("act", lambda h: h.activation(
                    out=sb["wsb"][:], in_=sb["wsb"][:], func=AF.Exp,
                    bias=sb["nmax"][:, 1:2], scale=1.0,
                    accum_out=sb["den"][:, 1:2]),
                    reads=[r("wsb"), r("nmax")], writes=[r("wsb"), r("den")])
                p.op("dve", lambda h: h.reciprocal(
                    out=sb["den"][:, 1:2], in_=sb["den"][:, 1:2]),
                    reads=[r("den")], writes=[r("den")], strict=True)
                p.op("dve", lambda h: h.tensor_scalar(
                    out=sb["pwb"][:], in0=sb["wsb"][:],
                    scalar1=sb["den"][:, 1:2], scalar2=None, op0=ALU.mult),
                    reads=[r("wsb"), r("den")], writes=[r("pwb")])
                bank4 = sb["pb4"]

                def wtr(h, bank4=bank4):
                    last = None
                    for ks in range(4):
                        last = h.matmul(bank4[:, ks * 128:(ks + 1) * 128],
                                        sb["pwb"][:, ks * 128:(ks + 1) * 128],
                                        sb["ident"][:], start=True, stop=True)
                    return last
                p.op("pe", wtr, reads=[r("pwb"), r("ident")], writes=[r("pb4")])
                p.op("act", lambda h, bank4=bank4: h.copy(
                    out=sb["ptw"][:], in_=bank4[:]),
                    reads=[r("pb4")], writes=[r("ptw")])
                bank7 = sb["pb7"]
                ks_lo = 2 if nblk == 0 else 0

                def pvw(h, bank7=bank7, nblk=nblk, ks_lo=ks_lo):
                    last = None
                    for ks in range(ks_lo, 4):
                        ktile = (nblk - 1) * 2 + ks
                        last = h.matmul(bank7[:, 0:128],
                                        sb["v_sb"][:, ktile * 128:(ktile + 1) * 128],
                                        sb["ptw"][:, ks * 128:(ks + 1) * 128],
                                        start=(ks == ks_lo), stop=(ks == 3))
                    return last
                p.op("pe", pvw, reads=[r("v_sb"), r("ptw")], writes=[r("pb7")])
                p.op("dve", lambda h, bank7=bank7, hh=hh, tsub=tsub:
                     h.tensor_tensor(
                         out=sb["attn_bf"][:, hh * 512 + tsub * 128:
                                           hh * 512 + (tsub + 1) * 128],
                         in0=sb["tmps"][:], in1=bank7[:, 0:128], op=ALU.add),
                     reads=[r("tmps"), r("pb7")], writes=[r("attn_bf")])

        for g in range(4) if "group" not in DISABLE else []:
            for osub in range(4):
                bank = sb["pb0"]

                def gmm(h, g=g, osub=osub, bank=bank):
                    last = None
                    for isub in range(4):
                        last = h.matmul(
                            bank[:],
                            sb["groupw_bf"][:, (g * 4 + isub) * 512 + osub * 128:
                                            (g * 4 + isub) * 512 + (osub + 1) * 128],
                            sb["attn_bf"][:, (g * 4 + isub) * 512:
                                          (g * 4 + isub + 1) * 512],
                            start=(isub == 0), stop=(isub == 3))
                    return last
                p.op("pe", gmm, reads=[r("groupw_bf"), r("attn_bf")],
                     writes=[r("pb0")])
                p.op("dve", lambda h, g=g, osub=osub, bank=bank: h.tensor_copy(
                    out=sb["goT"][:, (g * 4 + osub) * 512:
                                  (g * 4 + osub + 1) * 512], in_=bank[:]),
                    reads=[r("pb0")], writes=[r("goT")])

        for pair in range(2) if "final" not in DISABLE else []:
            for kt in range(DT):
                slot = kt % 2
                fw = sb["fwstage"][:, slot * D:(slot + 1) * D]
                p.op("sp", lambda h, fw=fw, kt=kt: h.dma_start(
                    out=fw, in_=dr["fwT_scr"][kt, :, :]),
                    writes=[r("fwstage%d" % slot)], dma="fwstage%d" % slot)
                for ts2 in range(2):
                    tsub = pair * 2 + ts2
                    bn = "pb%d" % (ts2 * 4)
                    for oc in range(4):
                        bno = "pb%d" % (ts2 * 4 + oc)
                        p.op("pe", lambda h, kt=kt, tsub=tsub, oc=oc, fw=fw,
                             bno=bno: h.matmul(
                                 sb[bno][:],
                                 sb["goT"][:, kt * 512 + tsub * 128:
                                           kt * 512 + (tsub + 1) * 128],
                                 fw[:, oc * 512:(oc + 1) * 512],
                                 start=(kt == 0), stop=(kt == DT - 1)),
                             reads=[r("goT"), r("fwstage%d" % (kt % 2))],
                             writes=[r(bno)])
            for ts2 in range(2):
                tsub = pair * 2 + ts2
                st_idx = (t0 + tsub * 128) // 128
                for oc in range(4):
                    bno = "pb%d" % (ts2 * 4 + oc)
                    p.op("dve", lambda h, oc=oc, bno=bno: h.tensor_copy(
                        out=sb["sstage"][:, oc * 512:(oc + 1) * 512],
                        in_=sb[bno][:]),
                        reads=[r(bno)], writes=[r("sstage")])
                p.op("pool", lambda h, st=st_idx: h.dma_start(
                    out=dr["s_scr"][st, :, :], in_=sb["sstage"][:]),
                    reads=[r("sstage")], dma="s_sp")

    def p05_alloc(self, ctx):
        sb = self.sbuf
        sb("wq_i32", (128, 2048), I32, ctx)
        sb("wq_bf", (128, 2048), BF16, ctx)
        sb("tev", (128, 512), BF16, ctx)
        self._units = ([("g", j, 0) for j in range(HT)]
                       + [("u", j, 0) for j in range(HT)]
                       + [("d", ot, part) for ot in range(DT) for part in range(3)])
        self._ui = 0

    def p05_chunk(self, n_units):
        p, sb, dr, r = self.p, self.sb, self.dram, self.r
        for _ in range(n_units):
            if self._ui >= len(self._units):
                return
            kind, j, part = self._units[self._ui]
            self._ui += 1
            i32 = sb["wq_i32"][:, 0:2048]
            wbf = sb["wq_bf"][:, 0:2048]
            if kind in ("g", "u"):
                src = dr["wq_gate"] if kind == "g" else dr["wq_up"]
                dst = dr["wgT_scr"] if kind == "g" else dr["wuT_scr"]
                p.op("sp", lambda h, i32=i32, src=src, j=j: h.dma_start(
                    out=i32, in_=src[j * 128:(j + 1) * 128, :]),
                    writes=[r("wq_i32")], dma="wq_i32")
                p.op("dve", lambda h, i32=i32, wbf=wbf: h.tensor_scalar(
                    out=wbf, in0=i32, scalar1=-128.0, scalar2=None, op0=ALU.add),
                    reads=[r("wq_i32")], writes=[r("wq_bf")])
                for grp in range(4):
                    bn = "pb%d" % (6 + grp % 2)
                    bank = sb[bn]

                    def tmm(h, wbf=wbf, grp=grp, bank=bank):
                        last = None
                        for s in range(4):
                            dt = grp * 4 + s
                            last = h.matmul(bank[:, s * 128:(s + 1) * 128],
                                            wbf[:, dt * 128:(dt + 1) * 128],
                                            sb["ident"][:], start=True,
                                            stop=True)
                        return last
                    p.op("pe", tmm, reads=[r("wq_bf"), r("ident")],
                         writes=[r(bn)])
                    tv = sb["tev"][:, 0:512]
                    p.op("act", lambda h, tv=tv, bank=bank: h.copy(
                        out=tv, in_=bank[:]),
                        reads=[r(bn)], writes=[r("tev")])
                    p.op("pool", lambda h, tv=tv, dst=dst, j=j, grp=grp:
                         h.dma_start(
                             out=dst[j, grp * 4:(grp + 1) * 4, :, :].rearrange(
                                 "dt p h -> p dt h"),
                             in_=tv.rearrange("p (dt h) -> p dt h", dt=4)),
                         reads=[r("tev")], dma="tev_st")
            elif kind == "d":
                ncol = 2048 if part < 2 else 1536
                nsub = ncol // 128
                base_ht = part * 16
                p.op("sp", lambda h, i32=i32, j=j, part=part, ncol=ncol:
                     h.dma_start(out=i32[:, :ncol],
                                 in_=dr["wq_down"][j * 128:(j + 1) * 128,
                                                   part * 2048:part * 2048 + ncol]),
                     writes=[r("wq_i32")], dma="wq_i32")
                p.op("dve", lambda h, i32=i32, wbf=wbf, ncol=ncol:
                     h.tensor_scalar(out=wbf[:, :ncol], in0=i32[:, :ncol],
                                     scalar1=-128.0, scalar2=None, op0=ALU.add),
                     reads=[r("wq_i32")], writes=[r("wq_bf")])
                for grp in range(nsub // 4):
                    bn = "pb%d" % (6 + grp % 2)
                    bank = sb[bn]

                    def tmm(h, wbf=wbf, grp=grp, bank=bank):
                        last = None
                        for s in range(4):
                            ht_ = grp * 4 + s
                            last = h.matmul(bank[:, s * 128:(s + 1) * 128],
                                            wbf[:, ht_ * 128:(ht_ + 1) * 128],
                                            sb["ident"][:], start=True,
                                            stop=True)
                        return last
                    p.op("pe", tmm, reads=[r("wq_bf"), r("ident")],
                         writes=[r(bn)])
                    tv = sb["tev"][:, 0:512]
                    p.op("act", lambda h, tv=tv, bank=bank: h.copy(
                        out=tv, in_=bank[:]),
                        reads=[r(bn)], writes=[r("tev")])
                    p.op("pool", lambda h, tv=tv, j=j, b=base_ht, grp=grp:
                         h.dma_start(
                             out=dr["wdT_scr"][b + grp * 4:b + (grp + 1) * 4,
                                               :, j * 128:(j + 1) * 128].rearrange(
                                 "ht p o -> p ht o"),
                             in_=tv.rearrange("p (ht o) -> p ht o", ht=4)),
                         reads=[r("tev")], dma="tev_st")
            else:
                raise AssertionError(kind)

    def p4_alloc(self, ctx):
        p, dr = self.p, self.dram
        sb = self.sbuf
        sb("sgate", (128, HT), F32, ctx)
        sb("sup", (128, HT), F32, ctx)
        sb("sdown", (128, D), F32, ctx)
        sb("xnTs", (128, DT * 1024), BF16, ctx)
        for s in range(2):
            sb("wg_t%d" % s, (128, DT * 128), BF16, ctx)
            sb("wu_t%d" % s, (128, DT * 128), BF16, ctx)
            sb("wd_t%d" % s, (128, D), BF16, ctx)
        sb("hact", (128, HT * 1024), BF16, ctx)
        sb("silu_s", (128, 512), BF16, ctx)
        sb("xres", (128, D), F32, ctx)
        sb("sres", (128, D), BF16, ctx)
        sb("gbuf", (128, D), BF16, ctx)
        sb("fout", (128, D), F32, ctx)
        sb("obuf", (128, D), F32, ctx)
        for nm in ("sgate", "sup"):
            p.op("sp", lambda h, d=self.sb[nm], s=dr[nm]: h.dma_start(
                out=d[:], in_=s[:, :]), writes=[self.r(nm)], dma=nm)
        p.op("sp", lambda h: h.dma_start(out=self.sb["sdown"][:],
                                         in_=dr["sdown_rep"][:, :]),
             writes=[self.r("sdown")], dma="sdown")

    def p4(self, v):
        p, sb, dr, r = self.p, self.sb, self.dram, self.r
        t0 = v * 1024
        p.op("pool", lambda h: h.dma_start(
            out=sb["xnTs"][:].rearrange("p (dt t) -> p dt t", dt=DT),
            in_=dr["xnT_scr"][:, :, t0:t0 + 1024].rearrange("dt p t -> p dt t")),
            writes=[r("xnTs")], dma="xnTs")

        for j in range(HT):
            sl = j % 2
            wg = sb["wg_t%d" % sl]
            wu = sb["wu_t%d" % sl]
            p.op("sp", lambda h, wg=wg, j=j: h.dma_start(
                out=wg[:].rearrange("p (dt hh) -> p dt hh", dt=DT),
                in_=dr["wgT_scr"][j, :, :, :].rearrange("dt p hh -> p dt hh")),
                writes=[r("wg_t%d" % sl)], dma="wg_t%d" % sl)
            p.op("sp", lambda h, wu=wu, j=j: h.dma_start(
                out=wu[:].rearrange("p (dt hh) -> p dt hh", dt=DT),
                in_=dr["wuT_scr"][j, :, :, :].rearrange("dt p hh -> p dt hh")),
                writes=[r("wu_t%d" % sl)], dma="wu_t%d" % sl)
            for c in range(2):
                for which, wt, bn in (("g", wg, "pb0"), ("u", wu, "pb1")):
                    bank = sb[bn]

                    def gmm(h, wt=wt, c=c, bank=bank):
                        last = None
                        for dt in range(DT):
                            last = h.matmul(
                                bank[:],
                                wt[:, dt * 128:(dt + 1) * 128],
                                sb["xnTs"][:, dt * 1024 + c * 512:
                                           dt * 1024 + c * 512 + 512],
                                start=(dt == 0), stop=(dt == DT - 1))
                        return last
                    p.op("pe", gmm, reads=[r("wg_t%d" % sl) if which == "g"
                                           else r("wu_t%d" % sl), r("xnTs")],
                         writes=[r(bn)])
                p.op("act", lambda h, j=j: h.activation(
                    out=sb["silu_s"][:], in_=sb["pb0"][:], func=AF.Silu,
                    bias=0.0, scale=sb["sgate"][:, j:j + 1]),
                    reads=[r("pb0"), r("sgate")], writes=[r("silu_s")])
                p.op("dve", lambda h, j=j, c=c: h.scalar_tensor_tensor(
                    out=sb["hact"][:, j * 1024 + c * 512:j * 1024 + c * 512 + 512],
                    in0=sb["pb1"][:], scalar=sb["sup"][:, j:j + 1],
                    in1=sb["silu_s"][:], op0=ALU.mult, op1=ALU.mult),
                    reads=[r("pb1"), r("sup"), r("silu_s")], writes=[r("hact")])

        for tg in range(4):
            for j in range(HT):
                sl = j % 2
                wd = sb["wd_t%d" % sl]
                p.op("sp", lambda h, wd=wd, j=j: h.dma_start(
                    out=wd[:], in_=dr["wdT_scr"][j, :, :]),
                    writes=[r("wd_t%d" % sl)], dma="wd_t%d" % sl)
                for ts2 in range(2):
                    for oc in range(4):
                        bno = "pb%d" % (ts2 * 4 + oc)
                        p.op("pe", lambda h, j=j, tg=tg, ts2=ts2, oc=oc, wd=wd,
                             bno=bno: h.matmul(
                                 sb[bno][:],
                                 sb["hact"][:, j * 1024 + tg * 256 + ts2 * 128:
                                            j * 1024 + tg * 256 + (ts2 + 1) * 128],
                                 wd[:, oc * 512:(oc + 1) * 512],
                                 start=(j == 0), stop=(j == HT - 1)),
                             reads=[r("hact"), r("wd_t%d" % sl)],
                             writes=[r(bno)])
            for ts2 in range(2):
                trow = t0 + tg * 256 + ts2 * 128
                st_idx = trow // 128
                for oc in range(4):
                    bno = "pb%d" % (ts2 * 4 + oc)
                    p.op("dve", lambda h, oc=oc, bno=bno: h.tensor_tensor(
                        out=sb["fout"][:, oc * 512:(oc + 1) * 512],
                        in0=sb[bno][:], in1=sb["sdown"][:, oc * 512:(oc + 1) * 512],
                        op=ALU.mult),
                        reads=[r(bno), r("sdown")], writes=[r("fout")])
                p.op("act", lambda h: h.activation(
                    out=sb["gbuf"][:], in_=sb["fout"][:], func=AF.Silu),
                    reads=[r("fout")], writes=[r("gbuf")])
                p.op("pool", lambda h, trow=trow: h.dma_start(
                    out=sb["xres"][:], in_=dr["x"][trow:trow + 128, :]),
                    writes=[r("xres")], dma="xres")
                p.op("pool", lambda h, st=st_idx: h.dma_start(
                    out=sb["sres"][:], in_=dr["s_scr"][st, :, :]),
                    writes=[r("sres")], dma="sres")
                p.op("dve", lambda h: h.tensor_tensor(
                    out=sb["obuf"][:], in0=sb["sres"][:], in1=sb["gbuf"][:],
                    op=ALU.mult),
                    reads=[r("sres"), r("gbuf")], writes=[r("obuf")])
                p.op("dve", lambda h: h.tensor_tensor(
                    out=sb["obuf"][:], in0=sb["obuf"][:], in1=sb["xres"][:],
                    op=ALU.add),
                    reads=[r("xres")], writes=[r("obuf")], strict=True)
                p.op("pool", lambda h, trow=trow: h.dma_start(
                    out=dr["y"][trow:trow + 128, :], in_=sb["obuf"][:]),
                    reads=[r("obuf")], dma="y_st")

    def emit_debug(self):
        sb = self.sb
        self.dbg("q_cT", lambda: sb["q_cT"][:], (128, 4 * T), BF16, ["q_cT"])
        self.dbg("khatT", lambda: sb["khatT"][:], (128, T), BF16, ["khatT"])
        self.dbg("v_sb", lambda: sb["v_sb"][:], (128, NT * 128), BF16, ["v_sb"])
        self.dbg("qiT", lambda: sb["qiT"][:], (128, 2 * T), BF16, ["qiT"])
        self.dbg("wih", lambda: sb["wih"][:], (128, NT * 4), F32, ["wih"])
        self.dbg("ckvT", lambda: sb["ckvT"][:], (128, NBLK), F32, ["ckvT"])
        self.dbg("kcompT", lambda: sb["kcompT"][:64, :], (64, NBLK), F32,
                 ["kcompT"])
        self.dbg("skv_ss", lambda: sb["skv_ss"][:], (128, NBLK), BF16,
                 ["skv_ss"])
        self.dbg("skv_pv", lambda: sb["skv_pv"][:], (128, 2 * 128), BF16,
                 ["skv_pv"])
        if "xnT" in self.dbg_names:
            d = self.dout("dbg_xnT", (DT, 128, T), BF16)
            self.p.op("pool", lambda h, d=d: h.dma_start(
                out=d[:, :, :], in_=self.dram["xnT_scr"][:, :, :]), dma="dbg")
        if "wgT" in self.dbg_names:
            d = self.dout("dbg_wgT", (HT, DT, 128, 128), BF16)
            self.p.op("pool", lambda h, d=d: h.dma_start(
                out=d[:, :, :, :], in_=self.dram["wgT_scr"][:, :, :, :]),
                dma="dbg")
        if "wdT" in self.dbg_names:
            d = self.dout("dbg_wdT", (HT, 128, D), BF16)
            self.p.op("pool", lambda h, d=d: h.dma_start(
                out=d[:, :, :], in_=self.dram["wdT_scr"][:, :, :]), dma="dbg")
        if "mask" in self.dbg_names:
            d = self.dout("dbg_mask", (128, NT * NBLK), BF16)
            self.p.op("pool", lambda h, d=d: h.dma_start(
                out=d[:, :], in_=sb["mask_sb"][:]),
                reads=[self.r("mask_sb")], dma="dbg")
        if "s" in self.dbg_names:
            d = self.dout("dbg_s", (NT, 128, D), BF16)
            self.p.op("pool", lambda h, d=d: h.dma_start(
                out=d[:, :, :], in_=self.dram["s_scr"][:, :, :]), dma="dbg")

    def build(self):
        p = self.p
        self.p0_globals()
        with ExitStack() as cA:
            self.p0_allocA(cA)
            with ExitStack() as cB:
                self.p0_weights(cB)
                self.p1_alloc(cB)
                self.p05_alloc(cB)
                p.barrier()
                for u in range(NSUP):
                    self.p1(u)
                    self.p05_chunk(19)
                self.p1_post()
                self.p05_chunk(10 ** 9)
                p.barrier()
            with ExitStack() as cC:
                self.p2_alloc(cC)
                for st in range(NT):
                    self.p2(st)
                    if st < DT:
                        self.p2_fwconv(st)
                p.barrier()
            with ExitStack() as cD:
                self.p3_alloc(cD)
                for u in range(NSUP):
                    self.p3(u)
                p.barrier()
            self.emit_debug()
            p.barrier()
        with ExitStack() as cE:
            self.p4_alloc(cE)
            for v in range(T // 1024):
                self.p4(v)
            p.barrier()
        block = self.ctx.enter_context(self.nc.Block())
        p.replay(block)
        self.ctx.close()
        return self.nc


_BUILD_CACHE = {}


def build_nc(dbg_names=()):
    key = tuple(sorted(dbg_names))
    if key not in _BUILD_CACHE:
        b = Builder(dbg_names)
        _BUILD_CACHE[key] = (b.build(), b)
    return _BUILD_CACHE[key]


def kernel(**inputs):
    shared, per_core = host_prep(inputs)
    nc, b = build_nc()
    in_maps = []
    for c in range(8):
        m = dict(shared)
        m.update(per_core[c])
        in_maps.append(m)
    res = run_bass_kernel_spmd(nc, in_maps, core_ids=list(range(8)))
    out = np.stack([res.results[c]["y"] for c in range(8)], axis=0)
    return out.astype(np.float32)


# version 05
# speedup vs baseline: 1.2126x; 1.2126x over previous
import math
from contextlib import ExitStack

import numpy as np
import ml_dtypes

import concourse.bass as bass
import concourse.mybir as mybir
from concourse.bass_utils import run_bass_kernel_spmd

F32 = mybir.dt.float32
BF16 = mybir.dt.bfloat16
I32 = mybir.dt.int32
AF = mybir.ActivationFunctionType
ALU = mybir.AluOpType
AX = mybir.AxisListType

BF = ml_dtypes.bfloat16

T = 4096
D = 2048
H = 16
HD = 128
QC = 512
NBLK = 256
MBLK = 16
CI = 64
NIH = 4
WIN = 256
HID = 5632
NEG = -1.0e30
SENT = -2.0e30
SCALE = 1.0 / math.sqrt(128.0)

NT = T // 128
NSUP = T // 512
HT = HID // 128
DT = D // 128

ENGINES = ("pe", "dve", "act", "pool", "sp")
DISABLE = set()
ROPE_CAP = 6


class Res:
    __slots__ = ("w", "rs")

    def __init__(self):
        self.w = None
        self.rs = []

    def for_read(self):
        return [self.w] if self.w is not None else []

    def for_write(self):
        out = list(self.rs)
        if self.w is not None:
            out.append(self.w)
        return out


class Prog:

    def __init__(self, nc, ctx):
        self.nc = nc
        self.ctx = ctx
        self.ops = {e: [] for e in ENGINES}
        self.sems = {}
        self.cnt = {}
        for name in ENGINES:
            self.sems[name] = ctx.enter_context(nc.semaphore("sem_" + name))
            self.cnt[name] = 0
        self.dma_sems = []
        self.last_tok = {e: None for e in ENGINES}
        self.floor = {e: {} for e in ENGINES}

    def _dma_sem(self, name):
        key = "dma_" + name
        if key not in self.sems:
            self.sems[key] = self.ctx.enter_context(
                self.nc.semaphore("sem_" + key))
            self.cnt[key] = 0
            self.dma_sems.append(key)
        return key

    def op(self, eng, fn, reads=(), writes=(), waits=(), dma=None, strict=False):
        wl = {}
        fl = self.floor[eng]

        def add(tok):
            if tok is None:
                return
            sem, val, teng = tok
            if fl.get(sem, -1) >= val:
                return
            if wl.get(sem, -1) < val:
                wl[sem] = val
        for rres in reads:
            for tok in rres.for_read():
                add(tok)
        for wres in writes:
            for tok in wres.for_write():
                add(tok)
        for tok in waits:
            add(tok)
        for sem, val in wl.items():
            assert val <= self.cnt[sem], (sem, val, self.cnt[sem])
            fl[sem] = val
        if dma is None:
            sem, k = eng, 1
        else:
            sem, k = self._dma_sem(dma), 16
        self.cnt[sem] += k
        tok = (sem, self.cnt[sem], None if dma is not None else eng)
        self.ops[eng].append((fn, sorted(wl.items()), sem, k))
        if dma is None:
            self.last_tok[eng] = tok
        for rres in reads:
            rres.rs.append(tok)
        for wres in writes:
            wres.w = tok
            wres.rs = []
        return tok

    def barrier(self):
        toks = [t for t in self.last_tok.values() if t is not None]
        for key in self.dma_sems:
            if self.cnt[key]:
                toks.append((key, self.cnt[key], None))
        for e in ENGINES:
            self.op(e, lambda h: h.nop(), waits=toks)

    def replay(self, block):
        sems = self.sems

        def mk_body(eng_name):
            def run(h):
                for fn, waits, sem, k in self.ops[eng_name]:
                    for wsem, val in waits:
                        h.wait_ge(sems[wsem], val)
                    instr = fn(h)
                    if instr is not None:
                        instr.then_inc(sems[sem], k)
                    else:
                        h.nop().then_inc(sems[sem], k)
            return run
        block.tensor(mk_body("pe"))
        block.vector(mk_body("dve"))
        block.scalar(mk_body("act"))
        block.gpsimd(mk_body("pool"))
        block.sync(mk_body("sp"))


def host_constants():
    c = {}
    c["ident"] = np.eye(128, dtype=BF)
    inv = 1.0 / 10000.0 ** (np.arange(0, 64, 2, dtype=np.float64) / 64.0)
    ang = np.arange(T, dtype=np.float64)[:, None] * inv
    cos = np.cos(ang).astype(np.float32)
    sin = np.sin(ang).astype(np.float32)
    c["cosp"] = np.ascontiguousarray(
        cos.reshape(NT, 128, 32).transpose(1, 0, 2).reshape(128, NT * 32))
    c["sinp"] = np.ascontiguousarray(
        sin.reshape(NT, 128, 32).transpose(1, 0, 2).reshape(128, NT * 32))
    tvec = np.arange(T)
    nvec = np.arange(NBLK)
    causal = nvec[None, :] < (tvec[:, None] // MBLK)
    ca = np.where(causal, 0.0, NEG).astype(np.float32)
    c["causal"] = np.ascontiguousarray(
        ca.reshape(NT, 128, NBLK).transpose(1, 0, 2).reshape(128, NT * NBLK)
    ).astype(BF)
    diff = np.arange(2 * WIN)[None, :] - np.arange(WIN)[:, None]
    win_ok = (diff > 0) & (diff <= WIN)
    first_ok = np.arange(2 * WIN)[None, :] >= WIN
    mask_n = np.where(win_ok, 0.0, NEG).astype(np.float32)
    mask_0 = np.where(win_ok & first_ok, 0.0, NEG).astype(np.float32)
    wm = np.concatenate([mask_n[:128], mask_n[128:], mask_0[:128], mask_0[128:]],
                        axis=1)
    c["winmask"] = np.ascontiguousarray(wm).astype(BF)
    return c


def host_prep(inputs):
    g = lambda k: np.ascontiguousarray(np.asarray(inputs[k]))
    shared = dict(host_constants())
    shared["w_qc"] = g("L0_attn_w_qc")
    shared["w_qup"] = g("L0_attn_w_qup")
    shared["W1"] = np.ascontiguousarray(np.concatenate(
        [g("L0_attn_k_proj_w"), g("L0_attn_v_proj_w"), g("L0_attn_w_w")], axis=1))
    shared["W2"] = np.ascontiguousarray(np.concatenate(
        [g("L0_attn_kvc_w"), g("L0_attn_kvc_wz"), g("L0_attn_idx_c_w"),
         g("L0_attn_idx_c_wz"), g("L0_attn_w_dq")], axis=1))
    shared["w_iuq"] = g("L0_attn_w_iuq")
    shared["biasTkv"] = np.ascontiguousarray(g("L0_attn_kvc_bias").T)
    shared["biasTidx"] = np.ascontiguousarray(g("L0_attn_idx_c_bias").T)
    qn = g("L0_attn_q_norm_w").reshape(-1)
    kn = g("L0_attn_k_norm_w").reshape(-1)
    shared["qn_col"] = np.ascontiguousarray(qn[:, None])
    shared["kn_col"] = np.ascontiguousarray(kn[:, None])
    shared["qn_rep"] = np.ascontiguousarray(np.broadcast_to(qn[None, :], (128, 128)))
    shared["kn_rep"] = np.ascontiguousarray(np.broadcast_to(kn[None, :], (128, 128)))
    shared["group_w"] = g("L0_attn_group_w")
    shared["final_w"] = g("L0_attn_final_w")
    shared["wq_gate"] = g("L0_ffn_wq_gate")
    shared["wq_up"] = g("L0_ffn_wq_up")
    shared["wq_down"] = g("L0_ffn_wq_down")
    shared["sgate"] = np.ascontiguousarray(g("L0_ffn_s_gate").reshape(HT, 128).T)
    shared["sup"] = np.ascontiguousarray(g("L0_ffn_s_up").reshape(HT, 128).T)
    shared["sdown_rep"] = np.ascontiguousarray(
        np.broadcast_to(g("L0_ffn_s_down").reshape(-1)[None, :], (128, D)))
    x = np.asarray(inputs["L0_attn_h"])
    per_core = [{"x": np.ascontiguousarray(x[b])} for b in range(8)]
    return shared, per_core


class Builder:
    def __init__(self, dbg_names=()):
        self.nc = bass.Bass("TRN2", target_bir_lowering=False, debug=False)
        self.ctx = ExitStack()
        self.p = Prog(self.nc, self.ctx)
        self.dbg_names = set(dbg_names)
        self.dram = {}
        self.sb = {}
        self.res = {}

    def din(self, name, shape, dtype):
        self.dram[name] = self.nc.dram_tensor(name, list(shape), dtype,
                                              kind="ExternalInput").ap()
        return self.dram[name]

    def dout(self, name, shape, dtype):
        self.dram[name] = self.nc.dram_tensor(name, list(shape), dtype,
                                              kind="ExternalOutput").ap()
        return self.dram[name]

    def dscr(self, name, shape, dtype):
        self.dram[name] = self.nc.dram_tensor(name, list(shape), dtype,
                                              kind="Internal").ap()
        return self.dram[name]

    def sbuf(self, name, shape, dtype, ctx=None):
        h = (ctx or self.ctx).enter_context(
            self.nc.sbuf_tensor("sb_" + name, list(shape), dtype))
        self.sb[name] = h
        self.res[name] = Res()
        return h

    def psum(self, name, ctx=None):
        h = (ctx or self.ctx).enter_context(
            self.nc.psum_tensor("ps_" + name, [128, 512], F32))
        self.sb[name] = h
        self.res[name] = Res()
        return h

    def r(self, name):
        return self.res[name]

    def dbg(self, name, ap_fn, shape, dtype, reads):
        if name not in self.dbg_names:
            return
        d = self.dout("dbg_" + name, shape, dtype)
        self.p.op("pool", lambda h, d=d, f=ap_fn: h.dma_start(out=d, in_=f()),
                  reads=[self.r(n) for n in reads], dma="dbg")

    def p0_globals(self):
        p, dr = self.p, self.dram
        for i in range(8):
            self.psum("pb%d" % i)
        self.din("x", (T, D), F32)
        for nm, sh, dt in [
            ("ident", (128, 128), BF16), ("cosp", (128, NT * 32), F32),
            ("sinp", (128, NT * 32), F32), ("causal", (128, NT * NBLK), BF16),
            ("winmask", (128, 4 * 512), BF16),
            ("w_qc", (D, QC), F32), ("w_qup", (QC, D), F32),
            ("W1", (D, 260), F32), ("W2", (D, 448), F32),
            ("w_iuq", (64, 256), F32),
            ("biasTkv", (128, 16), F32), ("biasTidx", (64, 16), F32),
            ("qn_col", (128, 1), F32), ("kn_col", (128, 1), F32),
            ("qn_rep", (128, 128), F32), ("kn_rep", (128, 128), F32),
            ("group_w", (4, 512, 512), F32), ("final_w", (D, D), F32),
            ("wq_gate", (HID, D), I32), ("wq_up", (HID, D), I32),
            ("wq_down", (D, HID), I32),
            ("sgate", (128, HT), F32), ("sup", (128, HT), F32),
            ("sdown_rep", (128, D), F32),
        ]:
            self.din(nm, sh, dt)
        self.dout("y", (T, D), F32)
        self.dscr("xnT_scr", (DT, 128, T), BF16)
        self.dscr("s_scr", (NT, 128, D), BF16)
        self.dscr("wgT_scr", (HT, 128, DT * 128), BF16)
        self.dscr("wuT_scr", (HT, 128, DT * 128), BF16)
        self.dscr("wdT_scr", (HT, 128, D), BF16)
        self.dscr("fwT_scr", (DT, 128, D), BF16)

        sb = self.sbuf
        sb("ident", (128, 128), BF16)
        sb("constp", (128, 8), F32)
        sb("sdown", (128, D), F32)
        sb("qn_col", (128, 1), F32)
        sb("kn_col", (128, 1), F32)
        sb("gains_col", (128, 1), F32)
        sb("ones_col", (128, 1), F32)
        sb("ones_row", (1, 128), F32)

        for nm in ("ident", "qn_col", "kn_col"):
            p.op("sp", lambda h, d=self.sb[nm], s=dr[nm]: h.dma_start(
                out=d[:], in_=s[:, :]), writes=[self.r(nm)], dma=nm)
        p.op("sp", lambda h: h.dma_start(out=self.sb["sdown"][:],
                                         in_=dr["sdown_rep"][:, :]),
             writes=[self.r("sdown")], dma="sdown")
        cp = self.sb["constp"]
        for slot, val in [(0, 1e-6), (1, 1e-5), (2, -0.5), (3, 1.0 / D),
                          (4, 1.0 / 128.0), (5, 0.0), (6, 1e30), (7, -1e30)]:
            p.op("dve", lambda h, s=slot, v=val: h.memset(cp[:, s:s + 1], v),
                 writes=[self.r("constp")])
        p.op("dve", lambda h: h.memset(self.sb["ones_col"][:], 1.0),
             writes=[self.r("ones_col")])
        p.op("dve", lambda h: h.memset(self.sb["ones_row"][:], 1.0),
             writes=[self.r("ones_row")])
        p.op("dve", lambda h: h.tensor_tensor(
            out=self.sb["gains_col"][:], in0=self.sb["qn_col"][:],
            in1=self.sb["kn_col"][:], op=ALU.mult),
            reads=[self.r("qn_col"), self.r("kn_col")],
            writes=[self.r("gains_col")])
        p.op("dve", lambda h: h.tensor_scalar(
            out=self.sb["gains_col"][:], in0=self.sb["gains_col"][:],
            scalar1=SCALE, scalar2=None, op0=ALU.mult),
            writes=[self.r("gains_col")], strict=True)

    def p0_allocA(self, ctx):
        p, dr = self.p, self.dram
        sb = self.sbuf
        sb("cos", (128, NT * 32), F32, ctx)
        sb("sin", (128, NT * 32), F32, ctx)
        sb("q_cT", (128, 4 * T), BF16, ctx)
        sb("khatT", (128, T), BF16, ctx)
        sb("v_sb", (128, NT * 128), BF16, ctx)
        sb("ckvT", (128, NBLK), F32, ctx)
        sb("kcompT", (64, NBLK), F32, ctx)
        sb("kcomp_bf", (128, NBLK), BF16, ctx)
        sb("skv_ss", (128, NBLK), BF16, ctx)
        sb("skv_pv", (128, 2 * 128), BF16, ctx)
        sb("mask_sb", (128, NT * NBLK), BF16, ctx)
        for nm, srcn in (("cos", "cosp"), ("sin", "sinp")):
            p.op("sp", lambda h, d=self.sb[nm], s=dr[srcn]: h.dma_start(
                out=d[:], in_=s[:, :]), writes=[self.r(nm)], dma=nm)

    def p0_weights(self, ctx):
        p, dr = self.p, self.dram
        sb = self.sbuf
        sb("wqs_rep", (128, 128), F32, ctx)
        sb("wk_rep", (128, 128), F32, ctx)
        sb("biasTkv", (128, 16), F32, ctx)
        sb("biasTidx", (64, 16), F32, ctx)
        sb("w_iuq", (64, 256), BF16, ctx)
        sb("wstage", (128, 512), F32, ctx)
        sb("w_qc_bf", (128, DT * QC), BF16, ctx)
        sb("W1_bf", (128, DT * 260), BF16, ctx)
        sb("W2_bf", (128, DT * 448), BF16, ctx)

        for nm in ("biasTkv", "biasTidx"):
            p.op("sp", lambda h, d=self.sb[nm], s=dr[nm]: h.dma_start(
                out=d[:], in_=s[:, :]), writes=[self.r(nm)], dma=nm)

        def conv(dstname, col, src_ap, ncols):
            st = self.sb["wstage"]
            p.op("sp", lambda h, s=src_ap, n=ncols: h.dma_start(
                out=st[:, :n], in_=s), writes=[self.r("wstage")], dma="wstage")
            p.op("dve", lambda h, d=self.sb[dstname], c=col, n=ncols:
                 h.tensor_copy(out=d[:, c:c + n], in_=st[:, :n]),
                 reads=[self.r("wstage")], writes=[self.r(dstname)])

        for i in range(DT):
            conv("w_qc_bf", i * QC, dr["w_qc"][i * 128:(i + 1) * 128, :], QC)
            conv("W1_bf", i * 260, dr["W1"][i * 128:(i + 1) * 128, :], 260)
            conv("W2_bf", i * 448, dr["W2"][i * 128:(i + 1) * 128, :], 448)
        st = self.sb["wstage"]
        p.op("sp", lambda h: h.dma_start(out=st[:64, :256], in_=dr["w_iuq"][:, :]),
             writes=[self.r("wstage")], dma="wstage")
        p.op("dve", lambda h: h.tensor_copy(out=self.sb["w_iuq"][:, :],
                                            in_=st[:64, :256]),
             reads=[self.r("wstage")], writes=[self.r("w_iuq")])
        p.op("sp", lambda h: h.dma_start(out=self.sb["wqs_rep"][:],
                                         in_=dr["qn_rep"][:, :]),
             writes=[self.r("wqs_rep")], dma="wqs_rep")
        p.op("dve", lambda h: h.tensor_scalar(
            out=self.sb["wqs_rep"][:], in0=self.sb["wqs_rep"][:],
            scalar1=SCALE, scalar2=None, op0=ALU.mult),
            reads=[self.r("wqs_rep")], writes=[self.r("wqs_rep")])
        p.op("sp", lambda h: h.dma_start(out=self.sb["wk_rep"][:],
                                         in_=dr["kn_rep"][:, :]),
             writes=[self.r("wk_rep")], dma="wk_rep")

    def p1_alloc(self, ctx):
        sb = self.sbuf
        sb("xstage", (128, D), F32, ctx)
        sb("lnstat", (128, 16), F32, ctx)
        sb("xn_tm", (128, 4 * D), BF16, ctx)
        sb("xnT", (128, DT * 512), BF16, ctx)
        sb("kstage", (128, 4 * 128), F32, ctx)
        sb("khat_tm", (128, 4 * 128), BF16, ctx)
        sb("dq_bf", (64, 512), BF16, ctx)
        sb("czkv", (128, 1024), F32, ctx)
        sb("czidx", (64, 1024), F32, ctx)
        sb("cmp_scr", (128, 512), F32, ctx)
        sb("cmp_red", (128, 64), F32, ctx)
        sb("rope_scr", (128, 128), F32, ctx)
        sb("skv_stg", (128, 256), BF16, ctx)

    def p1(self, u):
        p, sb, dr, r = self.p, self.sb, self.dram, self.r
        t0 = u * 512
        cp = sb["constp"]
        xst = sb["xstage"]

        for s in range(4):
            p.op("pool", lambda h, s=s: h.dma_start(
                out=xst[:],
                in_=dr["x"][t0 + s * 128:t0 + (s + 1) * 128, :]),
                writes=[r("xstage")], dma="xstage")
            xs = xst[:, 0:D]
            c_mean = sb["lnstat"][:, 4 * s:4 * s + 1]
            c_rv = sb["lnstat"][:, 4 * s + 1:4 * s + 2]
            c_t = sb["lnstat"][:, 4 * s + 2:4 * s + 3]
            p.op("dve", lambda h, xs=xs, c=c_mean: h.reduce_sum(
                out=c, in_=xs, axis=AX.X),
                reads=[r("xstage")], writes=[r("lnstat")])
            p.op("act", lambda h, xs=xs, s=s, c=c_rv: h.activation(
                out=sb["xn_tm"][:, s * D:(s + 1) * D], in_=xs, func=AF.Square,
                accum_out=c),
                reads=[r("xstage")], writes=[r("xn_tm"), r("lnstat")])
            p.op("dve", lambda h, c=c_mean: h.tensor_scalar(
                out=c, in0=c, scalar1=1.0 / D, scalar2=None, op0=ALU.mult),
                reads=[r("lnstat")], writes=[r("lnstat")], strict=True)
            p.op("dve", lambda h, c=c_t, m=c_mean: h.tensor_tensor(
                out=c, in0=m, in1=m, op=ALU.mult),
                reads=[r("lnstat")], writes=[r("lnstat")], strict=True)
            p.op("dve", lambda h, c=c_rv, t_=c_t: h.scalar_tensor_tensor(
                out=c, in0=c, scalar=1.0 / D, in1=t_, op0=ALU.mult,
                op1=ALU.subtract),
                reads=[r("lnstat")], writes=[r("lnstat")], strict=True)
            p.op("act", lambda h, c=c_rv: h.activation(
                out=c, in_=c, func=AF.Ln, bias=cp[:, 1:2], scale=1.0),
                reads=[r("lnstat"), r("constp")], writes=[r("lnstat")])
            p.op("act", lambda h, c=c_rv: h.activation(
                out=c, in_=c, func=AF.Exp, bias=0.0, scale=cp[:, 2:3]),
                reads=[r("lnstat"), r("constp")], writes=[r("lnstat")],
                strict=True)
            p.op("dve", lambda h, c=c_t, m=c_mean, v=c_rv: h.scalar_tensor_tensor(
                out=c, in0=m, scalar=-1.0, in1=v, op0=ALU.mult, op1=ALU.mult),
                reads=[r("lnstat")], writes=[r("lnstat")])
            p.op("act", lambda h, xs=xs, s=s, v=c_rv, b=c_t: h.activation(
                out=sb["xn_tm"][:, s * D:(s + 1) * D], in_=xs, func=AF.Identity,
                bias=b, scale=v),
                reads=[r("lnstat"), r("xstage")], writes=[r("xn_tm")])

        for dt in range(DT):
            bn = "pb%d" % (dt % 2)
            bank = sb[bn]

            def tmm(h, dt=dt, bank=bank):
                last = None
                for s in range(4):
                    last = h.matmul(bank[:, s * 128:(s + 1) * 128],
                                    sb["xn_tm"][:, s * D + dt * 128:
                                                s * D + (dt + 1) * 128],
                                    sb["ident"][:], start=True, stop=True)
                return last
            p.op("pe", tmm, reads=[r("xn_tm"), r("ident")], writes=[r(bn)])
            p.op("dve", lambda h, dt=dt, bank=bank: h.tensor_copy(
                out=sb["xnT"][:, dt * 512:(dt + 1) * 512], in_=bank[:]),
                reads=[r(bn)], writes=[r("xnT")])
        p.op("pool", lambda h: h.dma_start(
            out=dr["xnT_scr"][:, :, t0:t0 + 512].rearrange("dt p t -> p dt t"),
            in_=sb["xnT"][:].rearrange("p (dt t) -> p dt t", dt=DT)),
            reads=[r("xnT")], dma="xnT_sp")

        for mt in range(4):
            bank = sb["pb2"]

            def qcmm(h, mt=mt, bank=bank):
                last = None
                for kt in range(DT):
                    last = h.matmul(
                        bank[:],
                        sb["w_qc_bf"][:, kt * QC + mt * 128:
                                      kt * QC + (mt + 1) * 128],
                        sb["xnT"][:, kt * 512:(kt + 1) * 512],
                        start=(kt == 0), stop=(kt == DT - 1))
                return last
            p.op("pe", qcmm, reads=[r("w_qc_bf"), r("xnT")], writes=[r("pb2")])
            p.op("dve", lambda h, mt=mt, bank=bank: h.tensor_copy(
                out=sb["q_cT"][:, mt * T + t0: mt * T + t0 + 512], in_=bank[:]),
                reads=[r("pb2")], writes=[r("q_cT")])

        for s in range(4):
            bank = sb["pb3"]
            st_idx = u * 4 + s

            def w1mm(h, s=s, bank=bank):
                last = None
                for kt in range(DT):
                    last = h.matmul(
                        bank[:, :260],
                        sb["xnT"][:, kt * 512 + s * 128:kt * 512 + (s + 1) * 128],
                        sb["W1_bf"][:, kt * 260:(kt + 1) * 260],
                        start=(kt == 0), stop=(kt == DT - 1))
                return last
            p.op("pe", w1mm, reads=[r("W1_bf"), r("xnT")], writes=[r("pb3")])
            p.op("act", lambda h, s=s, bank=bank: h.copy(
                out=sb["kstage"][:, s * 128:(s + 1) * 128], in_=bank[:, 0:128]),
                reads=[r("pb3")], writes=[r("kstage")])
            p.op("act", lambda h, bank=bank, st=st_idx: h.copy(
                out=sb["v_sb"][:, st * 128:(st + 1) * 128], in_=bank[:, 128:256]),
                reads=[r("pb3")], writes=[r("v_sb")])
            p.op("act", lambda h, bank=bank, st=st_idx: h.copy(
                out=sb["wih"][:, st * 4:(st + 1) * 4], in_=bank[:, 256:260]),
                reads=[r("pb3")], writes=[r("wih")])

        for s in range(4):
            st_idx = u * 4 + s
            ks = sb["kstage"][:, s * 128:(s + 1) * 128]
            c_rs = sb["lnstat"][:, s:s + 1]
            p.op("act", lambda h, ks=ks, c=c_rs: h.activation(
                out=sb["xn_tm"][:, 0:128], in_=ks, func=AF.Square, accum_out=c),
                reads=[r("kstage")], writes=[r("xn_tm"), r("lnstat")])
            p.op("act", lambda h, c=c_rs: h.activation(
                out=c, in_=c, func=AF.Ln, bias=cp[:, 0:1], scale=cp[:, 4:5]),
                reads=[r("lnstat"), r("constp")], writes=[r("lnstat")],
                strict=True)
            p.op("act", lambda h, c=c_rs: h.activation(
                out=c, in_=c, func=AF.Exp, bias=0.0, scale=cp[:, 2:3]),
                reads=[r("lnstat"), r("constp")], writes=[r("lnstat")],
                strict=True)
            p.op("act", lambda h, ks=ks, c=c_rs: h.activation(
                out=ks, in_=ks, func=AF.Identity, bias=0.0, scale=c),
                reads=[r("lnstat")], writes=[r("kstage")], strict=True)
            p.op("dve", lambda h, ks=ks: h.tensor_tensor(
                out=ks, in0=ks, in1=sb["wk_rep"][:], op=ALU.mult),
                reads=[r("wk_rep"), r("kstage")], writes=[r("kstage")],
                strict=True)
            co = sb["cos"][:, st_idx * 32:(st_idx + 1) * 32]
            si = sb["sin"][:, st_idx * 32:(st_idx + 1) * 32]
            x1 = ks[:, 0:32]
            x2 = ks[:, 32:64]
            rs_ = sb["rope_scr"]
            p.op("dve", lambda h, x1=x1, co=co: h.tensor_tensor(
                out=rs_[:, 0:32], in0=x1, in1=co, op=ALU.mult),
                reads=[r("kstage"), r("cos")], writes=[r("rope_scr")],
                strict=True)
            p.op("dve", lambda h, x2=x2, co=co: h.tensor_tensor(
                out=rs_[:, 32:64], in0=x2, in1=co, op=ALU.mult),
                reads=[r("kstage"), r("cos")], writes=[r("rope_scr")],
                strict=True)
            p.op("dve", lambda h, x2=x2, si=si: h.scalar_tensor_tensor(
                out=rs_[:, 64:96], in0=x2, scalar=-1.0, in1=si,
                op0=ALU.mult, op1=ALU.mult),
                reads=[r("kstage"), r("sin")], writes=[r("rope_scr")],
                strict=True)
            p.op("dve", lambda h, x1=x1, si=si: h.tensor_tensor(
                out=rs_[:, 96:128], in0=x1, in1=si, op=ALU.mult),
                reads=[r("kstage"), r("sin")], writes=[r("rope_scr")],
                strict=True)
            p.op("dve", lambda h, x1=x1: h.tensor_tensor(
                out=x1, in0=rs_[:, 0:32], in1=rs_[:, 64:96], op=ALU.add),
                reads=[r("rope_scr")], writes=[r("kstage")], strict=True)
            p.op("dve", lambda h, x2=x2: h.tensor_tensor(
                out=x2, in0=rs_[:, 96:128], in1=rs_[:, 32:64], op=ALU.add),
                reads=[r("rope_scr")], writes=[r("kstage")], strict=True)
            p.op("dve", lambda h, ks=ks, s=s: h.tensor_tensor(
                out=sb["khat_tm"][:, s * 128:(s + 1) * 128], in0=ks,
                in1=sb["wqs_rep"][:], op=ALU.mult),
                reads=[r("kstage"), r("wqs_rep")], writes=[r("khat_tm")],
                strict=True)
        bank = sb["pb4"]

        def ktr(h, bank=bank):
            last = None
            for s in range(4):
                last = h.matmul(bank[:, s * 128:(s + 1) * 128],
                                sb["khat_tm"][:, s * 128:(s + 1) * 128],
                                sb["ident"][:], start=True, stop=True)
            return last
        p.op("pe", ktr, reads=[r("khat_tm"), r("ident")], writes=[r("pb4")])
        p.op("dve", lambda h, bank=bank: h.tensor_copy(
            out=sb["khatT"][:, t0:t0 + 512], in_=bank[:]),
            reads=[r("pb4")], writes=[r("khatT")])

        for mt, (c0, cn) in enumerate([(0, 128), (128, 128), (256, 128),
                                       (384, 64)]):
            bn = "pb%d" % (5 + mt % 2)
            bank = sb[bn]

            def w2mm(h, c0=c0, cn=cn, bank=bank):
                last = None
                for kt in range(DT):
                    last = h.matmul(
                        bank[:cn, :512],
                        sb["W2_bf"][:, kt * 448 + c0:kt * 448 + c0 + cn],
                        sb["xnT"][:, kt * 512:(kt + 1) * 512],
                        start=(kt == 0), stop=(kt == DT - 1))
                return last
            p.op("pe", w2mm, reads=[r("W2_bf"), r("xnT")], writes=[r(bn)])
            if mt == 0:
                p.op("act", lambda h, bank=bank: h.copy(
                    out=sb["czkv"][:, 0:512], in_=bank[:, :512]),
                    reads=[r(bn)], writes=[r("czkv")])
            elif mt == 1:
                p.op("dve", lambda h, bank=bank: h.tensor_tensor(
                    out=sb["czkv"][:, 512:1024].rearrange("p (b m) -> p b m",
                                                          m=16),
                    in0=bank[:, :512].rearrange("p (b m) -> p b m", m=16),
                    in1=sb["biasTkv"][:].unsqueeze(1).to_broadcast([128, 32, 16]),
                    op=ALU.add),
                    reads=[r(bn), r("biasTkv")], writes=[r("czkv")])
            elif mt == 2:
                p.op("dve", lambda h, bank=bank: h.tensor_copy(
                    out=sb["czidx"][:, 0:512], in_=bank[0:64, :512]),
                    reads=[r(bn)], writes=[r("czidx")])
                p.op("dve", lambda h, bank=bank: h.tensor_tensor(
                    out=sb["czidx"][:, 512:1024].rearrange("p (b m) -> p b m",
                                                           m=16),
                    in0=bank[64:128, :512].rearrange("p (b m) -> p b m", m=16),
                    in1=sb["biasTidx"][:].unsqueeze(1).to_broadcast([64, 32, 16]),
                    op=ALU.add),
                    reads=[r(bn), r("biasTidx")], writes=[r("czidx")])
            else:
                p.op("dve", lambda h, bank=bank: h.tensor_copy(
                    out=sb["dq_bf"][:], in_=bank[0:64, :512]),
                    reads=[r(bn)], writes=[r("dq_bf")])
                for mt2 in range(2):
                    bank2 = sb["pb7"]
                    p.op("pe", lambda h, mt2=mt2, bank2=bank2: h.matmul(
                        bank2[:, :512],
                        sb["w_iuq"][:, mt2 * 128:(mt2 + 1) * 128],
                        sb["dq_bf"][:], start=True, stop=True),
                        reads=[r("w_iuq"), r("dq_bf")], writes=[r("pb7")])
                    p.op("dve", lambda h, mt2=mt2, bank2=bank2: h.tensor_copy(
                        out=sb["qiT"][:, mt2 * T + t0:mt2 * T + t0 + 512],
                        in_=bank2[:, :512]),
                        reads=[r("pb7")], writes=[r("qiT")])

        def compress(cz_name, parts, dstname):
            cz = sb[cz_name]
            czr = r(cz_name)
            scr = sb["cmp_scr"]
            red = sb["cmp_red"]
            c_ap = cz[:parts, 0:512]
            z_ap = cz[:parts, 512:1024]
            p.op("dve", lambda h: h.reduce_max(
                out=red[:parts, 0:32],
                in_=z_ap.rearrange("p (b m) -> p b m", m=16),
                axis=AX.X, negate=True),
                reads=[czr], writes=[r("cmp_red")])
            p.op("dve", lambda h: h.tensor_tensor(
                out=scr[:parts, 0:512].rearrange("p (b m) -> p b m", m=16),
                in0=z_ap.rearrange("p (b m) -> p b m", m=16),
                in1=red[:parts, 0:32].unsqueeze(2).to_broadcast([parts, 32, 16]),
                op=ALU.add),
                reads=[czr, r("cmp_red")], writes=[r("cmp_scr")], strict=True)
            p.op("act", lambda h: h.activation(
                out=scr[:parts, 0:512], in_=scr[:parts, 0:512], func=AF.Exp),
                reads=[r("cmp_scr")], writes=[r("cmp_scr")])
            p.op("dve", lambda h: h.reduce_sum(
                out=red[:parts, 32:64],
                in_=scr[:parts, 0:512].rearrange("p (b m) -> p b m", m=16),
                axis=AX.X),
                reads=[r("cmp_scr")], writes=[r("cmp_red")])
            p.op("dve", lambda h: h.tensor_tensor(
                out=scr[:parts, 0:512], in0=scr[:parts, 0:512], in1=c_ap,
                op=ALU.mult),
                reads=[r("cmp_scr"), czr], writes=[r("cmp_scr")], strict=True)
            p.op("dve", lambda h: h.reduce_sum(
                out=red[:parts, 0:32],
                in_=scr[:parts, 0:512].rearrange("p (b m) -> p b m", m=16),
                axis=AX.X),
                reads=[r("cmp_scr")], writes=[r("cmp_red")], strict=True)
            p.op("dve", lambda h: h.reciprocal(
                out=red[:parts, 32:64], in_=red[:parts, 32:64]),
                reads=[r("cmp_red")], writes=[r("cmp_red")], strict=True)
            p.op("dve", lambda h: h.tensor_tensor(
                out=sb[dstname][:parts, u * 32:(u + 1) * 32],
                in0=red[:parts, 0:32], in1=red[:parts, 32:64], op=ALU.mult),
                reads=[r("cmp_red")], writes=[r(dstname)], strict=True)
        compress("czkv", 128, "ckvT")
        compress("czidx", 64, "kcompT")

    def p1_post(self):
        p, sb, r = self.p, self.sb, self.r
        cp = sb["constp"]
        scr = sb["cmp_scr"]
        p.op("dve", lambda h: h.tensor_tensor(
            out=scr[:, 0:256], in0=sb["ckvT"][:], in1=sb["ckvT"][:],
            op=ALU.mult),
            reads=[r("ckvT")], writes=[r("cmp_scr")])
        bank = sb["pb0"]
        p.op("pe", lambda h: h.matmul(bank[:1, 0:256], sb["ones_col"][:],
                                      scr[:, 0:256], start=True, stop=True),
             reads=[r("ones_col"), r("cmp_scr")], writes=[r("pb0")])
        p.op("act", lambda h: h.activation(
            out=scr[:1, 256:512], in_=bank[:1, 0:256], func=AF.Ln,
            bias=cp[:1, 0:1], scale=cp[:1, 4:5]),
            reads=[r("pb0"), r("constp")], writes=[r("cmp_scr")])
        p.op("act", lambda h: h.activation(
            out=scr[:1, 256:512], in_=scr[:1, 256:512], func=AF.Exp,
            bias=0.0, scale=cp[:1, 2:3]),
            reads=[r("cmp_scr"), r("constp")], writes=[r("cmp_scr")],
            strict=True)
        bank1 = sb["pb1"]
        p.op("pe", lambda h: h.matmul(bank1[:, 0:256], sb["ones_row"][:],
                                      scr[:1, 256:512], start=True, stop=True),
             reads=[r("ones_row"), r("cmp_scr")], writes=[r("pb1")])
        p.op("dve", lambda h: h.tensor_tensor(
            out=scr[:, 0:256], in0=sb["ckvT"][:], in1=bank1[:, 0:256],
            op=ALU.mult),
            reads=[r("ckvT"), r("pb1")], writes=[r("cmp_scr")])
        p.op("dve", lambda h: h.tensor_scalar(
            out=sb["skv_ss"][:], in0=scr[:, 0:256],
            scalar1=sb["gains_col"][:, 0:1], scalar2=None, op0=ALU.mult),
            reads=[r("cmp_scr"), r("gains_col")], writes=[r("skv_ss")])
        p.op("dve", lambda h: h.tensor_copy(
            out=sb["skv_stg"][:], in_=scr[:, 0:256]),
            reads=[r("cmp_scr")], writes=[r("skv_stg")])
        bank2 = sb["pb2"]

        def tr2(h, bank2=bank2):
            last = None
            for i in range(2):
                last = h.matmul(
                    bank2[:, i * 128:(i + 1) * 128],
                    sb["skv_stg"][:, i * 128:(i + 1) * 128],
                    sb["ident"][:], start=True, stop=True)
            return last
        p.op("pe", tr2, reads=[r("skv_stg"), r("ident")], writes=[r("pb2")])
        p.op("dve", lambda h, bank2=bank2: h.tensor_copy(
            out=sb["skv_pv"][:], in_=bank2[:, 0:256]),
            reads=[r("pb2")], writes=[r("skv_pv")])
        p.op("dve", lambda h: h.tensor_tensor(
            out=sb["skv_pv"][:].rearrange("p (i c) -> p i c", i=2),
            in0=sb["skv_pv"][:].rearrange("p (i c) -> p i c", i=2),
            in1=sb["wk_rep"][:].unsqueeze(1).to_broadcast([128, 2, 128]),
            op=ALU.mult),
            reads=[r("wk_rep")], writes=[r("skv_pv")], strict=True)
        p.op("dve", lambda h: h.tensor_copy(out=sb["kcomp_bf"][0:64, :],
                                            in_=sb["kcompT"][:]),
             reads=[r("kcompT")], writes=[r("kcomp_bf")])
        p.op("pool", lambda h: h.dma_start(out=sb["kcomp_bf"][64:128, :],
                                           in_=sb["kcomp_bf"][0:64, :]),
             reads=[r("kcomp_bf")], writes=[r("kcomp_bf")], dma="kcompdup")

    def p2_alloc(self, ctx):
        p, dr = self.p, self.dram
        sb = self.sbuf
        sb("causal", (128, NT * NBLK), BF16, ctx)
        sb("fw_f32", (128, D), F32, ctx)
        sb("fw_bf", (128, D), BF16, ctx)
        sb("iscc", (128, NBLK), F32, ctx)
        sb("isc2", (128, NBLK), F32, ctx)
        sb("m8", (128, 8), F32, ctx)
        p.op("sp", lambda h: h.dma_start(out=self.sb["causal"][:],
                                         in_=dr["causal"][:, :]),
             writes=[self.r("causal")], dma="causal")

    def p2_fwconv(self, kt):
        p, sb, dr, r = self.p, self.sb, self.dram, self.r
        p.op("sp", lambda h, kt=kt: h.dma_start(
            out=sb["fw_f32"][:], in_=dr["final_w"][kt * 128:(kt + 1) * 128, :]),
            writes=[r("fw_f32")], dma="fw_f32")
        p.op("dve", lambda h: h.tensor_copy(out=sb["fw_bf"][:],
                                            in_=sb["fw_f32"][:]),
             reads=[r("fw_f32")], writes=[r("fw_bf")])
        p.op("pool", lambda h, kt=kt: h.dma_start(
            out=self.dram["fwT_scr"][kt, :, :], in_=sb["fw_bf"][:]),
            reads=[r("fw_bf")], dma="fwbf_st")

    def p2(self, st):
        p, sb, r = self.p, self.sb, self.r
        t0 = st * 128
        cp = sb["constp"]
        for ih in range(NIH):
            bn = "pb%d" % (ih % 2)
            bank = sb[bn]
            ptile = ih // 2
            row0 = (ih % 2) * 64
            p.op("pe", lambda h, bank=bank, ptile=ptile, row0=row0: h.matmul(
                bank[:, 0:NBLK],
                sb["qiT"][row0:row0 + 64, ptile * T + t0:ptile * T + t0 + 128],
                sb["kcomp_bf"][row0:row0 + 64, :], start=True, stop=True),
                reads=[r("qiT"), r("kcomp_bf")], writes=[r(bn)])
            wcol = sb["wih"][:, st * 4 + ih:st * 4 + ih + 1]
            if ih == 0:
                p.op("dve", lambda h, bank=bank, wcol=wcol: h.scalar_tensor_tensor(
                    out=sb["iscc"][:], in0=bank[:, 0:NBLK], scalar=0.0,
                    in1=wcol.to_broadcast([128, NBLK]), op0=ALU.max, op1=ALU.mult),
                    reads=[r(bn), r("wih")], writes=[r("iscc")])
            else:
                p.op("dve", lambda h, bank=bank, wcol=wcol: h.scalar_tensor_tensor(
                    out=sb["isc2"][:], in0=bank[:, 0:NBLK], scalar=0.0,
                    in1=wcol.to_broadcast([128, NBLK]), op0=ALU.max, op1=ALU.mult),
                    reads=[r(bn), r("wih")], writes=[r("isc2")])
                p.op("dve", lambda h: h.tensor_tensor(
                    out=sb["iscc"][:], in0=sb["iscc"][:], in1=sb["isc2"][:],
                    op=ALU.add),
                    reads=[r("isc2")], writes=[r("iscc")], strict=True)
        p.op("dve", lambda h, st=st: h.tensor_tensor(
            out=sb["iscc"][:], in0=sb["iscc"][:],
            in1=sb["causal"][:, st * NBLK:(st + 1) * NBLK], op=ALU.add),
            reads=[r("causal")], writes=[r("iscc")], strict=True)
        for rnd in range(8):
            p.op("dve", lambda h: h.max(out=sb["m8"][:], in_=sb["iscc"][:]),
                 reads=[r("iscc")], writes=[r("m8")], strict=True)
            p.op("dve", lambda h: h.match_replace(
                out=sb["iscc"][:], in_to_replace=sb["m8"][:],
                in_values=sb["iscc"][:], imm_value=SENT),
                reads=[r("m8")], writes=[r("iscc")], strict=True)
        p.op("dve", lambda h: h.tensor_scalar(
            out=sb["isc2"][:], in0=sb["iscc"][:], scalar1=SENT, scalar2=None,
            op0=ALU.is_equal),
            reads=[r("iscc")], writes=[r("isc2")], strict=True)
        p.op("act", lambda h: h.activation(
            out=sb["isc2"][:], in_=sb["isc2"][:], func=AF.Identity,
            bias=cp[:, 7:8], scale=cp[:, 6:7]),
            reads=[r("isc2"), r("constp")], writes=[r("isc2")])
        p.op("dve", lambda h, st=st: h.tensor_tensor(
            out=sb["mask_sb"][:, st * NBLK:(st + 1) * NBLK], in0=sb["isc2"][:],
            in1=sb["causal"][:, st * NBLK:(st + 1) * NBLK], op=ALU.add),
            reads=[r("isc2"), r("causal")], writes=[r("mask_sb")])

    def p3_alloc(self, ctx):
        p, dr = self.p, self.dram
        sb = self.sbuf
        sb("winmask", (128, 4 * 512), BF16, ctx)
        sb("w_qup_bf", (128, 4 * D), BF16, ctx)
        sb("groupw_bf", (128, 16 * 512), BF16, ctx)
        sb("wstage3", (128, 512), F32, ctx)
        for s in range(2):
            sb("qpre%d" % s, (128, 512), F32, ctx)
            sb("qrt%d" % s, (128, 512), F32, ctx)
            sb("qhat_tm%d" % s, (128, 512), BF16, ctx)
            sb("qstat%d" % s, (128, 4), F32, ctx)
            sb("ssb%d" % s, (128, 256), F32, ctx)
            sb("wsb%d" % s, (128, 512), F32, ctx)
            sb("nd%d" % s, (128, 4), F32, ctx)
            sb("pbf%d" % s, (128, 256), BF16, ctx)
            sb("pwb%d" % s, (128, 512), BF16, ctx)
            sb("ptb%d" % s, (128, 256), BF16, ctx)
            sb("ptw%d" % s, (128, 512), BF16, ctx)
            sb("tmps%d" % s, (128, 128), F32, ctx)
        sb("qhatT", (128, 16 * 512), BF16, ctx)
        sb("attn_bf", (128, 16 * 512), BF16, ctx)
        sb("goT", (128, 16 * 512), BF16, ctx)
        sb("sstage", (128, D), BF16, ctx)
        sb("fwstage", (128, 2 * D), BF16, ctx)
        self.res["fwstage0"] = Res()
        self.res["fwstage1"] = Res()
        p.op("sp", lambda h: h.dma_start(out=self.sb["winmask"][:],
                                         in_=dr["winmask"][:, :]),
             writes=[self.r("winmask")], dma="winmask")

        def conv(dstname, col, src_ap, ncols):
            st = self.sb["wstage3"]
            p.op("sp", lambda h, s=src_ap, n=ncols: h.dma_start(
                out=st[:, :n], in_=s), writes=[self.r("wstage3")], dma="wstage3")
            p.op("dve", lambda h, d=self.sb[dstname], c=col, n=ncols:
                 h.tensor_copy(out=d[:, c:c + n], in_=st[:, :n]),
                 reads=[self.r("wstage3")], writes=[self.r(dstname)])
        for i in range(4):
            for c4 in range(4):
                conv("w_qup_bf", i * D + c4 * 512,
                     dr["w_qup"][i * 128:(i + 1) * 128, c4 * 512:(c4 + 1) * 512],
                     512)
        for gi in range(4):
            for isub in range(4):
                conv("groupw_bf", (gi * 4 + isub) * 512,
                     dr["group_w"][gi, isub * 128:(isub + 1) * 128, :], 512)

    def p3(self, u):
        p, sb, dr, r = self.p, self.sb, self.dram, self.r
        t0 = u * 512
        cp = sb["constp"]

        for it in range(16) if "qhat" not in DISABLE else []:
            tsub, hg = it // 4, it % 4
            gt0 = t0 + tsub * 128
            par = it % 2
            qpre = sb["qpre%d" % par]
            qrt = sb["qrt%d" % par]
            qstat = sb["qstat%d" % par]
            qhat_tm = sb["qhat_tm%d" % par]
            rpre, rrt, rst, rtm = (r("qpre%d" % par), r("qrt%d" % par),
                                   r("qstat%d" % par), r("qhat_tm%d" % par))
            bn0 = "pb%d" % (par * 2)
            bn1 = "pb%d" % (par * 2 + 1)
            bank = sb[bn0]

            def qmm(h, gt0=gt0, hg=hg, bank=bank):
                last = None
                for mt in range(4):
                    last = h.matmul(
                        bank[:],
                        sb["q_cT"][:, mt * T + gt0:mt * T + gt0 + 128],
                        sb["w_qup_bf"][:, mt * D + hg * 512:
                                       mt * D + (hg + 1) * 512],
                        start=(mt == 0), stop=(mt == 3))
                return last
            p.op("pe", qmm, reads=[r("q_cT"), r("w_qup_bf")], writes=[r(bn0)])
            p.op("act", lambda h, bank=bank, qpre=qpre: h.copy(
                out=qpre[:], in_=bank[:]),
                reads=[r(bn0)], writes=[rpre])
            st_idx = gt0 // 128
            co = sb["cos"][:, st_idx * 32:(st_idx + 1) * 32]
            si = sb["sin"][:, st_idx * 32:(st_idx + 1) * 32]
            cob = co.unsqueeze(1).to_broadcast([128, 4, 32])
            sib = si.unsqueeze(1).to_broadcast([128, 4, 32])
            x1 = qpre[:].rearrange("p (h c) -> p h c", h=4)[:, :, 0:32]
            x2 = qpre[:].rearrange("p (h c) -> p h c", h=4)[:, :, 32:64]
            a_ = qrt[:, 0:128].rearrange("p (h c) -> p h c", h=4)
            b_ = qrt[:, 128:256].rearrange("p (h c) -> p h c", h=4)
            c_ = qrt[:, 256:384].rearrange("p (h c) -> p h c", h=4)
            d_ = qrt[:, 384:512].rearrange("p (h c) -> p h c", h=4)
            for dst, s0, s1 in ((a_, x1, cob), (b_, x2, sib),
                                (c_, x1, sib), (d_, x2, cob)):
                p.op("dve", lambda h, dst=dst, s0=s0, s1=s1: h.tensor_tensor(
                    out=dst, in0=s0, in1=s1, op=ALU.mult),
                    reads=[rpre, r("cos"), r("sin")], writes=[rrt])
            p.op("dve", lambda h, a_=a_, b_=b_, x1=x1: h.tensor_tensor(
                out=x1, in0=a_, in1=b_, op=ALU.subtract),
                reads=[rrt], writes=[rpre])
            p.op("dve", lambda h, c_=c_, d_=d_, x2=x2: h.tensor_tensor(
                out=x2, in0=c_, in1=d_, op=ALU.add),
                reads=[rrt], writes=[rpre])
            for hh4 in range(4):
                p.op("act", lambda h, hh4=hh4, qpre=qpre, qrt=qrt, qstat=qstat:
                     h.activation(
                         out=qrt[:, 0:128],
                         in_=qpre[:, hh4 * 128:(hh4 + 1) * 128],
                         func=AF.Square, accum_out=qstat[:, hh4:hh4 + 1]),
                     reads=[rpre], writes=[rrt, rst])
            p.op("act", lambda h, qstat=qstat: h.activation(
                out=qstat[:], in_=qstat[:], func=AF.Ln,
                bias=cp[:, 0:1], scale=cp[:, 4:5]),
                reads=[rst, r("constp")], writes=[rst])
            p.op("act", lambda h, qstat=qstat: h.activation(
                out=qstat[:], in_=qstat[:], func=AF.Exp,
                bias=0.0, scale=cp[:, 2:3]),
                reads=[rst, r("constp")], writes=[rst])
            p.op("dve", lambda h, qpre=qpre, qstat=qstat, qhat_tm=qhat_tm:
                 h.tensor_tensor(
                     out=qhat_tm[:],
                     in0=qpre[:].rearrange("p (h c) -> p h c", h=4),
                     in1=qstat[:].unsqueeze(2).to_broadcast([128, 4, 128]),
                     op=ALU.mult),
                 reads=[rst, rpre], writes=[rtm])
            bank1 = sb[bn1]

            def qtr(h, bank1=bank1, qhat_tm=qhat_tm):
                last = None
                for hh4 in range(4):
                    last = h.matmul(
                        bank1[:, hh4 * 128:(hh4 + 1) * 128],
                        qhat_tm[:, hh4 * 128:(hh4 + 1) * 128],
                        sb["ident"][:], start=True, stop=True)
                return last
            p.op("pe", qtr, reads=[rtm, r("ident")], writes=[r(bn1)])
            p.op("dve", lambda h, bank1=bank1, hg=hg, tsub=tsub: h.tensor_copy(
                out=sb["qhatT"][:].rearrange(
                    "p (hd t) -> p hd t", hd=16)[:, hg * 4:hg * 4 + 4,
                                                 tsub * 128:(tsub + 1) * 128],
                in_=bank1[:].rearrange("p (hh t) -> p hh t", hh=4)),
                reads=[r(bn1)], writes=[r("qhatT")])

        for tsub in range(4) if "attn" not in DISABLE else []:
            st_idx = (t0 + tsub * 128) // 128
            nblk = st_idx // 2
            parity = st_idx % 2
            sel = parity + (2 if nblk == 0 else 0)
            for wave in range(8):
                heads = (wave * 2, wave * 2 + 1)
                ctxs = []
                for i, hh in enumerate(heads):
                    s = i
                    ctxs.append(dict(
                        hh=hh, s=s,
                        qsl=sb["qhatT"][:, hh * 512 + tsub * 128:
                                        hh * 512 + (tsub + 1) * 128],
                        bA="pb%d" % s, bB="pb%d" % (2 + s),
                        bC="pb%d" % (4 + s), bD="pb%d" % (6 + s),
                        ssb=sb["ssb%d" % s], wsb=sb["wsb%d" % s],
                        nd=sb["nd%d" % s], pbf=sb["pbf%d" % s],
                        pwb=sb["pwb%d" % s], ptb=sb["ptb%d" % s],
                        ptw=sb["ptw%d" % s],
                        rs=r("ssb%d" % s), rw=r("wsb%d" % s),
                        rnd=r("nd%d" % s), rpb=r("pbf%d" % s),
                        rpw=r("pwb%d" % s), rtb=r("ptb%d" % s),
                        rtw=r("ptw%d" % s), tmps=sb["tmps%d" % s],
                        rtm=r("tmps%d" % s)))
                for c in ctxs:
                    p.op("pe", lambda h, c=c: h.matmul(
                        sb[c["bA"]][:, 0:NBLK], c["qsl"], sb["skv_ss"][:, :],
                        start=True, stop=True),
                        reads=[r("qhatT"), r("skv_ss")], writes=[r(c["bA"])])
                for c in ctxs:
                    if nblk == 0:
                        def ws0(h, c=c):
                            h.matmul(sb[c["bB"]][:, 0:256], c["qsl"],
                                     sb["khatT"][:, 0:256],
                                     start=True, stop=True)
                            return h.matmul(sb[c["bB"]][:, 256:512], c["qsl"],
                                            sb["khatT"][:, 0:256],
                                            start=True, stop=True)
                        p.op("pe", ws0, reads=[r("qhatT"), r("khatT")],
                             writes=[r(c["bB"])])
                    else:
                        p.op("pe", lambda h, c=c, nblk=nblk: h.matmul(
                            sb[c["bB"]][:], c["qsl"],
                            sb["khatT"][:, (nblk - 1) * 256:(nblk + 1) * 256],
                            start=True, stop=True),
                            reads=[r("qhatT"), r("khatT")], writes=[r(c["bB"])])
                for c in ctxs:
                    p.op("dve", lambda h, c=c, st_idx=st_idx: h.tensor_tensor(
                        out=c["ssb"][:], in0=sb[c["bA"]][:, 0:NBLK],
                        in1=sb["mask_sb"][:, st_idx * NBLK:(st_idx + 1) * NBLK],
                        op=ALU.add),
                        reads=[r(c["bA"]), r("mask_sb")], writes=[c["rs"]])
                for c in ctxs:
                    p.op("dve", lambda h, c=c: h.reduce_max(
                        out=c["nd"][:, 0:1], in_=c["ssb"][:], axis=AX.X,
                        negate=True),
                        reads=[c["rs"]], writes=[c["rnd"]])
                for c in ctxs:
                    p.op("act", lambda h, c=c: h.activation(
                        out=c["ssb"][:], in_=c["ssb"][:], func=AF.Exp,
                        bias=c["nd"][:, 0:1], scale=1.0,
                        accum_out=c["nd"][:, 1:2]),
                        reads=[c["rs"], c["rnd"]], writes=[c["rs"], c["rnd"]])
                for c in ctxs:
                    p.op("dve", lambda h, c=c, sel=sel: h.tensor_tensor(
                        out=c["wsb"][:], in0=sb[c["bB"]][:],
                        in1=sb["winmask"][:, sel * 512:(sel + 1) * 512],
                        op=ALU.add),
                        reads=[r(c["bB"]), r("winmask")], writes=[c["rw"]])
                for c in ctxs:
                    p.op("dve", lambda h, c=c: h.reduce_max(
                        out=c["nd"][:, 2:3], in_=c["wsb"][:], axis=AX.X,
                        negate=True),
                        reads=[c["rw"]], writes=[c["rnd"]])
                for c in ctxs:
                    p.op("act", lambda h, c=c: h.activation(
                        out=c["wsb"][:], in_=c["wsb"][:], func=AF.Exp,
                        bias=c["nd"][:, 2:3], scale=1.0,
                        accum_out=c["nd"][:, 3:4]),
                        reads=[c["rw"], c["rnd"]], writes=[c["rw"], c["rnd"]])
                for c in ctxs:
                    p.op("dve", lambda h, c=c: h.reciprocal(
                        out=c["nd"][:, 1:2], in_=c["nd"][:, 1:2]),
                        reads=[c["rnd"]], writes=[c["rnd"]])
                    p.op("dve", lambda h, c=c: h.tensor_scalar(
                        out=c["pbf"][:], in0=c["ssb"][:],
                        scalar1=c["nd"][:, 1:2], scalar2=None, op0=ALU.mult),
                        reads=[c["rs"], c["rnd"]], writes=[c["rpb"]])
                    if st_idx == 0:
                        p.op("dve", lambda h, c=c: h.memset(
                            c["pbf"][0:16, :], 0.0),
                            writes=[c["rpb"]], strict=True)
                for c in ctxs:
                    def ptr(h, c=c):
                        last = None
                        for i in range(2):
                            last = h.matmul(
                                sb[c["bC"]][:, i * 128:(i + 1) * 128],
                                c["pbf"][:, i * 128:(i + 1) * 128],
                                sb["ident"][:], start=True, stop=True)
                        return last
                    p.op("pe", ptr, reads=[c["rpb"], r("ident")],
                         writes=[r(c["bC"])])
                for c in ctxs:
                    p.op("act", lambda h, c=c: h.copy(
                        out=c["ptb"][:], in_=sb[c["bC"]][:, 0:256]),
                        reads=[r(c["bC"])], writes=[c["rtb"]])
                for c in ctxs:
                    p.op("dve", lambda h, c=c: h.reciprocal(
                        out=c["nd"][:, 3:4], in_=c["nd"][:, 3:4]),
                        reads=[c["rnd"]], writes=[c["rnd"]])
                    p.op("dve", lambda h, c=c: h.tensor_scalar(
                        out=c["pwb"][:], in0=c["wsb"][:],
                        scalar1=c["nd"][:, 3:4], scalar2=None, op0=ALU.mult),
                        reads=[c["rw"], c["rnd"]], writes=[c["rpw"]])
                for c in ctxs:
                    def pvs(h, c=c):
                        last = None
                        for i in range(2):
                            last = h.matmul(sb[c["bC"]][:, 256:384],
                                            sb["skv_pv"][:, i * 128:(i + 1) * 128],
                                            c["ptb"][:, i * 128:(i + 1) * 128],
                                            start=(i == 0), stop=(i == 1))
                        return last
                    p.op("pe", pvs, reads=[r("skv_pv"), c["rtb"]],
                         writes=[r(c["bC"])])
                for c in ctxs:
                    p.op("act", lambda h, c=c: h.copy(
                        out=c["tmps"][:], in_=sb[c["bC"]][:, 256:384]),
                        reads=[r(c["bC"])], writes=[c["rtm"]])
                for c in ctxs:
                    def wtr(h, c=c):
                        last = None
                        for ks in range(4):
                            last = h.matmul(sb[c["bD"]][:, ks * 128:(ks + 1) * 128],
                                            c["pwb"][:, ks * 128:(ks + 1) * 128],
                                            sb["ident"][:], start=True, stop=True)
                        return last
                    p.op("pe", wtr, reads=[c["rpw"], r("ident")],
                         writes=[r(c["bD"])])
                for c in ctxs:
                    p.op("act", lambda h, c=c: h.copy(
                        out=c["ptw"][:], in_=sb[c["bD"]][:]),
                        reads=[r(c["bD"])], writes=[c["rtw"]])
                for c in ctxs:
                    ks_lo = 2 if nblk == 0 else 0

                    def pvw(h, c=c, ks_lo=ks_lo, nblk=nblk):
                        last = None
                        for ks in range(ks_lo, 4):
                            ktile = (nblk - 1) * 2 + ks
                            last = h.matmul(sb[c["bD"]][:, 0:128],
                                            sb["v_sb"][:, ktile * 128:(ktile + 1) * 128],
                                            c["ptw"][:, ks * 128:(ks + 1) * 128],
                                            start=(ks == ks_lo), stop=(ks == 3))
                        return last
                    p.op("pe", pvw, reads=[r("v_sb"), c["rtw"]],
                         writes=[r(c["bD"])])
                for c in ctxs:
                    p.op("dve", lambda h, c=c, tsub=tsub: h.tensor_tensor(
                        out=sb["attn_bf"][:, c["hh"] * 512 + tsub * 128:
                                          c["hh"] * 512 + (tsub + 1) * 128],
                        in0=c["tmps"][:], in1=sb[c["bD"]][:, 0:128], op=ALU.add),
                        reads=[c["rtm"], r(c["bD"])], writes=[r("attn_bf")])

        for g in range(4) if "group" not in DISABLE else []:
            for osub in range(4):
                bn = "pb%d" % ((g * 4 + osub) % 2)
                bank = sb[bn]

                def gmm(h, g=g, osub=osub, bank=bank):
                    last = None
                    for isub in range(4):
                        last = h.matmul(
                            bank[:],
                            sb["groupw_bf"][:, (g * 4 + isub) * 512 + osub * 128:
                                            (g * 4 + isub) * 512 + (osub + 1) * 128],
                            sb["attn_bf"][:, (g * 4 + isub) * 512:
                                          (g * 4 + isub + 1) * 512],
                            start=(isub == 0), stop=(isub == 3))
                    return last
                p.op("pe", gmm, reads=[r("groupw_bf"), r("attn_bf")],
                     writes=[r(bn)])
                p.op("dve", lambda h, g=g, osub=osub, bank=bank: h.tensor_copy(
                    out=sb["goT"][:, (g * 4 + osub) * 512:
                                  (g * 4 + osub + 1) * 512], in_=bank[:]),
                    reads=[r(bn)], writes=[r("goT")])

        for pair in range(2) if "final" not in DISABLE else []:
            for kt in range(DT):
                slot = kt % 2
                fw = sb["fwstage"][:, slot * D:(slot + 1) * D]
                p.op("sp", lambda h, fw=fw, kt=kt: h.dma_start(
                    out=fw, in_=dr["fwT_scr"][kt, :, :]),
                    writes=[r("fwstage%d" % slot)], dma="fwstage%d" % slot)
                for ts2 in range(2):
                    tsub = pair * 2 + ts2
                    bn = "pb%d" % (ts2 * 4)
                    for oc in range(4):
                        bno = "pb%d" % (ts2 * 4 + oc)
                        p.op("pe", lambda h, kt=kt, tsub=tsub, oc=oc, fw=fw,
                             bno=bno: h.matmul(
                                 sb[bno][:],
                                 sb["goT"][:, kt * 512 + tsub * 128:
                                           kt * 512 + (tsub + 1) * 128],
                                 fw[:, oc * 512:(oc + 1) * 512],
                                 start=(kt == 0), stop=(kt == DT - 1)),
                             reads=[r("goT"), r("fwstage%d" % (kt % 2))],
                             writes=[r(bno)])
            for ts2 in range(2):
                tsub = pair * 2 + ts2
                st_idx = (t0 + tsub * 128) // 128
                for oc in range(4):
                    bno = "pb%d" % (ts2 * 4 + oc)
                    p.op("dve", lambda h, oc=oc, bno=bno: h.tensor_copy(
                        out=sb["sstage"][:, oc * 512:(oc + 1) * 512],
                        in_=sb[bno][:]),
                        reads=[r(bno)], writes=[r("sstage")])
                p.op("pool", lambda h, st=st_idx: h.dma_start(
                    out=dr["s_scr"][st, :, :], in_=sb["sstage"][:]),
                    reads=[r("sstage")], dma="s_sp")

    def p05_alloc(self, ctx):
        sb = self.sbuf
        sb("wq_i32", (128, 1024), I32, ctx)
        sb("wq_bf", (128, 2048), BF16, ctx)
        sb("tev", (128, 512), BF16, ctx)
        self._units = ([("g", j, 0) for j in range(HT)]
                       + [("u", j, 0) for j in range(HT)]
                       + [("d", ot, part) for ot in range(DT) for part in range(3)])
        self._ui = 0

    def p05_chunk(self, n_units):
        p, sb, dr, r = self.p, self.sb, self.dram, self.r
        for _ in range(n_units):
            if self._ui >= len(self._units):
                return
            kind, j, part = self._units[self._ui]
            self._ui += 1
            i32 = sb["wq_i32"]
            wbf = sb["wq_bf"][:, 0:2048]

            def load_conv(src_rows, col0, ncol):
                done = 0
                while done < ncol:
                    n = min(1024, ncol - done)
                    p.op("sp", lambda h, s=src_rows, d0=done, n=n: h.dma_start(
                        out=i32[:, :n], in_=s[:, col0 + d0:col0 + d0 + n]),
                        writes=[r("wq_i32")], dma="wq_i32")
                    p.op("dve", lambda h, d0=done, n=n: h.tensor_scalar(
                        out=wbf[:, col0 + d0:col0 + d0 + n], in0=i32[:, :n],
                        scalar1=-128.0, scalar2=None, op0=ALU.add),
                        reads=[r("wq_i32")], writes=[r("wq_bf")])
                    done += n
            if kind in ("g", "u"):
                src = dr["wq_gate"] if kind == "g" else dr["wq_up"]
                dst = dr["wgT_scr"] if kind == "g" else dr["wuT_scr"]
                load_conv(src[j * 128:(j + 1) * 128, :], 0, 2048)
                for grp in range(4):
                    bn = "pb%d" % (6 + grp % 2)
                    bank = sb[bn]

                    def tmm(h, wbf=wbf, grp=grp, bank=bank):
                        last = None
                        for s in range(4):
                            dt = grp * 4 + s
                            last = h.matmul(bank[:, s * 128:(s + 1) * 128],
                                            wbf[:, dt * 128:(dt + 1) * 128],
                                            sb["ident"][:], start=True,
                                            stop=True)
                        return last
                    p.op("pe", tmm, reads=[r("wq_bf"), r("ident")],
                         writes=[r(bn)])
                    tv = sb["tev"][:, 0:512]
                    p.op("act", lambda h, tv=tv, bank=bank: h.copy(
                        out=tv, in_=bank[:]),
                        reads=[r(bn)], writes=[r("tev")])
                    p.op("pool", lambda h, tv=tv, dst=dst, j=j, grp=grp:
                         h.dma_start(
                             out=dst[j, :, grp * 512:(grp + 1) * 512], in_=tv),
                         reads=[r("tev")], dma="tev_st")
            elif kind == "d":
                ncol = 2048 if part < 2 else 1536
                nsub = ncol // 128
                base_ht = part * 16
                load_conv(dr["wq_down"][j * 128:(j + 1) * 128,
                                        part * 2048:part * 2048 + 2048][:, :ncol]
                          if False else
                          dr["wq_down"][j * 128:(j + 1) * 128,
                                        part * 2048:part * 2048 + ncol], 0, ncol)
                for grp in range(nsub // 4):
                    bn = "pb%d" % (6 + grp % 2)
                    bank = sb[bn]

                    def tmm(h, wbf=wbf, grp=grp, bank=bank):
                        last = None
                        for s in range(4):
                            ht_ = grp * 4 + s
                            last = h.matmul(bank[:, s * 128:(s + 1) * 128],
                                            wbf[:, ht_ * 128:(ht_ + 1) * 128],
                                            sb["ident"][:], start=True,
                                            stop=True)
                        return last
                    p.op("pe", tmm, reads=[r("wq_bf"), r("ident")],
                         writes=[r(bn)])
                    tv = sb["tev"][:, 0:512]
                    sdsl = sb["sdown"][:, j * 128:(j + 1) * 128]
                    p.op("dve", lambda h, tv=tv, bank=bank, sdsl=sdsl:
                         h.tensor_tensor(
                             out=tv.rearrange("p (s o) -> p s o", s=4),
                             in0=bank[:].rearrange("p (s o) -> p s o", s=4),
                             in1=sdsl.unsqueeze(1).to_broadcast([128, 4, 128]),
                             op=ALU.mult),
                         reads=[r(bn), r("sdown")], writes=[r("tev")])
                    p.op("pool", lambda h, tv=tv, j=j, b=base_ht, grp=grp:
                         h.dma_start(
                             out=dr["wdT_scr"][b + grp * 4:b + (grp + 1) * 4,
                                               :, j * 128:(j + 1) * 128].rearrange(
                                 "ht p o -> p ht o"),
                             in_=tv.rearrange("p (ht o) -> p ht o", ht=4)),
                         reads=[r("tev")], dma="tev_st")
            else:
                raise AssertionError(kind)

    def p4_alloc(self, ctx):
        p, dr = self.p, self.dram
        sb = self.sbuf
        sb("sgate", (128, HT), F32, ctx)
        sb("sup", (128, HT), F32, ctx)
        sb("xnTs", (128, DT * 1024), BF16, ctx)
        for s in range(2):
            sb("wg_t%d" % s, (128, DT * 128), BF16, ctx)
            sb("wu_t%d" % s, (128, DT * 128), BF16, ctx)
            sb("wd_t%d" % s, (128, D), BF16, ctx)
        sb("hact", (128, HT * 1024), BF16, ctx)
        sb("silu_s", (128, 2 * 512), BF16, ctx)
        self.res["silu_s1"] = Res()
        sb("xres", (128, D), F32, ctx)
        sb("sres", (128, D), BF16, ctx)
        sb("gbuf", (128, D), BF16, ctx)
        sb("obuf", (128, D), F32, ctx)
        for nm in ("sgate", "sup"):
            p.op("sp", lambda h, d=self.sb[nm], s=dr[nm]: h.dma_start(
                out=d[:], in_=s[:, :]), writes=[self.r(nm)], dma=nm)

    def p4(self, v):
        p, sb, dr, r = self.p, self.sb, self.dram, self.r
        t0 = v * 1024
        p.op("pool", lambda h: h.dma_start(
            out=sb["xnTs"][:].rearrange("p (dt t) -> p dt t", dt=DT),
            in_=dr["xnT_scr"][:, :, t0:t0 + 1024].rearrange("dt p t -> p dt t")),
            writes=[r("xnTs")], dma="xnTs")

        for j in range(HT):
            sl = j % 2
            wg = sb["wg_t%d" % sl]
            wu = sb["wu_t%d" % sl]
            p.op("sp", lambda h, wg=wg, j=j: h.dma_start(
                out=wg[:], in_=dr["wgT_scr"][j, :, :]),
                writes=[r("wg_t%d" % sl)], dma="wg_t%d" % sl)
            p.op("sp", lambda h, wu=wu, j=j: h.dma_start(
                out=wu[:], in_=dr["wuT_scr"][j, :, :]),
                writes=[r("wu_t%d" % sl)], dma="wu_t%d" % sl)
            bg = "pb%d" % (2 * sl)
            bu = "pb%d" % (2 * sl + 1)
            silu_res = "silu_s" if sl == 0 else "silu_s1"
            ssl = sb["silu_s"][:, sl * 512:(sl + 1) * 512]
            for c in range(2):
                for which, wt, bn in (("g", wg, bg), ("u", wu, bu)):
                    bank = sb[bn]

                    def gmm(h, wt=wt, c=c, bank=bank):
                        last = None
                        for dt in range(DT):
                            last = h.matmul(
                                bank[:],
                                wt[:, dt * 128:(dt + 1) * 128],
                                sb["xnTs"][:, dt * 1024 + c * 512:
                                           dt * 1024 + c * 512 + 512],
                                start=(dt == 0), stop=(dt == DT - 1))
                        return last
                    p.op("pe", gmm, reads=[r("wg_t%d" % sl) if which == "g"
                                           else r("wu_t%d" % sl), r("xnTs")],
                         writes=[r(bn)])
                p.op("act", lambda h, j=j, bg=bg, ssl=ssl: h.activation(
                    out=ssl, in_=sb[bg][:], func=AF.Silu,
                    bias=0.0, scale=sb["sgate"][:, j:j + 1]),
                    reads=[r(bg), r("sgate")], writes=[r(silu_res)])
                p.op("dve", lambda h, j=j, c=c, bu=bu, ssl=ssl:
                     h.scalar_tensor_tensor(
                         out=sb["hact"][:, j * 1024 + c * 512:
                                        j * 1024 + c * 512 + 512],
                         in0=sb[bu][:], scalar=sb["sup"][:, j:j + 1],
                         in1=ssl, op0=ALU.mult, op1=ALU.mult),
                     reads=[r(bu), r("sup"), r(silu_res)], writes=[r("hact")])

        for tg in range(4):
            for j in range(HT):
                sl = j % 2
                wd = sb["wd_t%d" % sl]
                p.op("sp", lambda h, wd=wd, j=j: h.dma_start(
                    out=wd[:], in_=dr["wdT_scr"][j, :, :]),
                    writes=[r("wd_t%d" % sl)], dma="wd_t%d" % sl)
                for ts2 in range(2):
                    for oc in range(4):
                        bno = "pb%d" % (ts2 * 4 + oc)
                        p.op("pe", lambda h, j=j, tg=tg, ts2=ts2, oc=oc, wd=wd,
                             bno=bno: h.matmul(
                                 sb[bno][:],
                                 sb["hact"][:, j * 1024 + tg * 256 + ts2 * 128:
                                            j * 1024 + tg * 256 + (ts2 + 1) * 128],
                                 wd[:, oc * 512:(oc + 1) * 512],
                                 start=(j == 0), stop=(j == HT - 1)),
                             reads=[r("hact"), r("wd_t%d" % sl)],
                             writes=[r(bno)])
            for ts2 in range(2):
                trow = t0 + tg * 256 + ts2 * 128
                st_idx = trow // 128
                for oc in range(4):
                    bno = "pb%d" % (ts2 * 4 + oc)
                    p.op("act", lambda h, oc=oc, bno=bno: h.activation(
                        out=sb["gbuf"][:, oc * 512:(oc + 1) * 512],
                        in_=sb[bno][:], func=AF.Silu),
                        reads=[r(bno)], writes=[r("gbuf")])
                p.op("pool", lambda h, trow=trow: h.dma_start(
                    out=sb["xres"][:], in_=dr["x"][trow:trow + 128, :]),
                    writes=[r("xres")], dma="xres")
                p.op("pool", lambda h, st=st_idx: h.dma_start(
                    out=sb["sres"][:], in_=dr["s_scr"][st, :, :]),
                    writes=[r("sres")], dma="sres")
                p.op("dve", lambda h: h.tensor_tensor(
                    out=sb["obuf"][:], in0=sb["sres"][:], in1=sb["gbuf"][:],
                    op=ALU.mult),
                    reads=[r("sres"), r("gbuf")], writes=[r("obuf")])
                p.op("dve", lambda h: h.tensor_tensor(
                    out=sb["obuf"][:], in0=sb["obuf"][:], in1=sb["xres"][:],
                    op=ALU.add),
                    reads=[r("xres")], writes=[r("obuf")], strict=True)
                p.op("pool", lambda h, trow=trow: h.dma_start(
                    out=dr["y"][trow:trow + 128, :], in_=sb["obuf"][:]),
                    reads=[r("obuf")], dma="y_st")

    def emit_debug(self):
        sb = self.sb
        self.dbg("q_cT", lambda: sb["q_cT"][:], (128, 4 * T), BF16, ["q_cT"])
        self.dbg("khatT", lambda: sb["khatT"][:], (128, T), BF16, ["khatT"])
        self.dbg("v_sb", lambda: sb["v_sb"][:], (128, NT * 128), BF16, ["v_sb"])
        self.dbg("qiT", lambda: sb["qiT"][:], (128, 2 * T), BF16, ["qiT"])
        self.dbg("wih", lambda: sb["wih"][:], (128, NT * 4), F32, ["wih"])
        self.dbg("ckvT", lambda: sb["ckvT"][:], (128, NBLK), F32, ["ckvT"])
        self.dbg("kcompT", lambda: sb["kcompT"][:64, :], (64, NBLK), F32,
                 ["kcompT"])
        self.dbg("skv_ss", lambda: sb["skv_ss"][:], (128, NBLK), BF16,
                 ["skv_ss"])
        self.dbg("skv_pv", lambda: sb["skv_pv"][:], (128, 2 * 128), BF16,
                 ["skv_pv"])
        if "xnT" in self.dbg_names:
            d = self.dout("dbg_xnT", (DT, 128, T), BF16)
            self.p.op("pool", lambda h, d=d: h.dma_start(
                out=d[:, :, :], in_=self.dram["xnT_scr"][:, :, :]), dma="dbg")
        if "wgT" in self.dbg_names:
            d = self.dout("dbg_wgT", (HT, 128, DT * 128), BF16)
            self.p.op("pool", lambda h, d=d: h.dma_start(
                out=d[:, :, :], in_=self.dram["wgT_scr"][:, :, :]),
                dma="dbg")
        if "wdT" in self.dbg_names:
            d = self.dout("dbg_wdT", (HT, 128, D), BF16)
            self.p.op("pool", lambda h, d=d: h.dma_start(
                out=d[:, :, :], in_=self.dram["wdT_scr"][:, :, :]), dma="dbg")
        if "mask" in self.dbg_names:
            d = self.dout("dbg_mask", (128, NT * NBLK), BF16)
            self.p.op("pool", lambda h, d=d: h.dma_start(
                out=d[:, :], in_=sb["mask_sb"][:]),
                reads=[self.r("mask_sb")], dma="dbg")
        for nm in ("ssb0", "wsb0", "pbf0", "pwb0", "ptb0", "ptw0", "tmps0"):
            if nm in self.dbg_names:
                hnd = self.sb[nm]
                shp = (128, hnd.shape[1])
                dt = hnd.dtype if hasattr(hnd, "dtype") else None
                d = self.dout("dbg_" + nm, shp, dt)
                self.p.op("pool", lambda h, d=d, hnd=hnd: h.dma_start(
                    out=d[:, :], in_=hnd[:]),
                    reads=[self.r(nm)], dma="dbg")
        if "qhatT" in self.dbg_names:
            d = self.dout("dbg_qhatT", (128, 16 * 512), BF16)
            self.p.op("pool", lambda h, d=d: h.dma_start(
                out=d[:, :], in_=self.sb["qhatT"][:]),
                reads=[self.r("qhatT")], dma="dbg")
        if "attn_bf" in self.dbg_names:
            d = self.dout("dbg_attn", (128, 16 * 512), BF16)
            self.p.op("pool", lambda h, d=d: h.dma_start(
                out=d[:, :], in_=self.sb["attn_bf"][:]),
                reads=[self.r("attn_bf")], dma="dbg")
        if "goT" in self.dbg_names:
            d = self.dout("dbg_goT", (128, 16 * 512), BF16)
            self.p.op("pool", lambda h, d=d: h.dma_start(
                out=d[:, :], in_=self.sb["goT"][:]),
                reads=[self.r("goT")], dma="dbg")
        if "s" in self.dbg_names:
            d = self.dout("dbg_s", (NT, 128, D), BF16)
            self.p.op("pool", lambda h, d=d: h.dma_start(
                out=d[:, :, :], in_=self.dram["s_scr"][:, :, :]), dma="dbg")

    def build(self):
        p = self.p
        self.p0_globals()
        with ExitStack() as cA:
            self.p0_allocA(cA)
            with ExitStack() as cAB:
                self.sbuf("qiT", (128, 2 * T), BF16, cAB)
                self.sbuf("wih", (128, NT * 4), F32, cAB)
                with ExitStack() as cB:
                    self.p0_weights(cB)
                    self.p1_alloc(cB)
                    self.p05_alloc(cB)
                    p.barrier()
                    for u in range(NSUP):
                        self.p1(u)
                        self.p05_chunk(19)
                    self.p1_post()
                    self.p05_chunk(10 ** 9)
                    p.barrier()
                with ExitStack() as cC:
                    self.p2_alloc(cC)
                    for st in range(NT):
                        self.p2(st)
                        if st < DT:
                            self.p2_fwconv(st)
                    p.barrier()
            with ExitStack() as cD:
                self.p3_alloc(cD)
                for u in range(NSUP):
                    self.p3(u)
                p.barrier()
            self.emit_debug()
            p.barrier()
        with ExitStack() as cE:
            self.p4_alloc(cE)
            for v in range(T // 1024):
                self.p4(v)
            p.barrier()
        block = self.ctx.enter_context(self.nc.Block())
        p.replay(block)
        self.ctx.close()
        return self.nc


_BUILD_CACHE = {}


def build_nc(dbg_names=()):
    key = tuple(sorted(dbg_names))
    if key not in _BUILD_CACHE:
        b = Builder(dbg_names)
        _BUILD_CACHE[key] = (b.build(), b)
    return _BUILD_CACHE[key]


def kernel(**inputs):
    shared, per_core = host_prep(inputs)
    nc, b = build_nc()
    in_maps = []
    for c in range(8):
        m = dict(shared)
        m.update(per_core[c])
        in_maps.append(m)
    res = run_bass_kernel_spmd(nc, in_maps, core_ids=list(range(8)))
    out = np.stack([res.results[c]["y"] for c in range(8)], axis=0)
    return out.astype(np.float32)


# version 06
# speedup vs baseline: 1.2670x; 1.0449x over previous
import math
from contextlib import ExitStack

import numpy as np
import ml_dtypes

import concourse.bass as bass
import concourse.mybir as mybir
from concourse.bass_utils import run_bass_kernel_spmd

F32 = mybir.dt.float32
BF16 = mybir.dt.bfloat16
I32 = mybir.dt.int32
AF = mybir.ActivationFunctionType
ALU = mybir.AluOpType
AX = mybir.AxisListType

BF = ml_dtypes.bfloat16

T = 4096
D = 2048
H = 16
HD = 128
QC = 512
NBLK = 256
MBLK = 16
CI = 64
NIH = 4
WIN = 256
HID = 5632
NEG = -1.0e30
SENT = -2.0e30
SCALE = 1.0 / math.sqrt(128.0)

NT = T // 128
NSUP = T // 512
HT = HID // 128
DT = D // 128

ENGINES = ("pe", "dve", "act", "pool", "sp")
DISABLE = set()
ROPE_CAP = 6


class Res:
    __slots__ = ("w", "rs")

    def __init__(self):
        self.w = None
        self.rs = []

    def for_read(self):
        return [self.w] if self.w is not None else []

    def for_write(self):
        out = list(self.rs)
        if self.w is not None:
            out.append(self.w)
        return out


class Prog:

    def __init__(self, nc, ctx):
        self.nc = nc
        self.ctx = ctx
        self.ops = {e: [] for e in ENGINES}
        self.sems = {}
        self.cnt = {}
        for name in ENGINES:
            self.sems[name] = ctx.enter_context(nc.semaphore("sem_" + name))
            self.cnt[name] = 0
        self.dma_sems = []
        self.last_tok = {e: None for e in ENGINES}
        self.floor = {e: {} for e in ENGINES}

    def _dma_sem(self, name):
        key = "dma_" + name
        if key not in self.sems:
            self.sems[key] = self.ctx.enter_context(
                self.nc.semaphore("sem_" + key))
            self.cnt[key] = 0
            self.dma_sems.append(key)
        return key

    def op(self, eng, fn, reads=(), writes=(), waits=(), dma=None, strict=False):
        wl = {}
        fl = self.floor[eng]

        def add(tok):
            if tok is None:
                return
            sem, val, teng = tok
            if fl.get(sem, -1) >= val:
                return
            if wl.get(sem, -1) < val:
                wl[sem] = val
        for rres in reads:
            for tok in rres.for_read():
                add(tok)
        for wres in writes:
            for tok in wres.for_write():
                add(tok)
        for tok in waits:
            add(tok)
        for sem, val in wl.items():
            assert val <= self.cnt[sem], (sem, val, self.cnt[sem])
            fl[sem] = val
        if dma is None:
            sem, k = eng, 1
        else:
            sem, k = self._dma_sem(dma), 16
        self.cnt[sem] += k
        tok = (sem, self.cnt[sem], None if dma is not None else eng)
        self.ops[eng].append((fn, sorted(wl.items()), sem, k))
        if dma is None:
            self.last_tok[eng] = tok
        for rres in reads:
            rres.rs.append(tok)
        for wres in writes:
            wres.w = tok
            wres.rs = []
        return tok

    def barrier(self):
        toks = [t for t in self.last_tok.values() if t is not None]
        for key in self.dma_sems:
            if self.cnt[key]:
                toks.append((key, self.cnt[key], None))
        for e in ENGINES:
            self.op(e, lambda h: h.nop(), waits=toks)

    def replay(self, block):
        sems = self.sems

        def mk_body(eng_name):
            def run(h):
                for fn, waits, sem, k in self.ops[eng_name]:
                    for wsem, val in waits:
                        h.wait_ge(sems[wsem], val)
                    instr = fn(h)
                    if instr is not None:
                        instr.then_inc(sems[sem], k)
                    else:
                        h.nop().then_inc(sems[sem], k)
            return run
        block.tensor(mk_body("pe"))
        block.vector(mk_body("dve"))
        block.scalar(mk_body("act"))
        block.gpsimd(mk_body("pool"))
        block.sync(mk_body("sp"))


def host_constants():
    c = {}
    c["ident"] = np.eye(128, dtype=BF)
    inv = 1.0 / 10000.0 ** (np.arange(0, 64, 2, dtype=np.float64) / 64.0)
    ang = np.arange(T, dtype=np.float64)[:, None] * inv
    cos = np.cos(ang).astype(np.float32)
    sin = np.sin(ang).astype(np.float32)
    c["cosp"] = np.ascontiguousarray(
        cos.reshape(NT, 128, 32).transpose(1, 0, 2).reshape(128, NT * 32))
    c["sinp"] = np.ascontiguousarray(
        sin.reshape(NT, 128, 32).transpose(1, 0, 2).reshape(128, NT * 32))
    tvec = np.arange(T)
    nvec = np.arange(NBLK)
    causal = nvec[None, :] < (tvec[:, None] // MBLK)
    ca = np.where(causal, 0.0, NEG).astype(np.float32)
    c["causal"] = np.ascontiguousarray(
        ca.reshape(NT, 128, NBLK).transpose(1, 0, 2).reshape(128, NT * NBLK)
    ).astype(BF)
    diff = np.arange(2 * WIN)[None, :] - np.arange(WIN)[:, None]
    win_ok = (diff > 0) & (diff <= WIN)
    first_ok = np.arange(2 * WIN)[None, :] >= WIN
    mask_n = np.where(win_ok, 0.0, NEG).astype(np.float32)
    mask_0 = np.where(win_ok & first_ok, 0.0, NEG).astype(np.float32)
    wm = np.concatenate([mask_n[:128], mask_n[128:], mask_0[:128], mask_0[128:]],
                        axis=1)
    c["winmask"] = np.ascontiguousarray(wm).astype(BF)
    return c


def host_prep(inputs):
    g = lambda k: np.ascontiguousarray(np.asarray(inputs[k]))
    shared = dict(host_constants())
    shared["w_qc"] = g("L0_attn_w_qc")
    shared["w_qup"] = g("L0_attn_w_qup")
    shared["W1"] = np.ascontiguousarray(np.concatenate(
        [g("L0_attn_k_proj_w"), g("L0_attn_v_proj_w"), g("L0_attn_w_w")], axis=1))
    shared["W2"] = np.ascontiguousarray(np.concatenate(
        [g("L0_attn_kvc_w"), g("L0_attn_kvc_wz"), g("L0_attn_idx_c_w"),
         g("L0_attn_idx_c_wz"), g("L0_attn_w_dq")], axis=1))
    shared["w_iuq"] = g("L0_attn_w_iuq")
    shared["biasTkv"] = np.ascontiguousarray(g("L0_attn_kvc_bias").T)
    shared["biasTidx"] = np.ascontiguousarray(g("L0_attn_idx_c_bias").T)
    qn = g("L0_attn_q_norm_w").reshape(-1)
    kn = g("L0_attn_k_norm_w").reshape(-1)
    shared["qn_col"] = np.ascontiguousarray(qn[:, None])
    shared["kn_col"] = np.ascontiguousarray(kn[:, None])
    shared["qn_rep"] = np.ascontiguousarray(np.broadcast_to(qn[None, :], (128, 128)))
    shared["kn_rep"] = np.ascontiguousarray(np.broadcast_to(kn[None, :], (128, 128)))
    shared["group_w"] = g("L0_attn_group_w")
    shared["final_w"] = g("L0_attn_final_w")
    shared["wq_gate"] = g("L0_ffn_wq_gate")
    shared["wq_up"] = g("L0_ffn_wq_up")
    shared["wq_down"] = g("L0_ffn_wq_down")
    shared["sgate"] = np.ascontiguousarray(g("L0_ffn_s_gate").reshape(HT, 128).T)
    shared["sup"] = np.ascontiguousarray(g("L0_ffn_s_up").reshape(HT, 128).T)
    shared["sdown_rep"] = np.ascontiguousarray(
        np.broadcast_to(g("L0_ffn_s_down").reshape(-1)[None, :], (128, D)))
    x = np.asarray(inputs["L0_attn_h"])
    per_core = [{"x": np.ascontiguousarray(x[b])} for b in range(8)]
    return shared, per_core


class Builder:
    def __init__(self, dbg_names=()):
        self.nc = bass.Bass("TRN2", target_bir_lowering=False, debug=False)
        self.ctx = ExitStack()
        self.p = Prog(self.nc, self.ctx)
        self.dbg_names = set(dbg_names)
        self.dram = {}
        self.sb = {}
        self.res = {}

    def din(self, name, shape, dtype):
        self.dram[name] = self.nc.dram_tensor(name, list(shape), dtype,
                                              kind="ExternalInput").ap()
        return self.dram[name]

    def dout(self, name, shape, dtype):
        self.dram[name] = self.nc.dram_tensor(name, list(shape), dtype,
                                              kind="ExternalOutput").ap()
        return self.dram[name]

    def dscr(self, name, shape, dtype):
        self.dram[name] = self.nc.dram_tensor(name, list(shape), dtype,
                                              kind="Internal").ap()
        return self.dram[name]

    def sbuf(self, name, shape, dtype, ctx=None):
        h = (ctx or self.ctx).enter_context(
            self.nc.sbuf_tensor("sb_" + name, list(shape), dtype))
        self.sb[name] = h
        self.res[name] = Res()
        return h

    def psum(self, name, ctx=None):
        h = (ctx or self.ctx).enter_context(
            self.nc.psum_tensor("ps_" + name, [128, 512], F32))
        self.sb[name] = h
        self.res[name] = Res()
        return h

    def r(self, name):
        return self.res[name]

    def dbg(self, name, ap_fn, shape, dtype, reads):
        if name not in self.dbg_names:
            return
        d = self.dout("dbg_" + name, shape, dtype)
        self.p.op("pool", lambda h, d=d, f=ap_fn: h.dma_start(out=d, in_=f()),
                  reads=[self.r(n) for n in reads], dma="dbg")

    def p0_globals(self):
        p, dr = self.p, self.dram
        for i in range(8):
            self.psum("pb%d" % i)
        self.din("x", (T, D), F32)
        for nm, sh, dt in [
            ("ident", (128, 128), BF16), ("cosp", (128, NT * 32), F32),
            ("sinp", (128, NT * 32), F32), ("causal", (128, NT * NBLK), BF16),
            ("winmask", (128, 4 * 512), BF16),
            ("w_qc", (D, QC), F32), ("w_qup", (QC, D), F32),
            ("W1", (D, 260), F32), ("W2", (D, 448), F32),
            ("w_iuq", (64, 256), F32),
            ("biasTkv", (128, 16), F32), ("biasTidx", (64, 16), F32),
            ("qn_col", (128, 1), F32), ("kn_col", (128, 1), F32),
            ("qn_rep", (128, 128), F32), ("kn_rep", (128, 128), F32),
            ("group_w", (4, 512, 512), F32), ("final_w", (D, D), F32),
            ("wq_gate", (HID, D), I32), ("wq_up", (HID, D), I32),
            ("wq_down", (D, HID), I32),
            ("sgate", (128, HT), F32), ("sup", (128, HT), F32),
            ("sdown_rep", (128, D), F32),
        ]:
            self.din(nm, sh, dt)
        self.dout("y", (T, D), F32)
        self.dscr("xnT_scr", (DT, 128, T), BF16)
        self.dscr("s_scr", (NT, 128, D), BF16)
        self.dscr("wgT_scr", (HT, 128, DT * 128), BF16)
        self.dscr("wuT_scr", (HT, 128, DT * 128), BF16)
        self.dscr("wdT_scr", (HT, 128, D), BF16)
        self.dscr("fwT_scr", (DT, 128, D), BF16)

        sb = self.sbuf
        sb("ident", (128, 128), BF16)
        sb("constp", (128, 8), F32)
        sb("sdown", (128, D), F32)
        sb("qn_col", (128, 1), F32)
        sb("kn_col", (128, 1), F32)
        sb("gains_col", (128, 1), F32)
        sb("ones_col", (128, 1), F32)
        sb("ones_row", (1, 128), F32)

        for nm in ("ident", "qn_col", "kn_col"):
            p.op("sp", lambda h, d=self.sb[nm], s=dr[nm]: h.dma_start(
                out=d[:], in_=s[:, :]), writes=[self.r(nm)], dma=nm)
        p.op("sp", lambda h: h.dma_start(out=self.sb["sdown"][:],
                                         in_=dr["sdown_rep"][:, :]),
             writes=[self.r("sdown")], dma="sdown")
        cp = self.sb["constp"]
        for slot, val in [(0, 1e-6), (1, 1e-5), (2, -0.5), (3, 1.0 / D),
                          (4, 1.0 / 128.0), (5, 0.0), (6, 1e30), (7, -1e30)]:
            p.op("dve", lambda h, s=slot, v=val: h.memset(cp[:, s:s + 1], v),
                 writes=[self.r("constp")])
        p.op("dve", lambda h: h.memset(self.sb["ones_col"][:], 1.0),
             writes=[self.r("ones_col")])
        p.op("dve", lambda h: h.memset(self.sb["ones_row"][:], 1.0),
             writes=[self.r("ones_row")])
        p.op("dve", lambda h: h.tensor_tensor(
            out=self.sb["gains_col"][:], in0=self.sb["qn_col"][:],
            in1=self.sb["kn_col"][:], op=ALU.mult),
            reads=[self.r("qn_col"), self.r("kn_col")],
            writes=[self.r("gains_col")])
        p.op("dve", lambda h: h.tensor_scalar(
            out=self.sb["gains_col"][:], in0=self.sb["gains_col"][:],
            scalar1=SCALE, scalar2=None, op0=ALU.mult),
            writes=[self.r("gains_col")], strict=True)

    def p0_allocA(self, ctx):
        p, dr = self.p, self.dram
        sb = self.sbuf
        sb("cos", (128, NT * 32), F32, ctx)
        sb("sin", (128, NT * 32), F32, ctx)
        sb("q_cT", (128, 4 * T), BF16, ctx)
        sb("khatT", (128, T), BF16, ctx)
        sb("v_sb", (128, NT * 128), BF16, ctx)
        sb("ckvT", (128, NBLK), F32, ctx)
        sb("kcompT", (64, NBLK), F32, ctx)
        sb("kcomp_bf", (128, NBLK), BF16, ctx)
        sb("skv_ss", (128, NBLK), BF16, ctx)
        sb("skv_pv", (128, 2 * 128), BF16, ctx)
        sb("mask_sb", (128, NT * NBLK), BF16, ctx)
        for nm, srcn in (("cos", "cosp"), ("sin", "sinp")):
            p.op("sp", lambda h, d=self.sb[nm], s=dr[srcn]: h.dma_start(
                out=d[:], in_=s[:, :]), writes=[self.r(nm)], dma=nm)

    def p0_weights(self, ctx):
        p, dr = self.p, self.dram
        sb = self.sbuf
        sb("wqs_rep", (128, 128), F32, ctx)
        sb("wk_rep", (128, 128), F32, ctx)
        sb("biasTkv", (128, 16), F32, ctx)
        sb("biasTidx", (64, 16), F32, ctx)
        sb("w_iuq", (64, 256), BF16, ctx)
        sb("wstage", (128, 512), F32, ctx)
        sb("w_qc_bf", (128, DT * QC), BF16, ctx)
        sb("W1_bf", (128, DT * 260), BF16, ctx)
        sb("W2_bf", (128, DT * 448), BF16, ctx)

        for nm in ("biasTkv", "biasTidx"):
            p.op("sp", lambda h, d=self.sb[nm], s=dr[nm]: h.dma_start(
                out=d[:], in_=s[:, :]), writes=[self.r(nm)], dma=nm)

        def conv(dstname, col, src_ap, ncols):
            st = self.sb["wstage"]
            p.op("sp", lambda h, s=src_ap, n=ncols: h.dma_start(
                out=st[:, :n], in_=s), writes=[self.r("wstage")], dma="wstage")
            p.op("dve", lambda h, d=self.sb[dstname], c=col, n=ncols:
                 h.tensor_copy(out=d[:, c:c + n], in_=st[:, :n]),
                 reads=[self.r("wstage")], writes=[self.r(dstname)])

        for i in range(DT):
            conv("w_qc_bf", i * QC, dr["w_qc"][i * 128:(i + 1) * 128, :], QC)
            conv("W1_bf", i * 260, dr["W1"][i * 128:(i + 1) * 128, :], 260)
            conv("W2_bf", i * 448, dr["W2"][i * 128:(i + 1) * 128, :], 448)
        st = self.sb["wstage"]
        p.op("sp", lambda h: h.dma_start(out=st[:64, :256], in_=dr["w_iuq"][:, :]),
             writes=[self.r("wstage")], dma="wstage")
        p.op("dve", lambda h: h.tensor_copy(out=self.sb["w_iuq"][:, :],
                                            in_=st[:64, :256]),
             reads=[self.r("wstage")], writes=[self.r("w_iuq")])
        p.op("sp", lambda h: h.dma_start(out=self.sb["wqs_rep"][:],
                                         in_=dr["qn_rep"][:, :]),
             writes=[self.r("wqs_rep")], dma="wqs_rep")
        p.op("dve", lambda h: h.tensor_scalar(
            out=self.sb["wqs_rep"][:], in0=self.sb["wqs_rep"][:],
            scalar1=SCALE, scalar2=None, op0=ALU.mult),
            reads=[self.r("wqs_rep")], writes=[self.r("wqs_rep")])
        p.op("sp", lambda h: h.dma_start(out=self.sb["wk_rep"][:],
                                         in_=dr["kn_rep"][:, :]),
             writes=[self.r("wk_rep")], dma="wk_rep")

    def p1_alloc(self, ctx):
        sb = self.sbuf
        sb("xstage", (128, D), F32, ctx)
        sb("lnstat", (128, 16), F32, ctx)
        sb("xn_tm", (128, 2 * D), BF16, ctx)
        sb("xnT", (128, DT * 512), BF16, ctx)
        sb("kstage", (128, 4 * 128), F32, ctx)
        sb("khat_tm", (128, 4 * 128), BF16, ctx)
        sb("dq_bf", (64, 512), BF16, ctx)
        sb("czkv", (128, 1024), F32, ctx)
        sb("czidx", (64, 1024), F32, ctx)
        sb("cmp_scr", (128, 512), F32, ctx)
        sb("cmp_red", (128, 64), F32, ctx)
        sb("rope_scr", (128, 128), F32, ctx)
        sb("skv_stg", (128, 256), BF16, ctx)

    def p1(self, u):
        p, sb, dr, r = self.p, self.sb, self.dram, self.r
        t0 = u * 512
        cp = sb["constp"]
        xst = sb["xstage"]

        for s in range(4):
            p.op("pool", lambda h, s=s: h.dma_start(
                out=xst[:],
                in_=dr["x"][t0 + s * 128:t0 + (s + 1) * 128, :]),
                writes=[r("xstage")], dma="xstage")
            xs = xst[:, 0:D]
            c_mean = sb["lnstat"][:, 4 * s:4 * s + 1]
            c_rv = sb["lnstat"][:, 4 * s + 1:4 * s + 2]
            c_t = sb["lnstat"][:, 4 * s + 2:4 * s + 3]
            p.op("dve", lambda h, xs=xs, c=c_mean: h.reduce_sum(
                out=c, in_=xs, axis=AX.X),
                reads=[r("xstage")], writes=[r("lnstat")])
            p.op("act", lambda h, xs=xs, s=s, c=c_rv: h.activation(
                out=sb["xn_tm"][:, (s % 2) * D:(s % 2 + 1) * D], in_=xs,
                func=AF.Square, accum_out=c),
                reads=[r("xstage")], writes=[r("xn_tm"), r("lnstat")])
            p.op("dve", lambda h, c=c_mean: h.tensor_scalar(
                out=c, in0=c, scalar1=1.0 / D, scalar2=None, op0=ALU.mult),
                reads=[r("lnstat")], writes=[r("lnstat")], strict=True)
            p.op("dve", lambda h, c=c_t, m=c_mean: h.tensor_tensor(
                out=c, in0=m, in1=m, op=ALU.mult),
                reads=[r("lnstat")], writes=[r("lnstat")], strict=True)
            p.op("dve", lambda h, c=c_rv, t_=c_t: h.scalar_tensor_tensor(
                out=c, in0=c, scalar=1.0 / D, in1=t_, op0=ALU.mult,
                op1=ALU.subtract),
                reads=[r("lnstat")], writes=[r("lnstat")], strict=True)
            p.op("act", lambda h, c=c_rv: h.activation(
                out=c, in_=c, func=AF.Ln, bias=cp[:, 1:2], scale=1.0),
                reads=[r("lnstat"), r("constp")], writes=[r("lnstat")])
            p.op("act", lambda h, c=c_rv: h.activation(
                out=c, in_=c, func=AF.Exp, bias=0.0, scale=cp[:, 2:3]),
                reads=[r("lnstat"), r("constp")], writes=[r("lnstat")],
                strict=True)
            p.op("dve", lambda h, c=c_t, m=c_mean, v=c_rv: h.scalar_tensor_tensor(
                out=c, in0=m, scalar=-1.0, in1=v, op0=ALU.mult, op1=ALU.mult),
                reads=[r("lnstat")], writes=[r("lnstat")])
            p.op("act", lambda h, xs=xs, s=s, v=c_rv, b=c_t: h.activation(
                out=sb["xn_tm"][:, (s % 2) * D:(s % 2 + 1) * D], in_=xs,
                func=AF.Identity, bias=b, scale=v),
                reads=[r("lnstat"), r("xstage")], writes=[r("xn_tm")])
            if s % 2 == 1:
                pair = s // 2
                for dt in range(DT):
                    bn = "pb%d" % (dt % 2)
                    bank = sb[bn]

                    def tmm(h, dt=dt, bank=bank, pair=pair):
                        last = None
                        for q2 in range(2):
                            last = h.matmul(
                                bank[:, (pair * 2 + q2) * 128:
                                     (pair * 2 + q2 + 1) * 128],
                                sb["xn_tm"][:, q2 * D + dt * 128:
                                            q2 * D + (dt + 1) * 128],
                                sb["ident"][:], start=True, stop=True)
                        return last
                    p.op("pe", tmm, reads=[r("xn_tm"), r("ident")],
                         writes=[r(bn)])
                    p.op("dve", lambda h, dt=dt, bank=bank, pair=pair:
                         h.tensor_copy(
                             out=sb["xnT"][:, dt * 512 + pair * 256:
                                           dt * 512 + (pair + 1) * 256],
                             in_=bank[:, pair * 256:(pair + 1) * 256]),
                         reads=[r(bn)], writes=[r("xnT")])
        p.op("pool", lambda h: h.dma_start(
            out=dr["xnT_scr"][:, :, t0:t0 + 512].rearrange("dt p t -> p dt t"),
            in_=sb["xnT"][:].rearrange("p (dt t) -> p dt t", dt=DT)),
            reads=[r("xnT")], dma="xnT_sp")

        for mt in range(4):
            bank = sb["pb2"]

            def qcmm(h, mt=mt, bank=bank):
                last = None
                for kt in range(DT):
                    last = h.matmul(
                        bank[:],
                        sb["w_qc_bf"][:, kt * QC + mt * 128:
                                      kt * QC + (mt + 1) * 128],
                        sb["xnT"][:, kt * 512:(kt + 1) * 512],
                        start=(kt == 0), stop=(kt == DT - 1))
                return last
            p.op("pe", qcmm, reads=[r("w_qc_bf"), r("xnT")], writes=[r("pb2")])
            p.op("dve", lambda h, mt=mt, bank=bank: h.tensor_copy(
                out=sb["q_cT"][:, mt * T + t0: mt * T + t0 + 512], in_=bank[:]),
                reads=[r("pb2")], writes=[r("q_cT")])

        for s in range(4):
            bank = sb["pb3"]
            st_idx = u * 4 + s

            def w1mm(h, s=s, bank=bank):
                last = None
                for kt in range(DT):
                    last = h.matmul(
                        bank[:, :260],
                        sb["xnT"][:, kt * 512 + s * 128:kt * 512 + (s + 1) * 128],
                        sb["W1_bf"][:, kt * 260:(kt + 1) * 260],
                        start=(kt == 0), stop=(kt == DT - 1))
                return last
            p.op("pe", w1mm, reads=[r("W1_bf"), r("xnT")], writes=[r("pb3")])
            p.op("act", lambda h, s=s, bank=bank: h.copy(
                out=sb["kstage"][:, s * 128:(s + 1) * 128], in_=bank[:, 0:128]),
                reads=[r("pb3")], writes=[r("kstage")])
            p.op("act", lambda h, bank=bank, st=st_idx: h.copy(
                out=sb["v_sb"][:, st * 128:(st + 1) * 128], in_=bank[:, 128:256]),
                reads=[r("pb3")], writes=[r("v_sb")])
            p.op("act", lambda h, bank=bank, st=st_idx: h.copy(
                out=sb["wih"][:, st * 4:(st + 1) * 4], in_=bank[:, 256:260]),
                reads=[r("pb3")], writes=[r("wih")])

        for s in range(4):
            st_idx = u * 4 + s
            ks = sb["kstage"][:, s * 128:(s + 1) * 128]
            c_rs = sb["lnstat"][:, s:s + 1]
            p.op("act", lambda h, ks=ks, c=c_rs: h.activation(
                out=sb["xn_tm"][:, 0:128], in_=ks, func=AF.Square, accum_out=c),
                reads=[r("kstage")], writes=[r("xn_tm"), r("lnstat")])
            p.op("act", lambda h, c=c_rs: h.activation(
                out=c, in_=c, func=AF.Ln, bias=cp[:, 0:1], scale=cp[:, 4:5]),
                reads=[r("lnstat"), r("constp")], writes=[r("lnstat")],
                strict=True)
            p.op("act", lambda h, c=c_rs: h.activation(
                out=c, in_=c, func=AF.Exp, bias=0.0, scale=cp[:, 2:3]),
                reads=[r("lnstat"), r("constp")], writes=[r("lnstat")],
                strict=True)
            p.op("act", lambda h, ks=ks, c=c_rs: h.activation(
                out=ks, in_=ks, func=AF.Identity, bias=0.0, scale=c),
                reads=[r("lnstat")], writes=[r("kstage")], strict=True)
            p.op("dve", lambda h, ks=ks: h.tensor_tensor(
                out=ks, in0=ks, in1=sb["wk_rep"][:], op=ALU.mult),
                reads=[r("wk_rep"), r("kstage")], writes=[r("kstage")],
                strict=True)
            co = sb["cos"][:, st_idx * 32:(st_idx + 1) * 32]
            si = sb["sin"][:, st_idx * 32:(st_idx + 1) * 32]
            x1 = ks[:, 0:32]
            x2 = ks[:, 32:64]
            rs_ = sb["rope_scr"]
            p.op("dve", lambda h, x1=x1, co=co: h.tensor_tensor(
                out=rs_[:, 0:32], in0=x1, in1=co, op=ALU.mult),
                reads=[r("kstage"), r("cos")], writes=[r("rope_scr")],
                strict=True)
            p.op("dve", lambda h, x2=x2, co=co: h.tensor_tensor(
                out=rs_[:, 32:64], in0=x2, in1=co, op=ALU.mult),
                reads=[r("kstage"), r("cos")], writes=[r("rope_scr")],
                strict=True)
            p.op("dve", lambda h, x2=x2, si=si: h.scalar_tensor_tensor(
                out=rs_[:, 64:96], in0=x2, scalar=-1.0, in1=si,
                op0=ALU.mult, op1=ALU.mult),
                reads=[r("kstage"), r("sin")], writes=[r("rope_scr")],
                strict=True)
            p.op("dve", lambda h, x1=x1, si=si: h.tensor_tensor(
                out=rs_[:, 96:128], in0=x1, in1=si, op=ALU.mult),
                reads=[r("kstage"), r("sin")], writes=[r("rope_scr")],
                strict=True)
            p.op("dve", lambda h, x1=x1: h.tensor_tensor(
                out=x1, in0=rs_[:, 0:32], in1=rs_[:, 64:96], op=ALU.add),
                reads=[r("rope_scr")], writes=[r("kstage")], strict=True)
            p.op("dve", lambda h, x2=x2: h.tensor_tensor(
                out=x2, in0=rs_[:, 96:128], in1=rs_[:, 32:64], op=ALU.add),
                reads=[r("rope_scr")], writes=[r("kstage")], strict=True)
            p.op("dve", lambda h, ks=ks, s=s: h.tensor_tensor(
                out=sb["khat_tm"][:, s * 128:(s + 1) * 128], in0=ks,
                in1=sb["wqs_rep"][:], op=ALU.mult),
                reads=[r("kstage"), r("wqs_rep")], writes=[r("khat_tm")],
                strict=True)
        bank = sb["pb4"]

        def ktr(h, bank=bank):
            last = None
            for s in range(4):
                last = h.matmul(bank[:, s * 128:(s + 1) * 128],
                                sb["khat_tm"][:, s * 128:(s + 1) * 128],
                                sb["ident"][:], start=True, stop=True)
            return last
        p.op("pe", ktr, reads=[r("khat_tm"), r("ident")], writes=[r("pb4")])
        p.op("dve", lambda h, bank=bank: h.tensor_copy(
            out=sb["khatT"][:, t0:t0 + 512], in_=bank[:]),
            reads=[r("pb4")], writes=[r("khatT")])

        for mt, (c0, cn) in enumerate([(0, 128), (128, 128), (256, 128),
                                       (384, 64)]):
            bn = "pb%d" % (5 + mt % 2)
            bank = sb[bn]

            def w2mm(h, c0=c0, cn=cn, bank=bank):
                last = None
                for kt in range(DT):
                    last = h.matmul(
                        bank[:cn, :512],
                        sb["W2_bf"][:, kt * 448 + c0:kt * 448 + c0 + cn],
                        sb["xnT"][:, kt * 512:(kt + 1) * 512],
                        start=(kt == 0), stop=(kt == DT - 1))
                return last
            p.op("pe", w2mm, reads=[r("W2_bf"), r("xnT")], writes=[r(bn)])
            if mt == 0:
                p.op("act", lambda h, bank=bank: h.copy(
                    out=sb["czkv"][:, 0:512], in_=bank[:, :512]),
                    reads=[r(bn)], writes=[r("czkv")])
            elif mt == 1:
                p.op("dve", lambda h, bank=bank: h.tensor_tensor(
                    out=sb["czkv"][:, 512:1024].rearrange("p (b m) -> p b m",
                                                          m=16),
                    in0=bank[:, :512].rearrange("p (b m) -> p b m", m=16),
                    in1=sb["biasTkv"][:].unsqueeze(1).to_broadcast([128, 32, 16]),
                    op=ALU.add),
                    reads=[r(bn), r("biasTkv")], writes=[r("czkv")])
            elif mt == 2:
                p.op("dve", lambda h, bank=bank: h.tensor_copy(
                    out=sb["czidx"][:, 0:512], in_=bank[0:64, :512]),
                    reads=[r(bn)], writes=[r("czidx")])
                p.op("dve", lambda h, bank=bank: h.tensor_tensor(
                    out=sb["czidx"][:, 512:1024].rearrange("p (b m) -> p b m",
                                                           m=16),
                    in0=bank[64:128, :512].rearrange("p (b m) -> p b m", m=16),
                    in1=sb["biasTidx"][:].unsqueeze(1).to_broadcast([64, 32, 16]),
                    op=ALU.add),
                    reads=[r(bn), r("biasTidx")], writes=[r("czidx")])
            else:
                p.op("dve", lambda h, bank=bank: h.tensor_copy(
                    out=sb["dq_bf"][:], in_=bank[0:64, :512]),
                    reads=[r(bn)], writes=[r("dq_bf")])
                for mt2 in range(2):
                    bank2 = sb["pb7"]
                    p.op("pe", lambda h, mt2=mt2, bank2=bank2: h.matmul(
                        bank2[:, :512],
                        sb["w_iuq"][:, mt2 * 128:(mt2 + 1) * 128],
                        sb["dq_bf"][:], start=True, stop=True),
                        reads=[r("w_iuq"), r("dq_bf")], writes=[r("pb7")])
                    p.op("dve", lambda h, mt2=mt2, bank2=bank2: h.tensor_copy(
                        out=sb["qiT"][:, mt2 * T + t0:mt2 * T + t0 + 512],
                        in_=bank2[:, :512]),
                        reads=[r("pb7")], writes=[r("qiT")])

        def compress(cz_name, parts, dstname):
            cz = sb[cz_name]
            czr = r(cz_name)
            scr = sb["cmp_scr"]
            red = sb["cmp_red"]
            c_ap = cz[:parts, 0:512]
            z_ap = cz[:parts, 512:1024]
            p.op("dve", lambda h: h.reduce_max(
                out=red[:parts, 0:32],
                in_=z_ap.rearrange("p (b m) -> p b m", m=16),
                axis=AX.X, negate=True),
                reads=[czr], writes=[r("cmp_red")])
            p.op("dve", lambda h: h.tensor_tensor(
                out=scr[:parts, 0:512].rearrange("p (b m) -> p b m", m=16),
                in0=z_ap.rearrange("p (b m) -> p b m", m=16),
                in1=red[:parts, 0:32].unsqueeze(2).to_broadcast([parts, 32, 16]),
                op=ALU.add),
                reads=[czr, r("cmp_red")], writes=[r("cmp_scr")], strict=True)
            p.op("act", lambda h: h.activation(
                out=scr[:parts, 0:512], in_=scr[:parts, 0:512], func=AF.Exp),
                reads=[r("cmp_scr")], writes=[r("cmp_scr")])
            p.op("dve", lambda h: h.reduce_sum(
                out=red[:parts, 32:64],
                in_=scr[:parts, 0:512].rearrange("p (b m) -> p b m", m=16),
                axis=AX.X),
                reads=[r("cmp_scr")], writes=[r("cmp_red")])
            p.op("dve", lambda h: h.tensor_tensor(
                out=scr[:parts, 0:512], in0=scr[:parts, 0:512], in1=c_ap,
                op=ALU.mult),
                reads=[r("cmp_scr"), czr], writes=[r("cmp_scr")], strict=True)
            p.op("dve", lambda h: h.reduce_sum(
                out=red[:parts, 0:32],
                in_=scr[:parts, 0:512].rearrange("p (b m) -> p b m", m=16),
                axis=AX.X),
                reads=[r("cmp_scr")], writes=[r("cmp_red")], strict=True)
            p.op("dve", lambda h: h.reciprocal(
                out=red[:parts, 32:64], in_=red[:parts, 32:64]),
                reads=[r("cmp_red")], writes=[r("cmp_red")], strict=True)
            p.op("dve", lambda h: h.tensor_tensor(
                out=sb[dstname][:parts, u * 32:(u + 1) * 32],
                in0=red[:parts, 0:32], in1=red[:parts, 32:64], op=ALU.mult),
                reads=[r("cmp_red")], writes=[r(dstname)], strict=True)
        compress("czkv", 128, "ckvT")
        compress("czidx", 64, "kcompT")

    def p1_post(self):
        p, sb, r = self.p, self.sb, self.r
        cp = sb["constp"]
        scr = sb["cmp_scr"]
        p.op("dve", lambda h: h.tensor_tensor(
            out=scr[:, 0:256], in0=sb["ckvT"][:], in1=sb["ckvT"][:],
            op=ALU.mult),
            reads=[r("ckvT")], writes=[r("cmp_scr")])
        bank = sb["pb0"]
        p.op("pe", lambda h: h.matmul(bank[:1, 0:256], sb["ones_col"][:],
                                      scr[:, 0:256], start=True, stop=True),
             reads=[r("ones_col"), r("cmp_scr")], writes=[r("pb0")])
        p.op("act", lambda h: h.activation(
            out=scr[:1, 256:512], in_=bank[:1, 0:256], func=AF.Ln,
            bias=cp[:1, 0:1], scale=cp[:1, 4:5]),
            reads=[r("pb0"), r("constp")], writes=[r("cmp_scr")])
        p.op("act", lambda h: h.activation(
            out=scr[:1, 256:512], in_=scr[:1, 256:512], func=AF.Exp,
            bias=0.0, scale=cp[:1, 2:3]),
            reads=[r("cmp_scr"), r("constp")], writes=[r("cmp_scr")],
            strict=True)
        bank1 = sb["pb1"]
        p.op("pe", lambda h: h.matmul(bank1[:, 0:256], sb["ones_row"][:],
                                      scr[:1, 256:512], start=True, stop=True),
             reads=[r("ones_row"), r("cmp_scr")], writes=[r("pb1")])
        p.op("dve", lambda h: h.tensor_tensor(
            out=scr[:, 0:256], in0=sb["ckvT"][:], in1=bank1[:, 0:256],
            op=ALU.mult),
            reads=[r("ckvT"), r("pb1")], writes=[r("cmp_scr")])
        p.op("dve", lambda h: h.tensor_scalar(
            out=sb["skv_ss"][:], in0=scr[:, 0:256],
            scalar1=sb["gains_col"][:, 0:1], scalar2=None, op0=ALU.mult),
            reads=[r("cmp_scr"), r("gains_col")], writes=[r("skv_ss")])
        p.op("dve", lambda h: h.tensor_copy(
            out=sb["skv_stg"][:], in_=scr[:, 0:256]),
            reads=[r("cmp_scr")], writes=[r("skv_stg")])
        bank2 = sb["pb2"]

        def tr2(h, bank2=bank2):
            last = None
            for i in range(2):
                last = h.matmul(
                    bank2[:, i * 128:(i + 1) * 128],
                    sb["skv_stg"][:, i * 128:(i + 1) * 128],
                    sb["ident"][:], start=True, stop=True)
            return last
        p.op("pe", tr2, reads=[r("skv_stg"), r("ident")], writes=[r("pb2")])
        p.op("dve", lambda h, bank2=bank2: h.tensor_copy(
            out=sb["skv_pv"][:], in_=bank2[:, 0:256]),
            reads=[r("pb2")], writes=[r("skv_pv")])
        p.op("dve", lambda h: h.tensor_tensor(
            out=sb["skv_pv"][:].rearrange("p (i c) -> p i c", i=2),
            in0=sb["skv_pv"][:].rearrange("p (i c) -> p i c", i=2),
            in1=sb["wk_rep"][:].unsqueeze(1).to_broadcast([128, 2, 128]),
            op=ALU.mult),
            reads=[r("wk_rep")], writes=[r("skv_pv")], strict=True)
        p.op("dve", lambda h: h.tensor_copy(out=sb["kcomp_bf"][0:64, :],
                                            in_=sb["kcompT"][:]),
             reads=[r("kcompT")], writes=[r("kcomp_bf")])
        p.op("pool", lambda h: h.dma_start(out=sb["kcomp_bf"][64:128, :],
                                           in_=sb["kcomp_bf"][0:64, :]),
             reads=[r("kcomp_bf")], writes=[r("kcomp_bf")], dma="kcompdup")

    def p2_alloc(self, ctx):
        p, dr = self.p, self.dram
        sb = self.sbuf
        sb("causal", (128, NT * NBLK), BF16, ctx)
        sb("fw_f32", (128, D), F32, ctx)
        sb("fw_bf", (128, D), BF16, ctx)
        sb("iscc", (128, NBLK), F32, ctx)
        sb("isc2", (128, NBLK), F32, ctx)
        sb("m8", (128, 8), F32, ctx)
        p.op("sp", lambda h: h.dma_start(out=self.sb["causal"][:],
                                         in_=dr["causal"][:, :]),
             writes=[self.r("causal")], dma="causal")

    def p2_fwconv(self, kt):
        p, sb, dr, r = self.p, self.sb, self.dram, self.r
        p.op("sp", lambda h, kt=kt: h.dma_start(
            out=sb["fw_f32"][:], in_=dr["final_w"][kt * 128:(kt + 1) * 128, :]),
            writes=[r("fw_f32")], dma="fw_f32")
        p.op("dve", lambda h: h.tensor_copy(out=sb["fw_bf"][:],
                                            in_=sb["fw_f32"][:]),
             reads=[r("fw_f32")], writes=[r("fw_bf")])
        p.op("pool", lambda h, kt=kt: h.dma_start(
            out=self.dram["fwT_scr"][kt, :, :], in_=sb["fw_bf"][:]),
            reads=[r("fw_bf")], dma="fwbf_st")

    def p2(self, st):
        p, sb, r = self.p, self.sb, self.r
        t0 = st * 128
        cp = sb["constp"]
        for ih in range(NIH):
            bn = "pb%d" % (ih % 2)
            bank = sb[bn]
            ptile = ih // 2
            row0 = (ih % 2) * 64
            p.op("pe", lambda h, bank=bank, ptile=ptile, row0=row0: h.matmul(
                bank[:, 0:NBLK],
                sb["qiT"][row0:row0 + 64, ptile * T + t0:ptile * T + t0 + 128],
                sb["kcomp_bf"][row0:row0 + 64, :], start=True, stop=True),
                reads=[r("qiT"), r("kcomp_bf")], writes=[r(bn)])
            wcol = sb["wih"][:, st * 4 + ih:st * 4 + ih + 1]
            if ih == 0:
                p.op("dve", lambda h, bank=bank, wcol=wcol: h.scalar_tensor_tensor(
                    out=sb["iscc"][:], in0=bank[:, 0:NBLK], scalar=0.0,
                    in1=wcol.to_broadcast([128, NBLK]), op0=ALU.max, op1=ALU.mult),
                    reads=[r(bn), r("wih")], writes=[r("iscc")])
            else:
                p.op("dve", lambda h, bank=bank, wcol=wcol: h.scalar_tensor_tensor(
                    out=sb["isc2"][:], in0=bank[:, 0:NBLK], scalar=0.0,
                    in1=wcol.to_broadcast([128, NBLK]), op0=ALU.max, op1=ALU.mult),
                    reads=[r(bn), r("wih")], writes=[r("isc2")])
                p.op("dve", lambda h: h.tensor_tensor(
                    out=sb["iscc"][:], in0=sb["iscc"][:], in1=sb["isc2"][:],
                    op=ALU.add),
                    reads=[r("isc2")], writes=[r("iscc")], strict=True)
        p.op("dve", lambda h, st=st: h.tensor_tensor(
            out=sb["iscc"][:], in0=sb["iscc"][:],
            in1=sb["causal"][:, st * NBLK:(st + 1) * NBLK], op=ALU.add),
            reads=[r("causal")], writes=[r("iscc")], strict=True)
        for rnd in range(8):
            p.op("dve", lambda h: h.max(out=sb["m8"][:], in_=sb["iscc"][:]),
                 reads=[r("iscc")], writes=[r("m8")], strict=True)
            p.op("dve", lambda h: h.match_replace(
                out=sb["iscc"][:], in_to_replace=sb["m8"][:],
                in_values=sb["iscc"][:], imm_value=SENT),
                reads=[r("m8")], writes=[r("iscc")], strict=True)
        p.op("dve", lambda h: h.tensor_scalar(
            out=sb["isc2"][:], in0=sb["iscc"][:], scalar1=SENT, scalar2=None,
            op0=ALU.is_equal),
            reads=[r("iscc")], writes=[r("isc2")], strict=True)
        p.op("act", lambda h: h.activation(
            out=sb["isc2"][:], in_=sb["isc2"][:], func=AF.Identity,
            bias=cp[:, 7:8], scale=cp[:, 6:7]),
            reads=[r("isc2"), r("constp")], writes=[r("isc2")])
        p.op("dve", lambda h, st=st: h.tensor_tensor(
            out=sb["mask_sb"][:, st * NBLK:(st + 1) * NBLK], in0=sb["isc2"][:],
            in1=sb["causal"][:, st * NBLK:(st + 1) * NBLK], op=ALU.add),
            reads=[r("isc2"), r("causal")], writes=[r("mask_sb")])

    def p3_alloc(self, ctx):
        p, dr = self.p, self.dram
        sb = self.sbuf
        sb("winmask", (128, 4 * 512), BF16, ctx)
        sb("w_qup_bf", (128, 4 * D), BF16, ctx)
        sb("groupw_bf", (128, 16 * 512), BF16, ctx)
        sb("wstage3", (128, 512), F32, ctx)
        for s in range(2):
            sb("qpre%d" % s, (128, 512), F32, ctx)
            sb("qrt%d" % s, (128, 512), F32, ctx)
            sb("qhat_tm%d" % s, (128, 512), BF16, ctx)
            sb("qstat%d" % s, (128, 4), F32, ctx)
            sb("ssb%d" % s, (128, 256), F32, ctx)
            sb("wsb%d" % s, (128, 512), F32, ctx)
            sb("nd%d" % s, (128, 4), F32, ctx)
            sb("pbf%d" % s, (128, 256), BF16, ctx)
            sb("pwb%d" % s, (128, 512), BF16, ctx)
            sb("ptb%d" % s, (128, 256), BF16, ctx)
            sb("ptw%d" % s, (128, 512), BF16, ctx)
            sb("tmps%d" % s, (128, 128), F32, ctx)
        sb("qhatT", (128, 16 * 512), BF16, ctx)
        sb("attn_bf", (128, 16 * 512), BF16, ctx)
        sb("goT", (128, 16 * 512), BF16, ctx)
        sb("sstage", (128, D), BF16, ctx)
        sb("fwstage", (128, 2 * D), BF16, ctx)
        self.res["fwstage0"] = Res()
        self.res["fwstage1"] = Res()
        p.op("sp", lambda h: h.dma_start(out=self.sb["winmask"][:],
                                         in_=dr["winmask"][:, :]),
             writes=[self.r("winmask")], dma="winmask")

        def conv(dstname, col, src_ap, ncols):
            st = self.sb["wstage3"]
            p.op("sp", lambda h, s=src_ap, n=ncols: h.dma_start(
                out=st[:, :n], in_=s), writes=[self.r("wstage3")], dma="wstage3")
            p.op("dve", lambda h, d=self.sb[dstname], c=col, n=ncols:
                 h.tensor_copy(out=d[:, c:c + n], in_=st[:, :n]),
                 reads=[self.r("wstage3")], writes=[self.r(dstname)])
        for i in range(4):
            for c4 in range(4):
                conv("w_qup_bf", i * D + c4 * 512,
                     dr["w_qup"][i * 128:(i + 1) * 128, c4 * 512:(c4 + 1) * 512],
                     512)
        for gi in range(4):
            for isub in range(4):
                conv("groupw_bf", (gi * 4 + isub) * 512,
                     dr["group_w"][gi, isub * 128:(isub + 1) * 128, :], 512)

    def p3(self, u):
        p, sb, dr, r = self.p, self.sb, self.dram, self.r
        t0 = u * 512
        cp = sb["constp"]

        for it in range(16) if "qhat" not in DISABLE else []:
            tsub, hg = it // 4, it % 4
            gt0 = t0 + tsub * 128
            par = it % 2
            qpre = sb["qpre%d" % par]
            qrt = sb["qrt%d" % par]
            qstat = sb["qstat%d" % par]
            qhat_tm = sb["qhat_tm%d" % par]
            rpre, rrt, rst, rtm = (r("qpre%d" % par), r("qrt%d" % par),
                                   r("qstat%d" % par), r("qhat_tm%d" % par))
            bn0 = "pb%d" % (par * 2)
            bn1 = "pb%d" % (par * 2 + 1)
            bank = sb[bn0]

            def qmm(h, gt0=gt0, hg=hg, bank=bank):
                last = None
                for mt in range(4):
                    last = h.matmul(
                        bank[:],
                        sb["q_cT"][:, mt * T + gt0:mt * T + gt0 + 128],
                        sb["w_qup_bf"][:, mt * D + hg * 512:
                                       mt * D + (hg + 1) * 512],
                        start=(mt == 0), stop=(mt == 3))
                return last
            p.op("pe", qmm, reads=[r("q_cT"), r("w_qup_bf")], writes=[r(bn0)])
            p.op("act", lambda h, bank=bank, qpre=qpre: h.copy(
                out=qpre[:], in_=bank[:]),
                reads=[r(bn0)], writes=[rpre])
            st_idx = gt0 // 128
            co = sb["cos"][:, st_idx * 32:(st_idx + 1) * 32]
            si = sb["sin"][:, st_idx * 32:(st_idx + 1) * 32]
            cob = co.unsqueeze(1).to_broadcast([128, 4, 32])
            sib = si.unsqueeze(1).to_broadcast([128, 4, 32])
            x1 = qpre[:].rearrange("p (h c) -> p h c", h=4)[:, :, 0:32]
            x2 = qpre[:].rearrange("p (h c) -> p h c", h=4)[:, :, 32:64]
            a_ = qrt[:, 0:128].rearrange("p (h c) -> p h c", h=4)
            b_ = qrt[:, 128:256].rearrange("p (h c) -> p h c", h=4)
            c_ = qrt[:, 256:384].rearrange("p (h c) -> p h c", h=4)
            d_ = qrt[:, 384:512].rearrange("p (h c) -> p h c", h=4)
            for dst, s0, s1 in ((a_, x1, cob), (b_, x2, sib),
                                (c_, x1, sib), (d_, x2, cob)):
                p.op("dve", lambda h, dst=dst, s0=s0, s1=s1: h.tensor_tensor(
                    out=dst, in0=s0, in1=s1, op=ALU.mult),
                    reads=[rpre, r("cos"), r("sin")], writes=[rrt])
            p.op("dve", lambda h, a_=a_, b_=b_, x1=x1: h.tensor_tensor(
                out=x1, in0=a_, in1=b_, op=ALU.subtract),
                reads=[rrt], writes=[rpre])
            p.op("dve", lambda h, c_=c_, d_=d_, x2=x2: h.tensor_tensor(
                out=x2, in0=c_, in1=d_, op=ALU.add),
                reads=[rrt], writes=[rpre])
            for hh4 in range(4):
                p.op("act", lambda h, hh4=hh4, qpre=qpre, qrt=qrt, qstat=qstat:
                     h.activation(
                         out=qrt[:, 0:128],
                         in_=qpre[:, hh4 * 128:(hh4 + 1) * 128],
                         func=AF.Square, accum_out=qstat[:, hh4:hh4 + 1]),
                     reads=[rpre], writes=[rrt, rst])
            p.op("act", lambda h, qstat=qstat: h.activation(
                out=qstat[:], in_=qstat[:], func=AF.Ln,
                bias=cp[:, 0:1], scale=cp[:, 4:5]),
                reads=[rst, r("constp")], writes=[rst])
            p.op("act", lambda h, qstat=qstat: h.activation(
                out=qstat[:], in_=qstat[:], func=AF.Exp,
                bias=0.0, scale=cp[:, 2:3]),
                reads=[rst, r("constp")], writes=[rst])
            p.op("dve", lambda h, qpre=qpre, qstat=qstat, qhat_tm=qhat_tm:
                 h.tensor_tensor(
                     out=qhat_tm[:],
                     in0=qpre[:].rearrange("p (h c) -> p h c", h=4),
                     in1=qstat[:].unsqueeze(2).to_broadcast([128, 4, 128]),
                     op=ALU.mult),
                 reads=[rst, rpre], writes=[rtm])
            bank1 = sb[bn1]

            def qtr(h, bank1=bank1, qhat_tm=qhat_tm):
                last = None
                for hh4 in range(4):
                    last = h.matmul(
                        bank1[:, hh4 * 128:(hh4 + 1) * 128],
                        qhat_tm[:, hh4 * 128:(hh4 + 1) * 128],
                        sb["ident"][:], start=True, stop=True)
                return last
            p.op("pe", qtr, reads=[rtm, r("ident")], writes=[r(bn1)])
            p.op("dve", lambda h, bank1=bank1, hg=hg, tsub=tsub: h.tensor_copy(
                out=sb["qhatT"][:].rearrange(
                    "p (hd t) -> p hd t", hd=16)[:, hg * 4:hg * 4 + 4,
                                                 tsub * 128:(tsub + 1) * 128],
                in_=bank1[:].rearrange("p (hh t) -> p hh t", hh=4)),
                reads=[r(bn1)], writes=[r("qhatT")])

        for tsub in range(4) if "attn" not in DISABLE else []:
            st_idx = (t0 + tsub * 128) // 128
            nblk = st_idx // 2
            parity = st_idx % 2
            sel = parity + (2 if nblk == 0 else 0)
            for wave in range(8):
                heads = (wave * 2, wave * 2 + 1)
                ctxs = []
                for i, hh in enumerate(heads):
                    s = i
                    ctxs.append(dict(
                        hh=hh, s=s,
                        qsl=sb["qhatT"][:, hh * 512 + tsub * 128:
                                        hh * 512 + (tsub + 1) * 128],
                        bA="pb%d" % s, bB="pb%d" % (2 + s),
                        bC="pb%d" % (4 + s), bD="pb%d" % (6 + s),
                        ssb=sb["ssb%d" % s], wsb=sb["wsb%d" % s],
                        nd=sb["nd%d" % s], pbf=sb["pbf%d" % s],
                        pwb=sb["pwb%d" % s], ptb=sb["ptb%d" % s],
                        ptw=sb["ptw%d" % s],
                        rs=r("ssb%d" % s), rw=r("wsb%d" % s),
                        rnd=r("nd%d" % s), rpb=r("pbf%d" % s),
                        rpw=r("pwb%d" % s), rtb=r("ptb%d" % s),
                        rtw=r("ptw%d" % s), tmps=sb["tmps%d" % s],
                        rtm=r("tmps%d" % s)))
                for c in ctxs:
                    p.op("pe", lambda h, c=c: h.matmul(
                        sb[c["bA"]][:, 0:NBLK], c["qsl"], sb["skv_ss"][:, :],
                        start=True, stop=True),
                        reads=[r("qhatT"), r("skv_ss")], writes=[r(c["bA"])])
                for c in ctxs:
                    if nblk == 0:
                        def ws0(h, c=c):
                            h.matmul(sb[c["bB"]][:, 0:256], c["qsl"],
                                     sb["khatT"][:, 0:256],
                                     start=True, stop=True)
                            return h.matmul(sb[c["bB"]][:, 256:512], c["qsl"],
                                            sb["khatT"][:, 0:256],
                                            start=True, stop=True)
                        p.op("pe", ws0, reads=[r("qhatT"), r("khatT")],
                             writes=[r(c["bB"])])
                    else:
                        p.op("pe", lambda h, c=c, nblk=nblk: h.matmul(
                            sb[c["bB"]][:], c["qsl"],
                            sb["khatT"][:, (nblk - 1) * 256:(nblk + 1) * 256],
                            start=True, stop=True),
                            reads=[r("qhatT"), r("khatT")], writes=[r(c["bB"])])
                for c in ctxs:
                    p.op("dve", lambda h, c=c, st_idx=st_idx: h.tensor_tensor(
                        out=c["ssb"][:], in0=sb[c["bA"]][:, 0:NBLK],
                        in1=sb["mask_sb"][:, st_idx * NBLK:(st_idx + 1) * NBLK],
                        op=ALU.add),
                        reads=[r(c["bA"]), r("mask_sb")], writes=[c["rs"]])
                for c in ctxs:
                    p.op("dve", lambda h, c=c: h.reduce_max(
                        out=c["nd"][:, 0:1], in_=c["ssb"][:], axis=AX.X,
                        negate=True),
                        reads=[c["rs"]], writes=[c["rnd"]])
                for c in ctxs:
                    p.op("act", lambda h, c=c: h.activation(
                        out=c["ssb"][:], in_=c["ssb"][:], func=AF.Exp,
                        bias=c["nd"][:, 0:1], scale=1.0,
                        accum_out=c["nd"][:, 1:2]),
                        reads=[c["rs"], c["rnd"]], writes=[c["rs"], c["rnd"]])
                for c in ctxs:
                    p.op("dve", lambda h, c=c, sel=sel: h.tensor_tensor(
                        out=c["wsb"][:], in0=sb[c["bB"]][:],
                        in1=sb["winmask"][:, sel * 512:(sel + 1) * 512],
                        op=ALU.add),
                        reads=[r(c["bB"]), r("winmask")], writes=[c["rw"]])
                for c in ctxs:
                    p.op("dve", lambda h, c=c: h.reduce_max(
                        out=c["nd"][:, 2:3], in_=c["wsb"][:], axis=AX.X,
                        negate=True),
                        reads=[c["rw"]], writes=[c["rnd"]])
                for c in ctxs:
                    p.op("act", lambda h, c=c: h.activation(
                        out=c["wsb"][:], in_=c["wsb"][:], func=AF.Exp,
                        bias=c["nd"][:, 2:3], scale=1.0,
                        accum_out=c["nd"][:, 3:4]),
                        reads=[c["rw"], c["rnd"]], writes=[c["rw"], c["rnd"]])
                for c in ctxs:
                    p.op("dve", lambda h, c=c: h.reciprocal(
                        out=c["nd"][:, 1:2], in_=c["nd"][:, 1:2]),
                        reads=[c["rnd"]], writes=[c["rnd"]])
                    p.op("dve", lambda h, c=c: h.tensor_scalar(
                        out=c["pbf"][:], in0=c["ssb"][:],
                        scalar1=c["nd"][:, 1:2], scalar2=None, op0=ALU.mult),
                        reads=[c["rs"], c["rnd"]], writes=[c["rpb"]])
                    if st_idx == 0:
                        p.op("dve", lambda h, c=c: h.memset(
                            c["pbf"][0:16, :], 0.0),
                            writes=[c["rpb"]], strict=True)
                for c in ctxs:
                    def ptr(h, c=c):
                        last = None
                        for i in range(2):
                            last = h.matmul(
                                sb[c["bC"]][:, i * 128:(i + 1) * 128],
                                c["pbf"][:, i * 128:(i + 1) * 128],
                                sb["ident"][:], start=True, stop=True)
                        return last
                    p.op("pe", ptr, reads=[c["rpb"], r("ident")],
                         writes=[r(c["bC"])])
                for c in ctxs:
                    p.op("act", lambda h, c=c: h.copy(
                        out=c["ptb"][:], in_=sb[c["bC"]][:, 0:256]),
                        reads=[r(c["bC"])], writes=[c["rtb"]])
                for c in ctxs:
                    p.op("dve", lambda h, c=c: h.reciprocal(
                        out=c["nd"][:, 3:4], in_=c["nd"][:, 3:4]),
                        reads=[c["rnd"]], writes=[c["rnd"]])
                    p.op("dve", lambda h, c=c: h.tensor_scalar(
                        out=c["pwb"][:], in0=c["wsb"][:],
                        scalar1=c["nd"][:, 3:4], scalar2=None, op0=ALU.mult),
                        reads=[c["rw"], c["rnd"]], writes=[c["rpw"]])
                for c in ctxs:
                    def pvs(h, c=c):
                        last = None
                        for i in range(2):
                            last = h.matmul(sb[c["bC"]][:, 256:384],
                                            sb["skv_pv"][:, i * 128:(i + 1) * 128],
                                            c["ptb"][:, i * 128:(i + 1) * 128],
                                            start=(i == 0), stop=(i == 1))
                        return last
                    p.op("pe", pvs, reads=[r("skv_pv"), c["rtb"]],
                         writes=[r(c["bC"])])
                for c in ctxs:
                    p.op("act", lambda h, c=c: h.copy(
                        out=c["tmps"][:], in_=sb[c["bC"]][:, 256:384]),
                        reads=[r(c["bC"])], writes=[c["rtm"]])
                for c in ctxs:
                    def wtr(h, c=c):
                        last = None
                        for ks in range(4):
                            last = h.matmul(sb[c["bD"]][:, ks * 128:(ks + 1) * 128],
                                            c["pwb"][:, ks * 128:(ks + 1) * 128],
                                            sb["ident"][:], start=True, stop=True)
                        return last
                    p.op("pe", wtr, reads=[c["rpw"], r("ident")],
                         writes=[r(c["bD"])])
                for c in ctxs:
                    p.op("act", lambda h, c=c: h.copy(
                        out=c["ptw"][:], in_=sb[c["bD"]][:]),
                        reads=[r(c["bD"])], writes=[c["rtw"]])
                for c in ctxs:
                    ks_lo = 2 if nblk == 0 else 0

                    def pvw(h, c=c, ks_lo=ks_lo, nblk=nblk):
                        last = None
                        for ks in range(ks_lo, 4):
                            ktile = (nblk - 1) * 2 + ks
                            last = h.matmul(sb[c["bD"]][:, 0:128],
                                            sb["v_sb"][:, ktile * 128:(ktile + 1) * 128],
                                            c["ptw"][:, ks * 128:(ks + 1) * 128],
                                            start=(ks == ks_lo), stop=(ks == 3))
                        return last
                    p.op("pe", pvw, reads=[r("v_sb"), c["rtw"]],
                         writes=[r(c["bD"])])
                for c in ctxs:
                    p.op("dve", lambda h, c=c, tsub=tsub: h.tensor_tensor(
                        out=sb["attn_bf"][:, c["hh"] * 512 + tsub * 128:
                                          c["hh"] * 512 + (tsub + 1) * 128],
                        in0=c["tmps"][:], in1=sb[c["bD"]][:, 0:128], op=ALU.add),
                        reads=[c["rtm"], r(c["bD"])], writes=[r("attn_bf")])

        for g in range(4) if "group" not in DISABLE else []:
            for osub in range(4):
                bn = "pb%d" % ((g * 4 + osub) % 2)
                bank = sb[bn]

                def gmm(h, g=g, osub=osub, bank=bank):
                    last = None
                    for isub in range(4):
                        last = h.matmul(
                            bank[:],
                            sb["groupw_bf"][:, (g * 4 + isub) * 512 + osub * 128:
                                            (g * 4 + isub) * 512 + (osub + 1) * 128],
                            sb["attn_bf"][:, (g * 4 + isub) * 512:
                                          (g * 4 + isub + 1) * 512],
                            start=(isub == 0), stop=(isub == 3))
                    return last
                p.op("pe", gmm, reads=[r("groupw_bf"), r("attn_bf")],
                     writes=[r(bn)])
                p.op("dve", lambda h, g=g, osub=osub, bank=bank: h.tensor_copy(
                    out=sb["goT"][:, (g * 4 + osub) * 512:
                                  (g * 4 + osub + 1) * 512], in_=bank[:]),
                    reads=[r(bn)], writes=[r("goT")])

        for pair in range(2) if "final" not in DISABLE else []:
            for kt in range(DT):
                slot = kt % 2
                fw = sb["fwstage"][:, slot * D:(slot + 1) * D]
                p.op("sp", lambda h, fw=fw, kt=kt: h.dma_start(
                    out=fw, in_=dr["fwT_scr"][kt, :, :]),
                    writes=[r("fwstage%d" % slot)], dma="fwstage%d" % slot)
                for ts2 in range(2):
                    tsub = pair * 2 + ts2
                    bn = "pb%d" % (ts2 * 4)
                    for oc in range(4):
                        bno = "pb%d" % (ts2 * 4 + oc)
                        p.op("pe", lambda h, kt=kt, tsub=tsub, oc=oc, fw=fw,
                             bno=bno: h.matmul(
                                 sb[bno][:],
                                 sb["goT"][:, kt * 512 + tsub * 128:
                                           kt * 512 + (tsub + 1) * 128],
                                 fw[:, oc * 512:(oc + 1) * 512],
                                 start=(kt == 0), stop=(kt == DT - 1)),
                             reads=[r("goT"), r("fwstage%d" % (kt % 2))],
                             writes=[r(bno)])
            for ts2 in range(2):
                tsub = pair * 2 + ts2
                st_idx = (t0 + tsub * 128) // 128
                for oc in range(4):
                    bno = "pb%d" % (ts2 * 4 + oc)
                    p.op("dve", lambda h, oc=oc, bno=bno: h.tensor_copy(
                        out=sb["sstage"][:, oc * 512:(oc + 1) * 512],
                        in_=sb[bno][:]),
                        reads=[r(bno)], writes=[r("sstage")])
                p.op("pool", lambda h, st=st_idx: h.dma_start(
                    out=dr["s_scr"][st, :, :], in_=sb["sstage"][:]),
                    reads=[r("sstage")], dma="s_sp")

    def p05_alloc(self, ctx):
        sb = self.sbuf
        sb("wq_i32", (128, 1024), I32, ctx)
        sb("wq_bf", (128, 2 * 2048), BF16, ctx)
        sb("tev", (128, 2 * 512), BF16, ctx)
        for nm in ("wq_i32", "wq_bf", "tev"):
            self.res[nm + "_1"] = Res()
        self._units = ([("g", j, 0) for j in range(HT)]
                       + [("u", j, 0) for j in range(HT)]
                       + [("d", ot, part) for ot in range(DT) for part in range(3)])
        self._ui = 0

    def p05_chunk(self, n_units):
        p, sb, dr, r = self.p, self.sb, self.dram, self.r
        for _ in range(n_units):
            if self._ui >= len(self._units):
                return
            kind, j, part = self._units[self._ui]
            usl = self._ui % 2
            self._ui += 1
            i32 = sb["wq_i32"]
            wbf = sb["wq_bf"][:, usl * 2048:(usl + 1) * 2048]
            ri32 = r("wq_i32")
            rbf = r("wq_bf") if usl == 0 else r("wq_bf_1")

            def load_conv(src_rows, col0, ncol):
                done = 0
                while done < ncol:
                    n = min(1024, ncol - done)
                    p.op("sp", lambda h, s=src_rows, d0=done, n=n, i32=i32,
                         col0=col0: h.dma_start(
                        out=i32[:, :n], in_=s[:, col0 + d0:col0 + d0 + n]),
                        writes=[ri32], dma="wq_i32")
                    p.op("dve", lambda h, d0=done, n=n, i32=i32, wbf=wbf,
                         col0=col0: h.tensor_scalar(
                        out=wbf[:, col0 + d0:col0 + d0 + n], in0=i32[:, :n],
                        scalar1=-128.0, scalar2=None, op0=ALU.add),
                        reads=[ri32], writes=[rbf])
                    done += n
            if kind in ("g", "u"):
                src = dr["wq_gate"] if kind == "g" else dr["wq_up"]
                dst = dr["wgT_scr"] if kind == "g" else dr["wuT_scr"]
                load_conv(src[j * 128:(j + 1) * 128, :], 0, 2048)
                for grp in range(4):
                    bn = "pb%d" % (6 + grp % 2)
                    bank = sb[bn]

                    def tmm(h, wbf=wbf, grp=grp, bank=bank):
                        last = None
                        for s in range(4):
                            dt = grp * 4 + s
                            last = h.matmul(bank[:, s * 128:(s + 1) * 128],
                                            wbf[:, dt * 128:(dt + 1) * 128],
                                            sb["ident"][:], start=True,
                                            stop=True)
                        return last
                    p.op("pe", tmm, reads=[rbf, r("ident")],
                         writes=[r(bn)])
                    tv = sb["tev"][:, usl * 512:(usl + 1) * 512]
                    rtv = r("tev") if usl == 0 else r("tev_1")
                    p.op("act", lambda h, tv=tv, bank=bank: h.copy(
                        out=tv, in_=bank[:]),
                        reads=[r(bn)], writes=[rtv])
                    p.op("pool", lambda h, tv=tv, dst=dst, j=j, grp=grp:
                         h.dma_start(
                             out=dst[j, :, grp * 512:(grp + 1) * 512], in_=tv),
                         reads=[rtv], dma="tev_st%d" % usl)
            elif kind == "d":
                ncol = 2048 if part < 2 else 1536
                nsub = ncol // 128
                base_ht = part * 16
                load_conv(dr["wq_down"][j * 128:(j + 1) * 128,
                                        part * 2048:part * 2048 + 2048][:, :ncol]
                          if False else
                          dr["wq_down"][j * 128:(j + 1) * 128,
                                        part * 2048:part * 2048 + ncol], 0, ncol)
                for grp in range(nsub // 4):
                    bn = "pb%d" % (6 + grp % 2)
                    bank = sb[bn]

                    def tmm(h, wbf=wbf, grp=grp, bank=bank):
                        last = None
                        for s in range(4):
                            ht_ = grp * 4 + s
                            last = h.matmul(bank[:, s * 128:(s + 1) * 128],
                                            wbf[:, ht_ * 128:(ht_ + 1) * 128],
                                            sb["ident"][:], start=True,
                                            stop=True)
                        return last
                    p.op("pe", tmm, reads=[rbf, r("ident")],
                         writes=[r(bn)])
                    tv = sb["tev"][:, usl * 512:(usl + 1) * 512]
                    rtv = r("tev") if usl == 0 else r("tev_1")
                    sdsl = sb["sdown"][:, j * 128:(j + 1) * 128]
                    p.op("dve", lambda h, tv=tv, bank=bank, sdsl=sdsl:
                         h.tensor_tensor(
                             out=tv.rearrange("p (s o) -> p s o", s=4),
                             in0=bank[:].rearrange("p (s o) -> p s o", s=4),
                             in1=sdsl.unsqueeze(1).to_broadcast([128, 4, 128]),
                             op=ALU.mult),
                         reads=[r(bn), r("sdown")], writes=[rtv])
                    p.op("pool", lambda h, tv=tv, j=j, b=base_ht, grp=grp:
                         h.dma_start(
                             out=dr["wdT_scr"][b + grp * 4:b + (grp + 1) * 4,
                                               :, j * 128:(j + 1) * 128].rearrange(
                                 "ht p o -> p ht o"),
                             in_=tv.rearrange("p (ht o) -> p ht o", ht=4)),
                         reads=[rtv], dma="tev_st%d" % usl)
            else:
                raise AssertionError(kind)

    def p4_alloc(self, ctx):
        p, dr = self.p, self.dram
        sb = self.sbuf
        sb("sgate", (128, HT), F32, ctx)
        sb("sup", (128, HT), F32, ctx)
        sb("xnTs", (128, DT * 1024), BF16, ctx)
        for s in range(2):
            sb("wg_t%d" % s, (128, DT * 128), BF16, ctx)
            sb("wu_t%d" % s, (128, DT * 128), BF16, ctx)
        for s in range(3):
            sb("wd_t%d" % s, (128, D), BF16, ctx)
        sb("hact", (128, HT * 1024), BF16, ctx)
        sb("silu_s", (128, 2 * 512), BF16, ctx)
        self.res["silu_s1"] = Res()
        sb("xres", (128, D), F32, ctx)
        sb("sres", (128, D), BF16, ctx)
        sb("gbuf", (128, D), BF16, ctx)
        sb("obuf", (128, D), F32, ctx)
        for nm in ("sgate", "sup"):
            p.op("sp", lambda h, d=self.sb[nm], s=dr[nm]: h.dma_start(
                out=d[:], in_=s[:, :]), writes=[self.r(nm)], dma=nm)

    def p4(self, v):
        p, sb, dr, r = self.p, self.sb, self.dram, self.r
        t0 = v * 1024
        p.op("pool", lambda h: h.dma_start(
            out=sb["xnTs"][:].rearrange("p (dt t) -> p dt t", dt=DT),
            in_=dr["xnT_scr"][:, :, t0:t0 + 1024].rearrange("dt p t -> p dt t")),
            writes=[r("xnTs")], dma="xnTs")

        for j in range(HT):
            sl = j % 2
            wg = sb["wg_t%d" % sl]
            wu = sb["wu_t%d" % sl]
            p.op("sp", lambda h, wg=wg, j=j: h.dma_start(
                out=wg[:], in_=dr["wgT_scr"][j, :, :]),
                writes=[r("wg_t%d" % sl)], dma="wg_t%d" % sl)
            p.op("sp", lambda h, wu=wu, j=j: h.dma_start(
                out=wu[:], in_=dr["wuT_scr"][j, :, :]),
                writes=[r("wu_t%d" % sl)], dma="wu_t%d" % sl)
            bg = "pb%d" % (2 * sl)
            bu = "pb%d" % (2 * sl + 1)
            silu_res = "silu_s" if sl == 0 else "silu_s1"
            ssl = sb["silu_s"][:, sl * 512:(sl + 1) * 512]
            for c in range(2):
                for which, wt, bn in (("g", wg, bg), ("u", wu, bu)):
                    bank = sb[bn]

                    def gmm(h, wt=wt, c=c, bank=bank):
                        last = None
                        for dt in range(DT):
                            last = h.matmul(
                                bank[:],
                                wt[:, dt * 128:(dt + 1) * 128],
                                sb["xnTs"][:, dt * 1024 + c * 512:
                                           dt * 1024 + c * 512 + 512],
                                start=(dt == 0), stop=(dt == DT - 1))
                        return last
                    p.op("pe", gmm, reads=[r("wg_t%d" % sl) if which == "g"
                                           else r("wu_t%d" % sl), r("xnTs")],
                         writes=[r(bn)])
                p.op("act", lambda h, j=j, bg=bg, ssl=ssl: h.activation(
                    out=ssl, in_=sb[bg][:], func=AF.Silu,
                    bias=0.0, scale=sb["sgate"][:, j:j + 1]),
                    reads=[r(bg), r("sgate")], writes=[r(silu_res)])
                p.op("dve", lambda h, j=j, c=c, bu=bu, ssl=ssl:
                     h.scalar_tensor_tensor(
                         out=sb["hact"][:, j * 1024 + c * 512:
                                        j * 1024 + c * 512 + 512],
                         in0=sb[bu][:], scalar=sb["sup"][:, j:j + 1],
                         in1=ssl, op0=ALU.mult, op1=ALU.mult),
                     reads=[r(bu), r("sup"), r(silu_res)], writes=[r("hact")])

        for tg in range(4):
            for j in range(HT):
                sl = j % 3
                wd = sb["wd_t%d" % sl]
                p.op("sp", lambda h, wd=wd, j=j: h.dma_start(
                    out=wd[:], in_=dr["wdT_scr"][j, :, :]),
                    writes=[r("wd_t%d" % sl)], dma="wd_t%d" % sl)
                for ts2 in range(2):
                    for oc in range(4):
                        bno = "pb%d" % (ts2 * 4 + oc)
                        p.op("pe", lambda h, j=j, tg=tg, ts2=ts2, oc=oc, wd=wd,
                             bno=bno: h.matmul(
                                 sb[bno][:],
                                 sb["hact"][:, j * 1024 + tg * 256 + ts2 * 128:
                                            j * 1024 + tg * 256 + (ts2 + 1) * 128],
                                 wd[:, oc * 512:(oc + 1) * 512],
                                 start=(j == 0), stop=(j == HT - 1)),
                             reads=[r("hact"), r("wd_t%d" % sl)],
                             writes=[r(bno)])
            for ts2 in range(2):
                trow = t0 + tg * 256 + ts2 * 128
                st_idx = trow // 128
                for oc in range(4):
                    bno = "pb%d" % (ts2 * 4 + oc)
                    p.op("act", lambda h, oc=oc, bno=bno: h.activation(
                        out=sb["gbuf"][:, oc * 512:(oc + 1) * 512],
                        in_=sb[bno][:], func=AF.Silu),
                        reads=[r(bno)], writes=[r("gbuf")])
                p.op("pool", lambda h, trow=trow: h.dma_start(
                    out=sb["xres"][:], in_=dr["x"][trow:trow + 128, :]),
                    writes=[r("xres")], dma="xres")
                p.op("pool", lambda h, st=st_idx: h.dma_start(
                    out=sb["sres"][:], in_=dr["s_scr"][st, :, :]),
                    writes=[r("sres")], dma="sres")
                p.op("dve", lambda h: h.tensor_tensor(
                    out=sb["obuf"][:], in0=sb["sres"][:], in1=sb["gbuf"][:],
                    op=ALU.mult),
                    reads=[r("sres"), r("gbuf")], writes=[r("obuf")])
                p.op("dve", lambda h: h.tensor_tensor(
                    out=sb["obuf"][:], in0=sb["obuf"][:], in1=sb["xres"][:],
                    op=ALU.add),
                    reads=[r("xres")], writes=[r("obuf")], strict=True)
                p.op("pool", lambda h, trow=trow: h.dma_start(
                    out=dr["y"][trow:trow + 128, :], in_=sb["obuf"][:]),
                    reads=[r("obuf")], dma="y_st")

    def emit_debug(self):
        sb = self.sb
        self.dbg("q_cT", lambda: sb["q_cT"][:], (128, 4 * T), BF16, ["q_cT"])
        self.dbg("khatT", lambda: sb["khatT"][:], (128, T), BF16, ["khatT"])
        self.dbg("v_sb", lambda: sb["v_sb"][:], (128, NT * 128), BF16, ["v_sb"])
        self.dbg("qiT", lambda: sb["qiT"][:], (128, 2 * T), BF16, ["qiT"])
        self.dbg("wih", lambda: sb["wih"][:], (128, NT * 4), F32, ["wih"])
        self.dbg("ckvT", lambda: sb["ckvT"][:], (128, NBLK), F32, ["ckvT"])
        self.dbg("kcompT", lambda: sb["kcompT"][:64, :], (64, NBLK), F32,
                 ["kcompT"])
        self.dbg("skv_ss", lambda: sb["skv_ss"][:], (128, NBLK), BF16,
                 ["skv_ss"])
        self.dbg("skv_pv", lambda: sb["skv_pv"][:], (128, 2 * 128), BF16,
                 ["skv_pv"])
        if "xnT" in self.dbg_names:
            d = self.dout("dbg_xnT", (DT, 128, T), BF16)
            self.p.op("pool", lambda h, d=d: h.dma_start(
                out=d[:, :, :], in_=self.dram["xnT_scr"][:, :, :]), dma="dbg")
        if "wgT" in self.dbg_names:
            d = self.dout("dbg_wgT", (HT, 128, DT * 128), BF16)
            self.p.op("pool", lambda h, d=d: h.dma_start(
                out=d[:, :, :], in_=self.dram["wgT_scr"][:, :, :]),
                dma="dbg")
        if "wdT" in self.dbg_names:
            d = self.dout("dbg_wdT", (HT, 128, D), BF16)
            self.p.op("pool", lambda h, d=d: h.dma_start(
                out=d[:, :, :], in_=self.dram["wdT_scr"][:, :, :]), dma="dbg")
        if "mask" in self.dbg_names:
            d = self.dout("dbg_mask", (128, NT * NBLK), BF16)
            self.p.op("pool", lambda h, d=d: h.dma_start(
                out=d[:, :], in_=sb["mask_sb"][:]),
                reads=[self.r("mask_sb")], dma="dbg")
        for nm in ("ssb0", "wsb0", "pbf0", "pwb0", "ptb0", "ptw0", "tmps0"):
            if nm in self.dbg_names:
                hnd = self.sb[nm]
                shp = (128, hnd.shape[1])
                dt = hnd.dtype if hasattr(hnd, "dtype") else None
                d = self.dout("dbg_" + nm, shp, dt)
                self.p.op("pool", lambda h, d=d, hnd=hnd: h.dma_start(
                    out=d[:, :], in_=hnd[:]),
                    reads=[self.r(nm)], dma="dbg")
        if "qhatT" in self.dbg_names:
            d = self.dout("dbg_qhatT", (128, 16 * 512), BF16)
            self.p.op("pool", lambda h, d=d: h.dma_start(
                out=d[:, :], in_=self.sb["qhatT"][:]),
                reads=[self.r("qhatT")], dma="dbg")
        if "attn_bf" in self.dbg_names:
            d = self.dout("dbg_attn", (128, 16 * 512), BF16)
            self.p.op("pool", lambda h, d=d: h.dma_start(
                out=d[:, :], in_=self.sb["attn_bf"][:]),
                reads=[self.r("attn_bf")], dma="dbg")
        if "goT" in self.dbg_names:
            d = self.dout("dbg_goT", (128, 16 * 512), BF16)
            self.p.op("pool", lambda h, d=d: h.dma_start(
                out=d[:, :], in_=self.sb["goT"][:]),
                reads=[self.r("goT")], dma="dbg")
        if "s" in self.dbg_names:
            d = self.dout("dbg_s", (NT, 128, D), BF16)
            self.p.op("pool", lambda h, d=d: h.dma_start(
                out=d[:, :, :], in_=self.dram["s_scr"][:, :, :]), dma="dbg")

    def build(self):
        p = self.p
        self.p0_globals()
        with ExitStack() as cA:
            self.p0_allocA(cA)
            with ExitStack() as cAB:
                self.sbuf("qiT", (128, 2 * T), BF16, cAB)
                self.sbuf("wih", (128, NT * 4), F32, cAB)
                with ExitStack() as cB:
                    self.p0_weights(cB)
                    self.p1_alloc(cB)
                    self.p05_alloc(cB)
                    p.barrier()
                    for u in range(NSUP):
                        self.p1(u)
                        self.p05_chunk(19)
                    self.p1_post()
                    self.p05_chunk(10 ** 9)
                    p.barrier()
                with ExitStack() as cC:
                    self.p2_alloc(cC)
                    for st in range(NT):
                        self.p2(st)
                        if st < DT:
                            self.p2_fwconv(st)
                    p.barrier()
            with ExitStack() as cD:
                self.p3_alloc(cD)
                for u in range(NSUP):
                    self.p3(u)
                p.barrier()
            self.emit_debug()
            p.barrier()
        with ExitStack() as cE:
            self.p4_alloc(cE)
            for v in range(T // 1024):
                self.p4(v)
            p.barrier()
        block = self.ctx.enter_context(self.nc.Block())
        p.replay(block)
        self.ctx.close()
        return self.nc


_BUILD_CACHE = {}


def build_nc(dbg_names=()):
    key = tuple(sorted(dbg_names))
    if key not in _BUILD_CACHE:
        b = Builder(dbg_names)
        _BUILD_CACHE[key] = (b.build(), b)
    return _BUILD_CACHE[key]


def kernel(**inputs):
    shared, per_core = host_prep(inputs)
    nc, b = build_nc()
    in_maps = []
    for c in range(8):
        m = dict(shared)
        m.update(per_core[c])
        in_maps.append(m)
    res = run_bass_kernel_spmd(nc, in_maps, core_ids=list(range(8)))
    out = np.stack([res.results[c]["y"] for c in range(8)], axis=0)
    return out.astype(np.float32)


# version 07
# speedup vs baseline: 1.4163x; 1.1178x over previous
import math
from contextlib import ExitStack

import numpy as np
import ml_dtypes

import concourse.bass as bass
import concourse.mybir as mybir
from concourse.bass_utils import run_bass_kernel_spmd

F32 = mybir.dt.float32
BF16 = mybir.dt.bfloat16
I32 = mybir.dt.int32
AF = mybir.ActivationFunctionType
ALU = mybir.AluOpType
AX = mybir.AxisListType

BF = ml_dtypes.bfloat16

T = 4096
D = 2048
H = 16
HD = 128
QC = 512
NBLK = 256
MBLK = 16
CI = 64
NIH = 4
WIN = 256
HID = 5632
NEG = -1.0e30
SENT = -2.0e30
SCALE = 1.0 / math.sqrt(128.0)

NT = T // 128
NSUP = T // 512
HT = HID // 128
DT = D // 128

ENGINES = ("pe", "dve", "act", "pool", "sp")
DISABLE = set()
ROPE_CAP = 6


class Res:
    __slots__ = ("w", "rs")

    def __init__(self):
        self.w = None
        self.rs = []

    def for_read(self):
        return [self.w] if self.w is not None else []

    def for_write(self):
        out = list(self.rs)
        if self.w is not None:
            out.append(self.w)
        return out


class Prog:

    def __init__(self, nc, ctx):
        self.nc = nc
        self.ctx = ctx
        self.ops = {e: [] for e in ENGINES}
        self.sems = {}
        self.cnt = {}
        for name in ENGINES:
            self.sems[name] = ctx.enter_context(nc.semaphore("sem_" + name))
            self.cnt[name] = 0
        self.dma_sems = []
        self.last_tok = {e: None for e in ENGINES}
        self.floor = {e: {} for e in ENGINES}

    def _dma_sem(self, name):
        key = "dma_" + name
        if key not in self.sems:
            self.sems[key] = self.ctx.enter_context(
                self.nc.semaphore("sem_" + key))
            self.cnt[key] = 0
            self.dma_sems.append(key)
        return key

    def op(self, eng, fn, reads=(), writes=(), waits=(), dma=None, strict=False):
        wl = {}
        fl = self.floor[eng]

        def add(tok):
            if tok is None:
                return
            sem, val, teng = tok
            if fl.get(sem, -1) >= val:
                return
            if wl.get(sem, -1) < val:
                wl[sem] = val
        for rres in reads:
            for tok in rres.for_read():
                add(tok)
        for wres in writes:
            for tok in wres.for_write():
                add(tok)
        for tok in waits:
            add(tok)
        for sem, val in wl.items():
            assert val <= self.cnt[sem], (sem, val, self.cnt[sem])
            fl[sem] = val
        if dma is None:
            sem, k = eng, 1
        else:
            sem, k = self._dma_sem(dma), 16
        self.cnt[sem] += k
        tok = (sem, self.cnt[sem], None if dma is not None else eng)
        self.ops[eng].append((fn, sorted(wl.items()), sem, k))
        if dma is None:
            self.last_tok[eng] = tok
        for rres in reads:
            rres.rs.append(tok)
        for wres in writes:
            wres.w = tok
            wres.rs = []
        return tok

    def barrier(self):
        toks = [t for t in self.last_tok.values() if t is not None]
        for key in self.dma_sems:
            if self.cnt[key]:
                toks.append((key, self.cnt[key], None))
        for e in ENGINES:
            self.op(e, lambda h: h.nop(), waits=toks)

    def replay(self, block):
        sems = self.sems

        def mk_body(eng_name):
            def run(h):
                for fn, waits, sem, k in self.ops[eng_name]:
                    for wsem, val in waits:
                        h.wait_ge(sems[wsem], val)
                    instr = fn(h)
                    if instr is not None:
                        instr.then_inc(sems[sem], k)
                    else:
                        h.nop().then_inc(sems[sem], k)
            return run
        block.tensor(mk_body("pe"))
        block.vector(mk_body("dve"))
        block.scalar(mk_body("act"))
        block.gpsimd(mk_body("pool"))
        block.sync(mk_body("sp"))


def host_constants():
    c = {}
    c["ident"] = np.eye(128, dtype=BF)
    inv = 1.0 / 10000.0 ** (np.arange(0, 64, 2, dtype=np.float64) / 64.0)
    ang = np.arange(T, dtype=np.float64)[:, None] * inv
    cos = np.cos(ang).astype(np.float32)
    sin = np.sin(ang).astype(np.float32)
    c["cosp"] = np.ascontiguousarray(
        cos.reshape(NT, 128, 32).transpose(1, 0, 2).reshape(128, NT * 32))
    c["sinp"] = np.ascontiguousarray(
        sin.reshape(NT, 128, 32).transpose(1, 0, 2).reshape(128, NT * 32))
    tvec = np.arange(T)
    nvec = np.arange(NBLK)
    causal = nvec[None, :] < (tvec[:, None] // MBLK)
    ca = np.where(causal, 0.0, NEG).astype(np.float32)
    c["causal"] = np.ascontiguousarray(
        ca.reshape(NT, 128, NBLK).transpose(1, 0, 2).reshape(128, NT * NBLK)
    ).astype(BF)
    diff = np.arange(2 * WIN)[None, :] - np.arange(WIN)[:, None]
    win_ok = (diff > 0) & (diff <= WIN)
    first_ok = np.arange(2 * WIN)[None, :] >= WIN
    mask_n = np.where(win_ok, 0.0, NEG).astype(np.float32)
    mask_0 = np.where(win_ok & first_ok, 0.0, NEG).astype(np.float32)
    wm = np.concatenate([mask_n[:128], mask_n[128:], mask_0[:128], mask_0[128:]],
                        axis=1)
    c["winmask"] = np.ascontiguousarray(wm).astype(BF)
    return c


def host_prep(inputs):
    g = lambda k: np.ascontiguousarray(np.asarray(inputs[k]))
    shared = dict(host_constants())
    shared["w_qc"] = g("L0_attn_w_qc")
    shared["w_qup"] = g("L0_attn_w_qup")
    shared["W1"] = np.ascontiguousarray(np.concatenate(
        [g("L0_attn_k_proj_w"), g("L0_attn_v_proj_w"), g("L0_attn_w_w")], axis=1))
    shared["W2"] = np.ascontiguousarray(np.concatenate(
        [g("L0_attn_kvc_w"), g("L0_attn_kvc_wz"), g("L0_attn_idx_c_w"),
         g("L0_attn_idx_c_wz"), g("L0_attn_w_dq")], axis=1))
    shared["w_iuq"] = g("L0_attn_w_iuq")
    shared["biasTkv"] = np.ascontiguousarray(g("L0_attn_kvc_bias").T)
    shared["biasTidx"] = np.ascontiguousarray(g("L0_attn_idx_c_bias").T)
    qn = g("L0_attn_q_norm_w").reshape(-1)
    kn = g("L0_attn_k_norm_w").reshape(-1)
    shared["qn_col"] = np.ascontiguousarray(qn[:, None])
    shared["kn_col"] = np.ascontiguousarray(kn[:, None])
    shared["qn_rep"] = np.ascontiguousarray(np.broadcast_to(qn[None, :], (128, 128)))
    shared["kn_rep"] = np.ascontiguousarray(np.broadcast_to(kn[None, :], (128, 128)))
    shared["group_w"] = g("L0_attn_group_w")
    shared["final_w"] = g("L0_attn_final_w")
    shared["wq_gate"] = g("L0_ffn_wq_gate")
    shared["wq_up"] = g("L0_ffn_wq_up")
    shared["wq_down"] = g("L0_ffn_wq_down")
    shared["sgate"] = np.ascontiguousarray(g("L0_ffn_s_gate").reshape(HT, 128).T)
    shared["sup"] = np.ascontiguousarray(g("L0_ffn_s_up").reshape(HT, 128).T)
    shared["sdown_rep"] = np.ascontiguousarray(
        np.broadcast_to(g("L0_ffn_s_down").reshape(-1)[None, :], (128, D)))
    x = np.asarray(inputs["L0_attn_h"])
    per_core = [{"x": np.ascontiguousarray(x[b])} for b in range(8)]
    return shared, per_core


class Builder:
    def __init__(self, dbg_names=()):
        self.nc = bass.Bass("TRN2", target_bir_lowering=False, debug=False,
                            num_swdge_queues=4)
        self.ctx = ExitStack()
        self.p = Prog(self.nc, self.ctx)
        self.dbg_names = set(dbg_names)
        self.dram = {}
        self.sb = {}
        self.res = {}

    def din(self, name, shape, dtype):
        self.dram[name] = self.nc.dram_tensor(name, list(shape), dtype,
                                              kind="ExternalInput").ap()
        return self.dram[name]

    def dout(self, name, shape, dtype):
        self.dram[name] = self.nc.dram_tensor(name, list(shape), dtype,
                                              kind="ExternalOutput").ap()
        return self.dram[name]

    def dscr(self, name, shape, dtype):
        self.dram[name] = self.nc.dram_tensor(name, list(shape), dtype,
                                              kind="Internal").ap()
        return self.dram[name]

    def sbuf(self, name, shape, dtype, ctx=None):
        h = (ctx or self.ctx).enter_context(
            self.nc.sbuf_tensor("sb_" + name, list(shape), dtype))
        self.sb[name] = h
        self.res[name] = Res()
        return h

    def psum(self, name, ctx=None):
        h = (ctx or self.ctx).enter_context(
            self.nc.psum_tensor("ps_" + name, [128, 512], F32))
        self.sb[name] = h
        self.res[name] = Res()
        return h

    def r(self, name):
        return self.res[name]

    def dbg(self, name, ap_fn, shape, dtype, reads):
        if name not in self.dbg_names:
            return
        d = self.dout("dbg_" + name, shape, dtype)
        self.p.op("pool", lambda h, d=d, f=ap_fn: h.dma_start(out=d, in_=f()),
                  reads=[self.r(n) for n in reads], dma="dbg")

    def p0_globals(self):
        p, dr = self.p, self.dram
        for i in range(8):
            self.psum("pb%d" % i)
        self.din("x", (T, D), F32)
        for nm, sh, dt in [
            ("ident", (128, 128), BF16), ("cosp", (128, NT * 32), F32),
            ("sinp", (128, NT * 32), F32), ("causal", (128, NT * NBLK), BF16),
            ("winmask", (128, 4 * 512), BF16),
            ("w_qc", (D, QC), F32), ("w_qup", (QC, D), F32),
            ("W1", (D, 260), F32), ("W2", (D, 448), F32),
            ("w_iuq", (64, 256), F32),
            ("biasTkv", (128, 16), F32), ("biasTidx", (64, 16), F32),
            ("qn_col", (128, 1), F32), ("kn_col", (128, 1), F32),
            ("qn_rep", (128, 128), F32), ("kn_rep", (128, 128), F32),
            ("group_w", (4, 512, 512), F32), ("final_w", (D, D), F32),
            ("wq_gate", (HID, D), I32), ("wq_up", (HID, D), I32),
            ("wq_down", (D, HID), I32),
            ("sgate", (128, HT), F32), ("sup", (128, HT), F32),
            ("sdown_rep", (128, D), F32),
        ]:
            self.din(nm, sh, dt)
        self.dout("y", (T, D), F32)
        self.dscr("xnT_scr", (DT, 128, T), BF16)
        self.dscr("s_scr", (NT, 128, D), BF16)
        self.dscr("wgT_scr", (HT, 128, DT * 128), BF16)
        self.dscr("wuT_scr", (HT, 128, DT * 128), BF16)
        self.dscr("wdT_scr", (HT, 128, D), BF16)
        self.dscr("fwT_scr", (DT, 128, D), BF16)

        sb = self.sbuf
        sb("ident", (128, 128), BF16)
        sb("constp", (128, 8), F32)
        sb("sdown", (128, D), F32)
        sb("qn_col", (128, 1), F32)
        sb("kn_col", (128, 1), F32)
        sb("gains_col", (128, 1), F32)
        sb("ones_col", (128, 1), F32)
        sb("ones_row", (1, 128), F32)

        for nm in ("ident", "qn_col", "kn_col"):
            p.op("sp", lambda h, d=self.sb[nm], s=dr[nm]: h.dma_start(
                out=d[:], in_=s[:, :]), writes=[self.r(nm)], dma=nm)
        p.op("sp", lambda h: h.dma_start(out=self.sb["sdown"][:],
                                         in_=dr["sdown_rep"][:, :]),
             writes=[self.r("sdown")], dma="sdown")
        cp = self.sb["constp"]
        for slot, val in [(0, 1e-6), (1, 1e-5), (2, -0.5), (3, 1.0 / D),
                          (4, 1.0 / 128.0), (5, 0.0), (6, 1e30), (7, -1e30)]:
            p.op("dve", lambda h, s=slot, v=val: h.memset(cp[:, s:s + 1], v),
                 writes=[self.r("constp")])
        p.op("dve", lambda h: h.memset(self.sb["ones_col"][:], 1.0),
             writes=[self.r("ones_col")])
        p.op("dve", lambda h: h.memset(self.sb["ones_row"][:], 1.0),
             writes=[self.r("ones_row")])
        p.op("dve", lambda h: h.tensor_tensor(
            out=self.sb["gains_col"][:], in0=self.sb["qn_col"][:],
            in1=self.sb["kn_col"][:], op=ALU.mult),
            reads=[self.r("qn_col"), self.r("kn_col")],
            writes=[self.r("gains_col")])
        p.op("dve", lambda h: h.tensor_scalar(
            out=self.sb["gains_col"][:], in0=self.sb["gains_col"][:],
            scalar1=SCALE, scalar2=None, op0=ALU.mult),
            writes=[self.r("gains_col")], strict=True)

    def p0_allocA(self, ctx):
        p, dr = self.p, self.dram
        sb = self.sbuf
        sb("cos", (128, NT * 32), F32, ctx)
        sb("sin", (128, NT * 32), F32, ctx)
        sb("q_cT", (128, 4 * T), BF16, ctx)
        sb("khatT", (128, T), BF16, ctx)
        sb("v_sb", (128, NT * 128), BF16, ctx)
        sb("ckvT", (128, NBLK), F32, ctx)
        sb("kcompT", (64, NBLK), F32, ctx)
        sb("kcomp_bf", (128, NBLK), BF16, ctx)
        sb("skv_ss", (128, NBLK), BF16, ctx)
        sb("skv_pv", (128, 2 * 128), BF16, ctx)
        sb("mask_sb", (128, NT * NBLK), BF16, ctx)
        for nm, srcn in (("cos", "cosp"), ("sin", "sinp")):
            p.op("sp", lambda h, d=self.sb[nm], s=dr[srcn]: h.dma_start(
                out=d[:], in_=s[:, :]), writes=[self.r(nm)], dma=nm)

    def p0_weights(self, ctx):
        p, dr = self.p, self.dram
        sb = self.sbuf
        sb("wqs_rep", (128, 128), F32, ctx)
        sb("wk_rep", (128, 128), F32, ctx)
        sb("biasTkv", (128, 16), F32, ctx)
        sb("biasTidx", (64, 16), F32, ctx)
        sb("w_iuq", (64, 256), BF16, ctx)
        sb("wstage", (128, 512), F32, ctx)
        sb("w_qc_bf", (128, DT * QC), BF16, ctx)
        sb("W1_bf", (128, DT * 260), BF16, ctx)
        sb("W2_bf", (128, DT * 448), BF16, ctx)

        for nm in ("biasTkv", "biasTidx"):
            p.op("sp", lambda h, d=self.sb[nm], s=dr[nm]: h.dma_start(
                out=d[:], in_=s[:, :]), writes=[self.r(nm)], dma=nm)

        def conv(dstname, col, src_ap, ncols):
            st = self.sb["wstage"]
            p.op("sp", lambda h, s=src_ap, n=ncols: h.dma_start(
                out=st[:, :n], in_=s), writes=[self.r("wstage")], dma="wstage")
            p.op("dve", lambda h, d=self.sb[dstname], c=col, n=ncols:
                 h.tensor_copy(out=d[:, c:c + n], in_=st[:, :n]),
                 reads=[self.r("wstage")], writes=[self.r(dstname)])

        for i in range(DT):
            conv("w_qc_bf", i * QC, dr["w_qc"][i * 128:(i + 1) * 128, :], QC)
            conv("W1_bf", i * 260, dr["W1"][i * 128:(i + 1) * 128, :], 260)
            conv("W2_bf", i * 448, dr["W2"][i * 128:(i + 1) * 128, :], 448)
        st = self.sb["wstage"]
        p.op("sp", lambda h: h.dma_start(out=st[:64, :256], in_=dr["w_iuq"][:, :]),
             writes=[self.r("wstage")], dma="wstage")
        p.op("dve", lambda h: h.tensor_copy(out=self.sb["w_iuq"][:, :],
                                            in_=st[:64, :256]),
             reads=[self.r("wstage")], writes=[self.r("w_iuq")])
        p.op("sp", lambda h: h.dma_start(out=self.sb["wqs_rep"][:],
                                         in_=dr["qn_rep"][:, :]),
             writes=[self.r("wqs_rep")], dma="wqs_rep")
        p.op("dve", lambda h: h.tensor_scalar(
            out=self.sb["wqs_rep"][:], in0=self.sb["wqs_rep"][:],
            scalar1=SCALE, scalar2=None, op0=ALU.mult),
            reads=[self.r("wqs_rep")], writes=[self.r("wqs_rep")])
        p.op("sp", lambda h: h.dma_start(out=self.sb["wk_rep"][:],
                                         in_=dr["kn_rep"][:, :]),
             writes=[self.r("wk_rep")], dma="wk_rep")

    def p1_alloc(self, ctx):
        sb = self.sbuf
        sb("xstage", (128, D), F32, ctx)
        sb("lnstat", (128, 16), F32, ctx)
        sb("xn_tm", (128, 2 * D), BF16, ctx)
        sb("xnT", (128, DT * 512), BF16, ctx)
        sb("kstage", (128, 4 * 128), F32, ctx)
        sb("khat_tm", (128, 4 * 128), BF16, ctx)
        sb("dq_bf", (64, 512), BF16, ctx)
        sb("czkv", (128, 1024), F32, ctx)
        sb("czidx", (64, 1024), F32, ctx)
        sb("cmp_scr", (128, 512), F32, ctx)
        sb("cmp_red", (128, 64), F32, ctx)
        sb("rope_scr", (128, 128), F32, ctx)
        sb("skv_stg", (128, 256), BF16, ctx)

    def p1(self, u):
        p, sb, dr, r = self.p, self.sb, self.dram, self.r
        t0 = u * 512
        cp = sb["constp"]
        xst = sb["xstage"]

        for s in range(4):
            p.op("pool", lambda h, s=s: h.dma_start(
                out=xst[:],
                in_=dr["x"][t0 + s * 128:t0 + (s + 1) * 128, :]),
                writes=[r("xstage")], dma="xstage")
            xs = xst[:, 0:D]
            c_mean = sb["lnstat"][:, 4 * s:4 * s + 1]
            c_rv = sb["lnstat"][:, 4 * s + 1:4 * s + 2]
            c_t = sb["lnstat"][:, 4 * s + 2:4 * s + 3]
            p.op("dve", lambda h, xs=xs, c=c_mean: h.reduce_sum(
                out=c, in_=xs, axis=AX.X),
                reads=[r("xstage")], writes=[r("lnstat")])
            p.op("act", lambda h, xs=xs, s=s, c=c_rv: h.activation(
                out=sb["xn_tm"][:, (s % 2) * D:(s % 2 + 1) * D], in_=xs,
                func=AF.Square, accum_out=c),
                reads=[r("xstage")], writes=[r("xn_tm"), r("lnstat")])
            p.op("dve", lambda h, c=c_mean: h.tensor_scalar(
                out=c, in0=c, scalar1=1.0 / D, scalar2=None, op0=ALU.mult),
                reads=[r("lnstat")], writes=[r("lnstat")], strict=True)
            p.op("dve", lambda h, c=c_t, m=c_mean: h.tensor_tensor(
                out=c, in0=m, in1=m, op=ALU.mult),
                reads=[r("lnstat")], writes=[r("lnstat")], strict=True)
            p.op("dve", lambda h, c=c_rv, t_=c_t: h.scalar_tensor_tensor(
                out=c, in0=c, scalar=1.0 / D, in1=t_, op0=ALU.mult,
                op1=ALU.subtract),
                reads=[r("lnstat")], writes=[r("lnstat")], strict=True)
            p.op("act", lambda h, c=c_rv: h.activation(
                out=c, in_=c, func=AF.Ln, bias=cp[:, 1:2], scale=1.0),
                reads=[r("lnstat"), r("constp")], writes=[r("lnstat")])
            p.op("act", lambda h, c=c_rv: h.activation(
                out=c, in_=c, func=AF.Exp, bias=0.0, scale=cp[:, 2:3]),
                reads=[r("lnstat"), r("constp")], writes=[r("lnstat")],
                strict=True)
            p.op("dve", lambda h, c=c_t, m=c_mean, v=c_rv: h.scalar_tensor_tensor(
                out=c, in0=m, scalar=-1.0, in1=v, op0=ALU.mult, op1=ALU.mult),
                reads=[r("lnstat")], writes=[r("lnstat")])
            p.op("act", lambda h, xs=xs, s=s, v=c_rv, b=c_t: h.activation(
                out=sb["xn_tm"][:, (s % 2) * D:(s % 2 + 1) * D], in_=xs,
                func=AF.Identity, bias=b, scale=v),
                reads=[r("lnstat"), r("xstage")], writes=[r("xn_tm")])
            if s % 2 == 1:
                pair = s // 2
                for dt in range(DT):
                    bn = "pb%d" % (dt % 2)
                    bank = sb[bn]

                    def tmm(h, dt=dt, bank=bank, pair=pair):
                        last = None
                        for q2 in range(2):
                            last = h.matmul(
                                bank[:, (pair * 2 + q2) * 128:
                                     (pair * 2 + q2 + 1) * 128],
                                sb["xn_tm"][:, q2 * D + dt * 128:
                                            q2 * D + (dt + 1) * 128],
                                sb["ident"][:], start=True, stop=True)
                        return last
                    p.op("pe", tmm, reads=[r("xn_tm"), r("ident")],
                         writes=[r(bn)])
                    p.op("dve", lambda h, dt=dt, bank=bank, pair=pair:
                         h.tensor_copy(
                             out=sb["xnT"][:, dt * 512 + pair * 256:
                                           dt * 512 + (pair + 1) * 256],
                             in_=bank[:, pair * 256:(pair + 1) * 256]),
                         reads=[r(bn)], writes=[r("xnT")])
        p.op("pool", lambda h: h.dma_start(
            out=dr["xnT_scr"][:, :, t0:t0 + 512].rearrange("dt p t -> p dt t"),
            in_=sb["xnT"][:].rearrange("p (dt t) -> p dt t", dt=DT)),
            reads=[r("xnT")], dma="xnT_sp")

        for mt in range(4):
            bank = sb["pb2"]

            def qcmm(h, mt=mt, bank=bank):
                last = None
                for kt in range(DT):
                    last = h.matmul(
                        bank[:],
                        sb["w_qc_bf"][:, kt * QC + mt * 128:
                                      kt * QC + (mt + 1) * 128],
                        sb["xnT"][:, kt * 512:(kt + 1) * 512],
                        start=(kt == 0), stop=(kt == DT - 1))
                return last
            p.op("pe", qcmm, reads=[r("w_qc_bf"), r("xnT")], writes=[r("pb2")])
            p.op("dve", lambda h, mt=mt, bank=bank: h.tensor_copy(
                out=sb["q_cT"][:, mt * T + t0: mt * T + t0 + 512], in_=bank[:]),
                reads=[r("pb2")], writes=[r("q_cT")])

        for s in range(4):
            bank = sb["pb3"]
            st_idx = u * 4 + s

            def w1mm(h, s=s, bank=bank):
                last = None
                for kt in range(DT):
                    last = h.matmul(
                        bank[:, :260],
                        sb["xnT"][:, kt * 512 + s * 128:kt * 512 + (s + 1) * 128],
                        sb["W1_bf"][:, kt * 260:(kt + 1) * 260],
                        start=(kt == 0), stop=(kt == DT - 1))
                return last
            p.op("pe", w1mm, reads=[r("W1_bf"), r("xnT")], writes=[r("pb3")])
            p.op("act", lambda h, s=s, bank=bank: h.copy(
                out=sb["kstage"][:, s * 128:(s + 1) * 128], in_=bank[:, 0:128]),
                reads=[r("pb3")], writes=[r("kstage")])
            p.op("act", lambda h, bank=bank, st=st_idx: h.copy(
                out=sb["v_sb"][:, st * 128:(st + 1) * 128], in_=bank[:, 128:256]),
                reads=[r("pb3")], writes=[r("v_sb")])
            p.op("act", lambda h, bank=bank, st=st_idx: h.copy(
                out=sb["wih"][:, st * 4:(st + 1) * 4], in_=bank[:, 256:260]),
                reads=[r("pb3")], writes=[r("wih")])

        for s in range(4):
            st_idx = u * 4 + s
            ks = sb["kstage"][:, s * 128:(s + 1) * 128]
            c_rs = sb["lnstat"][:, s:s + 1]
            p.op("act", lambda h, ks=ks, c=c_rs: h.activation(
                out=sb["xn_tm"][:, 0:128], in_=ks, func=AF.Square, accum_out=c),
                reads=[r("kstage")], writes=[r("xn_tm"), r("lnstat")])
            p.op("act", lambda h, c=c_rs: h.activation(
                out=c, in_=c, func=AF.Ln, bias=cp[:, 0:1], scale=cp[:, 4:5]),
                reads=[r("lnstat"), r("constp")], writes=[r("lnstat")],
                strict=True)
            p.op("act", lambda h, c=c_rs: h.activation(
                out=c, in_=c, func=AF.Exp, bias=0.0, scale=cp[:, 2:3]),
                reads=[r("lnstat"), r("constp")], writes=[r("lnstat")],
                strict=True)
            p.op("act", lambda h, ks=ks, c=c_rs: h.activation(
                out=ks, in_=ks, func=AF.Identity, bias=0.0, scale=c),
                reads=[r("lnstat")], writes=[r("kstage")], strict=True)
            p.op("dve", lambda h, ks=ks: h.tensor_tensor(
                out=ks, in0=ks, in1=sb["wk_rep"][:], op=ALU.mult),
                reads=[r("wk_rep"), r("kstage")], writes=[r("kstage")],
                strict=True)
            co = sb["cos"][:, st_idx * 32:(st_idx + 1) * 32]
            si = sb["sin"][:, st_idx * 32:(st_idx + 1) * 32]
            x1 = ks[:, 0:32]
            x2 = ks[:, 32:64]
            rs_ = sb["rope_scr"]
            p.op("dve", lambda h, x1=x1, co=co: h.tensor_tensor(
                out=rs_[:, 0:32], in0=x1, in1=co, op=ALU.mult),
                reads=[r("kstage"), r("cos")], writes=[r("rope_scr")],
                strict=True)
            p.op("dve", lambda h, x2=x2, co=co: h.tensor_tensor(
                out=rs_[:, 32:64], in0=x2, in1=co, op=ALU.mult),
                reads=[r("kstage"), r("cos")], writes=[r("rope_scr")],
                strict=True)
            p.op("dve", lambda h, x2=x2, si=si: h.scalar_tensor_tensor(
                out=rs_[:, 64:96], in0=x2, scalar=-1.0, in1=si,
                op0=ALU.mult, op1=ALU.mult),
                reads=[r("kstage"), r("sin")], writes=[r("rope_scr")],
                strict=True)
            p.op("dve", lambda h, x1=x1, si=si: h.tensor_tensor(
                out=rs_[:, 96:128], in0=x1, in1=si, op=ALU.mult),
                reads=[r("kstage"), r("sin")], writes=[r("rope_scr")],
                strict=True)
            p.op("dve", lambda h, x1=x1: h.tensor_tensor(
                out=x1, in0=rs_[:, 0:32], in1=rs_[:, 64:96], op=ALU.add),
                reads=[r("rope_scr")], writes=[r("kstage")], strict=True)
            p.op("dve", lambda h, x2=x2: h.tensor_tensor(
                out=x2, in0=rs_[:, 96:128], in1=rs_[:, 32:64], op=ALU.add),
                reads=[r("rope_scr")], writes=[r("kstage")], strict=True)
            p.op("dve", lambda h, ks=ks, s=s: h.tensor_tensor(
                out=sb["khat_tm"][:, s * 128:(s + 1) * 128], in0=ks,
                in1=sb["wqs_rep"][:], op=ALU.mult),
                reads=[r("kstage"), r("wqs_rep")], writes=[r("khat_tm")],
                strict=True)
        bank = sb["pb4"]

        def ktr(h, bank=bank):
            last = None
            for s in range(4):
                last = h.matmul(bank[:, s * 128:(s + 1) * 128],
                                sb["khat_tm"][:, s * 128:(s + 1) * 128],
                                sb["ident"][:], start=True, stop=True)
            return last
        p.op("pe", ktr, reads=[r("khat_tm"), r("ident")], writes=[r("pb4")])
        p.op("dve", lambda h, bank=bank: h.tensor_copy(
            out=sb["khatT"][:, t0:t0 + 512], in_=bank[:]),
            reads=[r("pb4")], writes=[r("khatT")])

        for mt, (c0, cn) in enumerate([(0, 128), (128, 128), (256, 128),
                                       (384, 64)]):
            bn = "pb%d" % (4 + mt % 2)
            bank = sb[bn]

            def w2mm(h, c0=c0, cn=cn, bank=bank):
                last = None
                for kt in range(DT):
                    last = h.matmul(
                        bank[:cn, :512],
                        sb["W2_bf"][:, kt * 448 + c0:kt * 448 + c0 + cn],
                        sb["xnT"][:, kt * 512:(kt + 1) * 512],
                        start=(kt == 0), stop=(kt == DT - 1))
                return last
            p.op("pe", w2mm, reads=[r("W2_bf"), r("xnT")], writes=[r(bn)])
            if mt == 0:
                p.op("act", lambda h, bank=bank: h.copy(
                    out=sb["czkv"][:, 0:512], in_=bank[:, :512]),
                    reads=[r(bn)], writes=[r("czkv")])
            elif mt == 1:
                p.op("dve", lambda h, bank=bank: h.tensor_tensor(
                    out=sb["czkv"][:, 512:1024].rearrange("p (b m) -> p b m",
                                                          m=16),
                    in0=bank[:, :512].rearrange("p (b m) -> p b m", m=16),
                    in1=sb["biasTkv"][:].unsqueeze(1).to_broadcast([128, 32, 16]),
                    op=ALU.add),
                    reads=[r(bn), r("biasTkv")], writes=[r("czkv")])
            elif mt == 2:
                p.op("dve", lambda h, bank=bank: h.tensor_copy(
                    out=sb["czidx"][:, 0:512], in_=bank[0:64, :512]),
                    reads=[r(bn)], writes=[r("czidx")])
                p.op("dve", lambda h, bank=bank: h.tensor_tensor(
                    out=sb["czidx"][:, 512:1024].rearrange("p (b m) -> p b m",
                                                           m=16),
                    in0=bank[64:128, :512].rearrange("p (b m) -> p b m", m=16),
                    in1=sb["biasTidx"][:].unsqueeze(1).to_broadcast([64, 32, 16]),
                    op=ALU.add),
                    reads=[r(bn), r("biasTidx")], writes=[r("czidx")])
            else:
                p.op("dve", lambda h, bank=bank: h.tensor_copy(
                    out=sb["dq_bf"][:], in_=bank[0:64, :512]),
                    reads=[r(bn)], writes=[r("dq_bf")])
                for mt2 in range(2):
                    bank2 = sb["pb3"]
                    p.op("pe", lambda h, mt2=mt2, bank2=bank2: h.matmul(
                        bank2[:, :512],
                        sb["w_iuq"][:, mt2 * 128:(mt2 + 1) * 128],
                        sb["dq_bf"][:], start=True, stop=True),
                        reads=[r("w_iuq"), r("dq_bf")], writes=[r("pb3")])
                    p.op("dve", lambda h, mt2=mt2, bank2=bank2: h.tensor_copy(
                        out=sb["qiT"][:, mt2 * T + t0:mt2 * T + t0 + 512],
                        in_=bank2[:, :512]),
                        reads=[r("pb3")], writes=[r("qiT")])

        def compress(cz_name, parts, dstname):
            cz = sb[cz_name]
            czr = r(cz_name)
            scr = sb["cmp_scr"]
            red = sb["cmp_red"]
            c_ap = cz[:parts, 0:512]
            z_ap = cz[:parts, 512:1024]
            p.op("dve", lambda h: h.reduce_max(
                out=red[:parts, 0:32],
                in_=z_ap.rearrange("p (b m) -> p b m", m=16),
                axis=AX.X, negate=True),
                reads=[czr], writes=[r("cmp_red")])
            p.op("dve", lambda h: h.tensor_tensor(
                out=scr[:parts, 0:512].rearrange("p (b m) -> p b m", m=16),
                in0=z_ap.rearrange("p (b m) -> p b m", m=16),
                in1=red[:parts, 0:32].unsqueeze(2).to_broadcast([parts, 32, 16]),
                op=ALU.add),
                reads=[czr, r("cmp_red")], writes=[r("cmp_scr")], strict=True)
            p.op("act", lambda h: h.activation(
                out=scr[:parts, 0:512], in_=scr[:parts, 0:512], func=AF.Exp),
                reads=[r("cmp_scr")], writes=[r("cmp_scr")])
            p.op("dve", lambda h: h.reduce_sum(
                out=red[:parts, 32:64],
                in_=scr[:parts, 0:512].rearrange("p (b m) -> p b m", m=16),
                axis=AX.X),
                reads=[r("cmp_scr")], writes=[r("cmp_red")])
            p.op("dve", lambda h: h.tensor_tensor(
                out=scr[:parts, 0:512], in0=scr[:parts, 0:512], in1=c_ap,
                op=ALU.mult),
                reads=[r("cmp_scr"), czr], writes=[r("cmp_scr")], strict=True)
            p.op("dve", lambda h: h.reduce_sum(
                out=red[:parts, 0:32],
                in_=scr[:parts, 0:512].rearrange("p (b m) -> p b m", m=16),
                axis=AX.X),
                reads=[r("cmp_scr")], writes=[r("cmp_red")], strict=True)
            p.op("dve", lambda h: h.reciprocal(
                out=red[:parts, 32:64], in_=red[:parts, 32:64]),
                reads=[r("cmp_red")], writes=[r("cmp_red")], strict=True)
            p.op("dve", lambda h: h.tensor_tensor(
                out=sb[dstname][:parts, u * 32:(u + 1) * 32],
                in0=red[:parts, 0:32], in1=red[:parts, 32:64], op=ALU.mult),
                reads=[r("cmp_red")], writes=[r(dstname)], strict=True)
        compress("czkv", 128, "ckvT")
        compress("czidx", 64, "kcompT")

    def p1_post(self):
        p, sb, r = self.p, self.sb, self.r
        cp = sb["constp"]
        scr = sb["cmp_scr"]
        p.op("dve", lambda h: h.tensor_tensor(
            out=scr[:, 0:256], in0=sb["ckvT"][:], in1=sb["ckvT"][:],
            op=ALU.mult),
            reads=[r("ckvT")], writes=[r("cmp_scr")])
        bank = sb["pb0"]
        p.op("pe", lambda h: h.matmul(bank[:1, 0:256], sb["ones_col"][:],
                                      scr[:, 0:256], start=True, stop=True),
             reads=[r("ones_col"), r("cmp_scr")], writes=[r("pb0")])
        p.op("act", lambda h: h.activation(
            out=scr[:1, 256:512], in_=bank[:1, 0:256], func=AF.Ln,
            bias=cp[:1, 0:1], scale=cp[:1, 4:5]),
            reads=[r("pb0"), r("constp")], writes=[r("cmp_scr")])
        p.op("act", lambda h: h.activation(
            out=scr[:1, 256:512], in_=scr[:1, 256:512], func=AF.Exp,
            bias=0.0, scale=cp[:1, 2:3]),
            reads=[r("cmp_scr"), r("constp")], writes=[r("cmp_scr")],
            strict=True)
        bank1 = sb["pb1"]
        p.op("pe", lambda h: h.matmul(bank1[:, 0:256], sb["ones_row"][:],
                                      scr[:1, 256:512], start=True, stop=True),
             reads=[r("ones_row"), r("cmp_scr")], writes=[r("pb1")])
        p.op("dve", lambda h: h.tensor_tensor(
            out=scr[:, 0:256], in0=sb["ckvT"][:], in1=bank1[:, 0:256],
            op=ALU.mult),
            reads=[r("ckvT"), r("pb1")], writes=[r("cmp_scr")])
        p.op("dve", lambda h: h.tensor_scalar(
            out=sb["skv_ss"][:], in0=scr[:, 0:256],
            scalar1=sb["gains_col"][:, 0:1], scalar2=None, op0=ALU.mult),
            reads=[r("cmp_scr"), r("gains_col")], writes=[r("skv_ss")])
        p.op("dve", lambda h: h.tensor_copy(
            out=sb["skv_stg"][:], in_=scr[:, 0:256]),
            reads=[r("cmp_scr")], writes=[r("skv_stg")])
        bank2 = sb["pb2"]

        def tr2(h, bank2=bank2):
            last = None
            for i in range(2):
                last = h.matmul(
                    bank2[:, i * 128:(i + 1) * 128],
                    sb["skv_stg"][:, i * 128:(i + 1) * 128],
                    sb["ident"][:], start=True, stop=True)
            return last
        p.op("pe", tr2, reads=[r("skv_stg"), r("ident")], writes=[r("pb2")])
        p.op("dve", lambda h, bank2=bank2: h.tensor_copy(
            out=sb["skv_pv"][:], in_=bank2[:, 0:256]),
            reads=[r("pb2")], writes=[r("skv_pv")])
        p.op("dve", lambda h: h.tensor_tensor(
            out=sb["skv_pv"][:].rearrange("p (i c) -> p i c", i=2),
            in0=sb["skv_pv"][:].rearrange("p (i c) -> p i c", i=2),
            in1=sb["wk_rep"][:].unsqueeze(1).to_broadcast([128, 2, 128]),
            op=ALU.mult),
            reads=[r("wk_rep")], writes=[r("skv_pv")], strict=True)
        p.op("dve", lambda h: h.tensor_copy(out=sb["kcomp_bf"][0:64, :],
                                            in_=sb["kcompT"][:]),
             reads=[r("kcompT")], writes=[r("kcomp_bf")])
        p.op("pool", lambda h: h.dma_start(out=sb["kcomp_bf"][64:128, :],
                                           in_=sb["kcomp_bf"][0:64, :]),
             reads=[r("kcomp_bf")], writes=[r("kcomp_bf")], dma="kcompdup")

    def p2_alloc(self, ctx):
        p, dr = self.p, self.dram
        sb = self.sbuf
        sb("causal", (128, NT * NBLK), BF16, ctx)
        sb("fw_f32", (128, D), F32, ctx)
        sb("fw_bf", (128, D), BF16, ctx)
        sb("iscc", (128, NBLK), F32, ctx)
        sb("isc2", (128, NBLK), F32, ctx)
        sb("m8", (128, 8), F32, ctx)
        p.op("sp", lambda h: h.dma_start(out=self.sb["causal"][:],
                                         in_=dr["causal"][:, :]),
             writes=[self.r("causal")], dma="causal")

    def p2_fwconv(self, kt):
        p, sb, dr, r = self.p, self.sb, self.dram, self.r
        p.op("sp", lambda h, kt=kt: h.dma_start(
            out=sb["fw_f32"][:], in_=dr["final_w"][kt * 128:(kt + 1) * 128, :]),
            writes=[r("fw_f32")], dma="fw_f32")
        p.op("dve", lambda h: h.tensor_copy(out=sb["fw_bf"][:],
                                            in_=sb["fw_f32"][:]),
             reads=[r("fw_f32")], writes=[r("fw_bf")])
        p.op("pool", lambda h, kt=kt: h.dma_start(
            out=self.dram["fwT_scr"][kt, :, :], in_=sb["fw_bf"][:]),
            reads=[r("fw_bf")], dma="fwbf_st")

    def p2(self, st):
        p, sb, r = self.p, self.sb, self.r
        t0 = st * 128
        cp = sb["constp"]
        for ih in range(NIH):
            bn = "pb%d" % (ih % 2)
            bank = sb[bn]
            ptile = ih // 2
            row0 = (ih % 2) * 64
            p.op("pe", lambda h, bank=bank, ptile=ptile, row0=row0: h.matmul(
                bank[:, 0:NBLK],
                sb["qiT"][row0:row0 + 64, ptile * T + t0:ptile * T + t0 + 128],
                sb["kcomp_bf"][row0:row0 + 64, :], start=True, stop=True),
                reads=[r("qiT"), r("kcomp_bf")], writes=[r(bn)])
            wcol = sb["wih"][:, st * 4 + ih:st * 4 + ih + 1]
            if ih == 0:
                p.op("dve", lambda h, bank=bank, wcol=wcol: h.scalar_tensor_tensor(
                    out=sb["iscc"][:], in0=bank[:, 0:NBLK], scalar=0.0,
                    in1=wcol.to_broadcast([128, NBLK]), op0=ALU.max, op1=ALU.mult),
                    reads=[r(bn), r("wih")], writes=[r("iscc")])
            else:
                p.op("dve", lambda h, bank=bank, wcol=wcol: h.scalar_tensor_tensor(
                    out=sb["isc2"][:], in0=bank[:, 0:NBLK], scalar=0.0,
                    in1=wcol.to_broadcast([128, NBLK]), op0=ALU.max, op1=ALU.mult),
                    reads=[r(bn), r("wih")], writes=[r("isc2")])
                p.op("dve", lambda h: h.tensor_tensor(
                    out=sb["iscc"][:], in0=sb["iscc"][:], in1=sb["isc2"][:],
                    op=ALU.add),
                    reads=[r("isc2")], writes=[r("iscc")], strict=True)
        p.op("dve", lambda h, st=st: h.tensor_tensor(
            out=sb["iscc"][:], in0=sb["iscc"][:],
            in1=sb["causal"][:, st * NBLK:(st + 1) * NBLK], op=ALU.add),
            reads=[r("causal")], writes=[r("iscc")], strict=True)
        for rnd in range(8):
            p.op("dve", lambda h: h.max(out=sb["m8"][:], in_=sb["iscc"][:]),
                 reads=[r("iscc")], writes=[r("m8")], strict=True)
            p.op("dve", lambda h: h.match_replace(
                out=sb["iscc"][:], in_to_replace=sb["m8"][:],
                in_values=sb["iscc"][:], imm_value=SENT),
                reads=[r("m8")], writes=[r("iscc")], strict=True)
        p.op("dve", lambda h: h.tensor_scalar(
            out=sb["isc2"][:], in0=sb["iscc"][:], scalar1=SENT, scalar2=None,
            op0=ALU.is_equal),
            reads=[r("iscc")], writes=[r("isc2")], strict=True)
        p.op("act", lambda h: h.activation(
            out=sb["isc2"][:], in_=sb["isc2"][:], func=AF.Identity,
            bias=cp[:, 7:8], scale=cp[:, 6:7]),
            reads=[r("isc2"), r("constp")], writes=[r("isc2")])
        p.op("dve", lambda h, st=st: h.tensor_tensor(
            out=sb["mask_sb"][:, st * NBLK:(st + 1) * NBLK], in0=sb["isc2"][:],
            in1=sb["causal"][:, st * NBLK:(st + 1) * NBLK], op=ALU.add),
            reads=[r("isc2"), r("causal")], writes=[r("mask_sb")])

    def p3_alloc(self, ctx):
        p, dr = self.p, self.dram
        sb = self.sbuf
        sb("winmask", (128, 4 * 512), BF16, ctx)
        sb("w_qup_bf", (128, 4 * D), BF16, ctx)
        sb("groupw_bf", (128, 16 * 512), BF16, ctx)
        sb("wstage3", (128, 512), F32, ctx)
        for s in range(2):
            sb("qpre%d" % s, (128, 512), F32, ctx)
            sb("qrt%d" % s, (128, 512), F32, ctx)
            sb("qhat_tm%d" % s, (128, 512), BF16, ctx)
            sb("qstat%d" % s, (128, 4), F32, ctx)
        for s in range(4):
            sb("ssb%d" % s, (128, 256), F32, ctx)
            sb("wsb%d" % s, (128, 512), F32, ctx)
            sb("nd%d" % s, (128, 4), F32, ctx)
            sb("pbf%d" % s, (128, 256), BF16, ctx)
            sb("pwb%d" % s, (128, 512), BF16, ctx)
            sb("ptb%d" % s, (128, 256), BF16, ctx)
            sb("ptw%d" % s, (128, 512), BF16, ctx)
        sb("qhatT", (128, 16 * 512), BF16, ctx)
        sb("attn_bf", (128, 16 * 512), BF16, ctx)
        sb("sstage", (128, D), BF16, ctx)
        sb("fwstage", (128, 2 * D), BF16, ctx)
        self.res["fwstage0"] = Res()
        self.res["fwstage1"] = Res()
        p.op("sp", lambda h: h.dma_start(out=self.sb["winmask"][:],
                                         in_=dr["winmask"][:, :]),
             writes=[self.r("winmask")], dma="winmask")

        def conv(dstname, col, src_ap, ncols):
            st = self.sb["wstage3"]
            p.op("sp", lambda h, s=src_ap, n=ncols: h.dma_start(
                out=st[:, :n], in_=s), writes=[self.r("wstage3")], dma="wstage3")
            p.op("dve", lambda h, d=self.sb[dstname], c=col, n=ncols:
                 h.tensor_copy(out=d[:, c:c + n], in_=st[:, :n]),
                 reads=[self.r("wstage3")], writes=[self.r(dstname)])
        for i in range(4):
            for c4 in range(4):
                conv("w_qup_bf", i * D + c4 * 512,
                     dr["w_qup"][i * 128:(i + 1) * 128, c4 * 512:(c4 + 1) * 512],
                     512)
        for gi in range(4):
            for isub in range(4):
                conv("groupw_bf", (gi * 4 + isub) * 512,
                     dr["group_w"][gi, isub * 128:(isub + 1) * 128, :], 512)

    def p3(self, u):
        p, sb, dr, r = self.p, self.sb, self.dram, self.r
        t0 = u * 512
        cp = sb["constp"]

        for it in range(16) if "qhat" not in DISABLE else []:
            tsub, hg = it // 4, it % 4
            gt0 = t0 + tsub * 128
            par = it % 2
            qpre = sb["qpre%d" % par]
            qrt = sb["qrt%d" % par]
            qstat = sb["qstat%d" % par]
            qhat_tm = sb["qhat_tm%d" % par]
            rpre, rrt, rst, rtm = (r("qpre%d" % par), r("qrt%d" % par),
                                   r("qstat%d" % par), r("qhat_tm%d" % par))
            bn0 = "pb%d" % (par * 2)
            bn1 = "pb%d" % (par * 2 + 1)
            bank = sb[bn0]

            def qmm(h, gt0=gt0, hg=hg, bank=bank):
                last = None
                for mt in range(4):
                    last = h.matmul(
                        bank[:],
                        sb["q_cT"][:, mt * T + gt0:mt * T + gt0 + 128],
                        sb["w_qup_bf"][:, mt * D + hg * 512:
                                       mt * D + (hg + 1) * 512],
                        start=(mt == 0), stop=(mt == 3))
                return last
            p.op("pe", qmm, reads=[r("q_cT"), r("w_qup_bf")], writes=[r(bn0)])
            p.op("act", lambda h, bank=bank, qpre=qpre: h.copy(
                out=qpre[:], in_=bank[:]),
                reads=[r(bn0)], writes=[rpre])
            st_idx = gt0 // 128
            co = sb["cos"][:, st_idx * 32:(st_idx + 1) * 32]
            si = sb["sin"][:, st_idx * 32:(st_idx + 1) * 32]
            cob = co.unsqueeze(1).to_broadcast([128, 4, 32])
            sib = si.unsqueeze(1).to_broadcast([128, 4, 32])
            x1 = qpre[:].rearrange("p (h c) -> p h c", h=4)[:, :, 0:32]
            x2 = qpre[:].rearrange("p (h c) -> p h c", h=4)[:, :, 32:64]
            a_ = qrt[:, 0:128].rearrange("p (h c) -> p h c", h=4)
            b_ = qrt[:, 128:256].rearrange("p (h c) -> p h c", h=4)
            c_ = qrt[:, 256:384].rearrange("p (h c) -> p h c", h=4)
            d_ = qrt[:, 384:512].rearrange("p (h c) -> p h c", h=4)
            for dst, s0, s1 in ((a_, x1, cob), (b_, x2, sib),
                                (c_, x1, sib), (d_, x2, cob)):
                p.op("dve", lambda h, dst=dst, s0=s0, s1=s1: h.tensor_tensor(
                    out=dst, in0=s0, in1=s1, op=ALU.mult),
                    reads=[rpre, r("cos"), r("sin")], writes=[rrt])
            p.op("dve", lambda h, a_=a_, b_=b_, x1=x1: h.tensor_tensor(
                out=x1, in0=a_, in1=b_, op=ALU.subtract),
                reads=[rrt], writes=[rpre])
            p.op("dve", lambda h, c_=c_, d_=d_, x2=x2: h.tensor_tensor(
                out=x2, in0=c_, in1=d_, op=ALU.add),
                reads=[rrt], writes=[rpre])
            for hh4 in range(4):
                p.op("act", lambda h, hh4=hh4, qpre=qpre, qrt=qrt, qstat=qstat:
                     h.activation(
                         out=qrt[:, 0:128],
                         in_=qpre[:, hh4 * 128:(hh4 + 1) * 128],
                         func=AF.Square, accum_out=qstat[:, hh4:hh4 + 1]),
                     reads=[rpre], writes=[rrt, rst])
            p.op("act", lambda h, qstat=qstat: h.activation(
                out=qstat[:], in_=qstat[:], func=AF.Ln,
                bias=cp[:, 0:1], scale=cp[:, 4:5]),
                reads=[rst, r("constp")], writes=[rst])
            p.op("act", lambda h, qstat=qstat: h.activation(
                out=qstat[:], in_=qstat[:], func=AF.Exp,
                bias=0.0, scale=cp[:, 2:3]),
                reads=[rst, r("constp")], writes=[rst])
            p.op("dve", lambda h, qpre=qpre, qstat=qstat, qhat_tm=qhat_tm:
                 h.tensor_tensor(
                     out=qhat_tm[:],
                     in0=qpre[:].rearrange("p (h c) -> p h c", h=4),
                     in1=qstat[:].unsqueeze(2).to_broadcast([128, 4, 128]),
                     op=ALU.mult),
                 reads=[rst, rpre], writes=[rtm])
            bank1 = sb[bn1]

            def qtr(h, bank1=bank1, qhat_tm=qhat_tm):
                last = None
                for hh4 in range(4):
                    last = h.matmul(
                        bank1[:, hh4 * 128:(hh4 + 1) * 128],
                        qhat_tm[:, hh4 * 128:(hh4 + 1) * 128],
                        sb["ident"][:], start=True, stop=True)
                return last
            p.op("pe", qtr, reads=[rtm, r("ident")], writes=[r(bn1)])
            p.op("dve", lambda h, bank1=bank1, hg=hg, tsub=tsub: h.tensor_copy(
                out=sb["qhatT"][:].rearrange(
                    "p (hd t) -> p hd t", hd=16)[:, hg * 4:hg * 4 + 4,
                                                 tsub * 128:(tsub + 1) * 128],
                in_=bank1[:].rearrange("p (hh t) -> p hh t", hh=4)),
                reads=[r(bn1)], writes=[r("qhatT")])

        for tsub in range(4) if "attn" not in DISABLE else []:
            st_idx = (t0 + tsub * 128) // 128
            nblk = st_idx // 2
            parity = st_idx % 2
            sel = parity + (2 if nblk == 0 else 0)
            for wave in range(4):
                ctxs = []
                for i in range(4):
                    hh = wave * 4 + i
                    s = i
                    ctxs.append(dict(
                        hh=hh,
                        qsl=sb["qhatT"][:, hh * 512 + tsub * 128:
                                        hh * 512 + (tsub + 1) * 128],
                        bX="pb%d" % s, bY="pb%d" % (4 + s),
                        ssb=sb["ssb%d" % s], wsb=sb["wsb%d" % s],
                        nd=sb["nd%d" % s], pbf=sb["pbf%d" % s],
                        pwb=sb["pwb%d" % s], ptb=sb["ptb%d" % s],
                        ptw=sb["ptw%d" % s],
                        rs=r("ssb%d" % s), rw=r("wsb%d" % s),
                        rnd=r("nd%d" % s), rpb=r("pbf%d" % s),
                        rpw=r("pwb%d" % s), rtb=r("ptb%d" % s),
                        rtw=r("ptw%d" % s)))
                for c in ctxs:
                    p.op("pe", lambda h, c=c: h.matmul(
                        sb[c["bX"]][:, 0:NBLK], c["qsl"], sb["skv_ss"][:, :],
                        start=True, stop=True),
                        reads=[r("qhatT"), r("skv_ss")], writes=[r(c["bX"])])
                for c in ctxs:
                    if nblk == 0:
                        def ws0(h, c=c):
                            h.matmul(sb[c["bY"]][:, 0:256], c["qsl"],
                                     sb["khatT"][:, 0:256],
                                     start=True, stop=True)
                            return h.matmul(sb[c["bY"]][:, 256:512], c["qsl"],
                                            sb["khatT"][:, 0:256],
                                            start=True, stop=True)
                        p.op("pe", ws0, reads=[r("qhatT"), r("khatT")],
                             writes=[r(c["bY"])])
                    else:
                        p.op("pe", lambda h, c=c, nblk=nblk: h.matmul(
                            sb[c["bY"]][:], c["qsl"],
                            sb["khatT"][:, (nblk - 1) * 256:(nblk + 1) * 256],
                            start=True, stop=True),
                            reads=[r("qhatT"), r("khatT")], writes=[r(c["bY"])])
                for c in ctxs:
                    p.op("dve", lambda h, c=c, st_idx=st_idx: h.tensor_tensor(
                        out=c["ssb"][:], in0=sb[c["bX"]][:, 0:NBLK],
                        in1=sb["mask_sb"][:, st_idx * NBLK:(st_idx + 1) * NBLK],
                        op=ALU.add),
                        reads=[r(c["bX"]), r("mask_sb")], writes=[c["rs"]])
                for c in ctxs:
                    p.op("dve", lambda h, c=c: h.reduce_max(
                        out=c["nd"][:, 0:1], in_=c["ssb"][:], axis=AX.X,
                        negate=True),
                        reads=[c["rs"]], writes=[c["rnd"]])
                for c in ctxs:
                    p.op("act", lambda h, c=c: h.activation(
                        out=c["ssb"][:], in_=c["ssb"][:], func=AF.Exp,
                        bias=c["nd"][:, 0:1], scale=1.0,
                        accum_out=c["nd"][:, 1:2]),
                        reads=[c["rs"], c["rnd"]], writes=[c["rs"], c["rnd"]])
                for c in ctxs:
                    p.op("dve", lambda h, c=c, sel=sel: h.tensor_tensor(
                        out=c["wsb"][:], in0=sb[c["bY"]][:],
                        in1=sb["winmask"][:, sel * 512:(sel + 1) * 512],
                        op=ALU.add),
                        reads=[r(c["bY"]), r("winmask")], writes=[c["rw"]])
                for c in ctxs:
                    p.op("dve", lambda h, c=c: h.reduce_max(
                        out=c["nd"][:, 2:3], in_=c["wsb"][:], axis=AX.X,
                        negate=True),
                        reads=[c["rw"]], writes=[c["rnd"]])
                for c in ctxs:
                    p.op("act", lambda h, c=c: h.activation(
                        out=c["wsb"][:], in_=c["wsb"][:], func=AF.Exp,
                        bias=c["nd"][:, 2:3], scale=1.0,
                        accum_out=c["nd"][:, 3:4]),
                        reads=[c["rw"], c["rnd"]], writes=[c["rw"], c["rnd"]])
                for c in ctxs:
                    p.op("dve", lambda h, c=c: h.reciprocal(
                        out=c["nd"][:, 1:2], in_=c["nd"][:, 1:2]),
                        reads=[c["rnd"]], writes=[c["rnd"]])
                    p.op("dve", lambda h, c=c: h.tensor_scalar(
                        out=c["pbf"][:], in0=c["ssb"][:],
                        scalar1=c["nd"][:, 1:2], scalar2=None, op0=ALU.mult),
                        reads=[c["rs"], c["rnd"]], writes=[c["rpb"]])
                    if st_idx == 0:
                        p.op("dve", lambda h, c=c: h.memset(
                            c["pbf"][0:16, :], 0.0),
                            writes=[c["rpb"]], strict=True)
                for c in ctxs:
                    def ptr(h, c=c):
                        last = None
                        for i in range(2):
                            last = h.matmul(
                                sb[c["bX"]][:, 256 + i * 128:256 + (i + 1) * 128],
                                c["pbf"][:, i * 128:(i + 1) * 128],
                                sb["ident"][:], start=True, stop=True)
                        return last
                    p.op("pe", ptr, reads=[c["rpb"], r("ident")],
                         writes=[r(c["bX"])])
                for c in ctxs:
                    p.op("act", lambda h, c=c: h.copy(
                        out=c["ptb"][:], in_=sb[c["bX"]][:, 256:512]),
                        reads=[r(c["bX"])], writes=[c["rtb"]])
                for c in ctxs:
                    p.op("dve", lambda h, c=c: h.reciprocal(
                        out=c["nd"][:, 3:4], in_=c["nd"][:, 3:4]),
                        reads=[c["rnd"]], writes=[c["rnd"]])
                    p.op("dve", lambda h, c=c: h.tensor_scalar(
                        out=c["pwb"][:], in0=c["wsb"][:],
                        scalar1=c["nd"][:, 3:4], scalar2=None, op0=ALU.mult),
                        reads=[c["rw"], c["rnd"]], writes=[c["rpw"]])
                for c in ctxs:
                    def pvs(h, c=c):
                        last = None
                        for i in range(2):
                            last = h.matmul(sb[c["bX"]][:, 256:384],
                                            sb["skv_pv"][:, i * 128:(i + 1) * 128],
                                            c["ptb"][:, i * 128:(i + 1) * 128],
                                            start=(i == 0), stop=False)
                        return last
                    p.op("pe", pvs, reads=[r("skv_pv"), c["rtb"]],
                         writes=[r(c["bX"])])
                for c in ctxs:
                    def wtr(h, c=c):
                        last = None
                        for ks in range(4):
                            last = h.matmul(sb[c["bY"]][:, ks * 128:(ks + 1) * 128],
                                            c["pwb"][:, ks * 128:(ks + 1) * 128],
                                            sb["ident"][:], start=True, stop=True)
                        return last
                    p.op("pe", wtr, reads=[c["rpw"], r("ident")],
                         writes=[r(c["bY"])])
                for c in ctxs:
                    p.op("act", lambda h, c=c: h.copy(
                        out=c["ptw"][:], in_=sb[c["bY"]][:]),
                        reads=[r(c["bY"])], writes=[c["rtw"]])
                for c in ctxs:
                    ks_lo = 2 if nblk == 0 else 0

                    def pvw(h, c=c, ks_lo=ks_lo, nblk=nblk):
                        last = None
                        for ks in range(ks_lo, 4):
                            ktile = (nblk - 1) * 2 + ks
                            last = h.matmul(sb[c["bX"]][:, 256:384],
                                            sb["v_sb"][:, ktile * 128:(ktile + 1) * 128],
                                            c["ptw"][:, ks * 128:(ks + 1) * 128],
                                            start=False, stop=(ks == 3))
                        return last
                    p.op("pe", pvw, reads=[r("v_sb"), c["rtw"]],
                         writes=[r(c["bX"])])
                for c in ctxs:
                    p.op("act", lambda h, c=c, tsub=tsub: h.copy(
                        out=sb["attn_bf"][:, c["hh"] * 512 + tsub * 128:
                                          c["hh"] * 512 + (tsub + 1) * 128],
                        in_=sb[c["bX"]][:, 256:384]),
                        reads=[r(c["bX"])], writes=[r("attn_bf")])

        for g in range(4) if "group" not in DISABLE else []:
            for osub in range(4):
                bn = "pb%d" % osub
                bank = sb[bn]

                def gmm(h, g=g, osub=osub, bank=bank):
                    last = None
                    for isub in range(4):
                        last = h.matmul(
                            bank[:],
                            sb["groupw_bf"][:, (g * 4 + isub) * 512 + osub * 128:
                                            (g * 4 + isub) * 512 + (osub + 1) * 128],
                            sb["attn_bf"][:, (g * 4 + isub) * 512:
                                          (g * 4 + isub + 1) * 512],
                            start=(isub == 0), stop=(isub == 3))
                    return last
                p.op("pe", gmm, reads=[r("groupw_bf"), r("attn_bf")],
                     writes=[r(bn)])
            for osub in range(4):
                bn = "pb%d" % osub
                bank = sb[bn]
                p.op("dve", lambda h, g=g, osub=osub, bank=bank: h.tensor_copy(
                    out=sb["attn_bf"][:, (g * 4 + osub) * 512:
                                      (g * 4 + osub + 1) * 512], in_=bank[:]),
                    reads=[r(bn)], writes=[r("attn_bf")])

        for pair in range(2) if "final" not in DISABLE else []:
            for kt in range(DT):
                slot = kt % 2
                fw = sb["fwstage"][:, slot * D:(slot + 1) * D]
                p.op("sp", lambda h, fw=fw, kt=kt: h.dma_start(
                    out=fw, in_=dr["fwT_scr"][kt, :, :]),
                    writes=[r("fwstage%d" % slot)], dma="fwstage%d" % slot)
                for ts2 in range(2):
                    tsub = pair * 2 + ts2
                    bn = "pb%d" % (ts2 * 4)
                    for oc in range(4):
                        bno = "pb%d" % (ts2 * 4 + oc)
                        p.op("pe", lambda h, kt=kt, tsub=tsub, oc=oc, fw=fw,
                             bno=bno: h.matmul(
                                 sb[bno][:],
                                 sb["attn_bf"][:, kt * 512 + tsub * 128:
                                               kt * 512 + (tsub + 1) * 128],
                                 fw[:, oc * 512:(oc + 1) * 512],
                                 start=(kt == 0), stop=(kt == DT - 1)),
                             reads=[r("attn_bf"), r("fwstage%d" % (kt % 2))],
                             writes=[r(bno)])
            for ts2 in range(2):
                tsub = pair * 2 + ts2
                st_idx = (t0 + tsub * 128) // 128
                for oc in range(4):
                    bno = "pb%d" % (ts2 * 4 + oc)
                    p.op("dve", lambda h, oc=oc, bno=bno: h.tensor_copy(
                        out=sb["sstage"][:, oc * 512:(oc + 1) * 512],
                        in_=sb[bno][:]),
                        reads=[r(bno)], writes=[r("sstage")])
                p.op("pool", lambda h, st=st_idx: h.dma_start(
                    out=dr["s_scr"][st, :, :], in_=sb["sstage"][:]),
                    reads=[r("sstage")], dma="s_sp")

    def p05_alloc(self, ctx):
        sb = self.sbuf
        sb("wq_i32", (128, 2 * 2048), I32, ctx)
        sb("wq_bf", (128, 2 * 2048), BF16, ctx)
        sb("tev", (128, 2 * 1024), BF16, ctx)
        for nm in ("wq_i32", "wq_bf", "tev"):
            self.res[nm + "_1"] = Res()
        self._units = ([("g", j, 0) for j in range(HT)]
                       + [("u", j, 0) for j in range(HT)]
                       + [("d", ot, part) for ot in range(DT) for part in range(3)])
        self._ui = 0

    def p05_chunk(self, n_units):
        p, sb, dr, r = self.p, self.sb, self.dram, self.r
        for _ in range(n_units):
            if self._ui >= len(self._units):
                return
            kind, j, part = self._units[self._ui]
            usl = self._ui % 2
            self._ui += 1
            i32 = sb["wq_i32"][:, usl * 2048:(usl + 1) * 2048]
            wbf = sb["wq_bf"][:, usl * 2048:(usl + 1) * 2048]
            tev = sb["tev"][:, usl * 1024:(usl + 1) * 1024]
            ri32 = r("wq_i32") if usl == 0 else r("wq_i32_1")
            rbf = r("wq_bf") if usl == 0 else r("wq_bf_1")
            rtv = r("tev") if usl == 0 else r("tev_1")
            if kind == "g" or kind == "u":
                srcd = dr["wq_gate"] if kind == "g" else dr["wq_up"]
                srcap = srcd[j * 128:(j + 1) * 128, :]
                ncol = 2048
            else:
                ncol = 2048 if part < 2 else 1536
                srcap = dr["wq_down"][j * 128:(j + 1) * 128,
                                      part * 2048:part * 2048 + ncol]
            p.op("sp", lambda h, s=srcap, i32=i32, ncol=ncol: h.dma_start(
                out=i32[:, :ncol], in_=s), writes=[ri32], dma="wq_ld%d" % usl)
            p.op("dve", lambda h, i32=i32, wbf=wbf, ncol=ncol: h.tensor_scalar(
                out=wbf[:, :ncol], in0=i32[:, :ncol], scalar1=-128.0,
                scalar2=None, op0=ALU.add),
                reads=[ri32], writes=[rbf])
            ngrp = ncol // 512
            for grp in range(ngrp):
                bn = "pb%d" % (6 + grp % 2)
                bank = sb[bn]

                def tmm(h, wbf=wbf, grp=grp, bank=bank):
                    last = None
                    for s4 in range(4):
                        dt = grp * 4 + s4
                        last = h.matmul(bank[:, s4 * 128:(s4 + 1) * 128],
                                        wbf[:, dt * 128:(dt + 1) * 128],
                                        sb["ident"][:], start=True, stop=True)
                    return last
                p.op("pe", tmm, reads=[rbf, r("ident")], writes=[r(bn)])
                half = grp % 2
                tv = tev[:, half * 512:(half + 1) * 512]
                if kind == "d":
                    sdsl = sb["sdown"][:, j * 128:(j + 1) * 128]
                    p.op("dve", lambda h, tv=tv, bank=bank, sdsl=sdsl:
                         h.tensor_tensor(
                             out=tv.rearrange("p (s o) -> p s o", s=4),
                             in0=bank[:].rearrange("p (s o) -> p s o", s=4),
                             in1=sdsl.unsqueeze(1).to_broadcast([128, 4, 128]),
                             op=ALU.mult),
                         reads=[r(bn), r("sdown")], writes=[rtv])
                else:
                    p.op("act", lambda h, tv=tv, bank=bank: h.copy(
                        out=tv, in_=bank[:]),
                        reads=[r(bn)], writes=[rtv])
                if half == 1 or grp == ngrp - 1:
                    nh = half + 1
                    g0 = grp - half
                    if kind == "g" or kind == "u":
                        dst = dr["wgT_scr" if kind == "g" else "wuT_scr"]
                        p.op("pool", lambda h, tev=tev, dst=dst, j=j, g0=g0,
                             nh=nh: h.dma_start(
                                 out=dst[j, :, g0 * 512:(g0 + nh) * 512],
                                 in_=tev[:, :nh * 512]),
                             reads=[rtv], dma="tev_st%d" % usl)
                    else:
                        base_ht = part * 16 + g0 * 4
                        p.op("pool", lambda h, tev=tev, j=j, b=base_ht, nh=nh:
                             h.dma_start(
                                 out=dr["wdT_scr"][b:b + nh * 4, :,
                                                   j * 128:(j + 1) * 128].rearrange(
                                     "ht p o -> p ht o"),
                                 in_=tev[:, :nh * 512].rearrange(
                                     "p (ht o) -> p ht o", ht=nh * 4)),
                             reads=[rtv], dma="tev_st%d" % usl)

    def p4_alloc(self, ctx):
        p, dr = self.p, self.dram
        sb = self.sbuf
        sb("sgate", (128, HT), F32, ctx)
        sb("sup", (128, HT), F32, ctx)
        sb("xnTs", (128, DT * 1024), BF16, ctx)
        for s in range(3):
            sb("wg_t%d" % s, (128, DT * 128), BF16, ctx)
            sb("wu_t%d" % s, (128, DT * 128), BF16, ctx)
            sb("wd_t%d" % s, (128, D), BF16, ctx)
        sb("hact", (128, HT * 1024), BF16, ctx)
        sb("silu_s", (128, 2 * 512), BF16, ctx)
        self.res["silu_s1"] = Res()
        sb("xres", (128, D), F32, ctx)
        sb("sres", (128, D), BF16, ctx)
        sb("gbuf", (128, D), BF16, ctx)
        sb("obuf", (128, D), F32, ctx)
        for nm in ("sgate", "sup"):
            p.op("sp", lambda h, d=self.sb[nm], s=dr[nm]: h.dma_start(
                out=d[:], in_=s[:, :]), writes=[self.r(nm)], dma=nm)

    def p4(self, v):
        p, sb, dr, r = self.p, self.sb, self.dram, self.r
        t0 = v * 1024
        p.op("pool", lambda h: h.dma_start(
            out=sb["xnTs"][:].rearrange("p (dt t) -> p dt t", dt=DT),
            in_=dr["xnT_scr"][:, :, t0:t0 + 1024].rearrange("dt p t -> p dt t")),
            writes=[r("xnTs")], dma="xnTs")

        for j in range(HT):
            sl = j % 3
            wg = sb["wg_t%d" % sl]
            wu = sb["wu_t%d" % sl]
            p.op("sp", lambda h, wg=wg, j=j: h.dma_start(
                out=wg[:], in_=dr["wgT_scr"][j, :, :]),
                writes=[r("wg_t%d" % sl)], dma="wg_t%d" % sl)
            p.op("sp", lambda h, wu=wu, j=j: h.dma_start(
                out=wu[:], in_=dr["wuT_scr"][j, :, :]),
                writes=[r("wu_t%d" % sl)], dma="wu_t%d" % sl)
            bg = "pb%d" % (2 * (j % 2))
            bu = "pb%d" % (2 * (j % 2) + 1)
            silu_res = "silu_s" if j % 2 == 0 else "silu_s1"
            ssl = sb["silu_s"][:, (j % 2) * 512:(j % 2 + 1) * 512]
            for c in range(2):
                for which, wt, bn in (("g", wg, bg), ("u", wu, bu)):
                    bank = sb[bn]

                    def gmm(h, wt=wt, c=c, bank=bank):
                        last = None
                        for dt in range(DT):
                            last = h.matmul(
                                bank[:],
                                wt[:, dt * 128:(dt + 1) * 128],
                                sb["xnTs"][:, dt * 1024 + c * 512:
                                           dt * 1024 + c * 512 + 512],
                                start=(dt == 0), stop=(dt == DT - 1))
                        return last
                    p.op("pe", gmm, reads=[r("wg_t%d" % sl) if which == "g"
                                           else r("wu_t%d" % sl), r("xnTs")],
                         writes=[r(bn)])
                p.op("act", lambda h, j=j, bg=bg, ssl=ssl: h.activation(
                    out=ssl, in_=sb[bg][:], func=AF.Silu,
                    bias=0.0, scale=sb["sgate"][:, j:j + 1]),
                    reads=[r(bg), r("sgate")], writes=[r(silu_res)])
                p.op("dve", lambda h, j=j, c=c, bu=bu, ssl=ssl:
                     h.scalar_tensor_tensor(
                         out=sb["hact"][:, j * 1024 + c * 512:
                                        j * 1024 + c * 512 + 512],
                         in0=sb[bu][:], scalar=sb["sup"][:, j:j + 1],
                         in1=ssl, op0=ALU.mult, op1=ALU.mult),
                     reads=[r(bu), r("sup"), r(silu_res)], writes=[r("hact")])

        for tg in range(4):
            for j in range(HT):
                sl = j % 3
                wd = sb["wd_t%d" % sl]
                p.op("sp", lambda h, wd=wd, j=j: h.dma_start(
                    out=wd[:], in_=dr["wdT_scr"][j, :, :]),
                    writes=[r("wd_t%d" % sl)], dma="wd_t%d" % sl)
                for ts2 in range(2):
                    for oc in range(4):
                        bno = "pb%d" % (ts2 * 4 + oc)
                        p.op("pe", lambda h, j=j, tg=tg, ts2=ts2, oc=oc, wd=wd,
                             bno=bno: h.matmul(
                                 sb[bno][:],
                                 sb["hact"][:, j * 1024 + tg * 256 + ts2 * 128:
                                            j * 1024 + tg * 256 + (ts2 + 1) * 128],
                                 wd[:, oc * 512:(oc + 1) * 512],
                                 start=(j == 0), stop=(j == HT - 1)),
                             reads=[r("hact"), r("wd_t%d" % sl)],
                             writes=[r(bno)])
            for ts2 in range(2):
                trow = t0 + tg * 256 + ts2 * 128
                st_idx = trow // 128
                for oc in range(4):
                    bno = "pb%d" % (ts2 * 4 + oc)
                    p.op("act", lambda h, oc=oc, bno=bno: h.activation(
                        out=sb["gbuf"][:, oc * 512:(oc + 1) * 512],
                        in_=sb[bno][:], func=AF.Silu),
                        reads=[r(bno)], writes=[r("gbuf")])
                p.op("pool", lambda h, trow=trow: h.dma_start(
                    out=sb["xres"][:], in_=dr["x"][trow:trow + 128, :]),
                    writes=[r("xres")], dma="xres")
                p.op("pool", lambda h, st=st_idx: h.dma_start(
                    out=sb["sres"][:], in_=dr["s_scr"][st, :, :]),
                    writes=[r("sres")], dma="sres")
                p.op("dve", lambda h: h.tensor_tensor(
                    out=sb["obuf"][:], in0=sb["sres"][:], in1=sb["gbuf"][:],
                    op=ALU.mult),
                    reads=[r("sres"), r("gbuf")], writes=[r("obuf")])
                p.op("dve", lambda h: h.tensor_tensor(
                    out=sb["obuf"][:], in0=sb["obuf"][:], in1=sb["xres"][:],
                    op=ALU.add),
                    reads=[r("xres")], writes=[r("obuf")], strict=True)
                p.op("pool", lambda h, trow=trow: h.dma_start(
                    out=dr["y"][trow:trow + 128, :], in_=sb["obuf"][:]),
                    reads=[r("obuf")], dma="y_st")

    def emit_debug(self):
        sb = self.sb
        self.dbg("q_cT", lambda: sb["q_cT"][:], (128, 4 * T), BF16, ["q_cT"])
        self.dbg("khatT", lambda: sb["khatT"][:], (128, T), BF16, ["khatT"])
        self.dbg("v_sb", lambda: sb["v_sb"][:], (128, NT * 128), BF16, ["v_sb"])
        self.dbg("qiT", lambda: sb["qiT"][:], (128, 2 * T), BF16, ["qiT"])
        self.dbg("wih", lambda: sb["wih"][:], (128, NT * 4), F32, ["wih"])
        self.dbg("ckvT", lambda: sb["ckvT"][:], (128, NBLK), F32, ["ckvT"])
        self.dbg("kcompT", lambda: sb["kcompT"][:64, :], (64, NBLK), F32,
                 ["kcompT"])
        self.dbg("skv_ss", lambda: sb["skv_ss"][:], (128, NBLK), BF16,
                 ["skv_ss"])
        self.dbg("skv_pv", lambda: sb["skv_pv"][:], (128, 2 * 128), BF16,
                 ["skv_pv"])
        if "xnT" in self.dbg_names:
            d = self.dout("dbg_xnT", (DT, 128, T), BF16)
            self.p.op("pool", lambda h, d=d: h.dma_start(
                out=d[:, :, :], in_=self.dram["xnT_scr"][:, :, :]), dma="dbg")
        if "wgT" in self.dbg_names:
            d = self.dout("dbg_wgT", (HT, 128, DT * 128), BF16)
            self.p.op("pool", lambda h, d=d: h.dma_start(
                out=d[:, :, :], in_=self.dram["wgT_scr"][:, :, :]),
                dma="dbg")
        if "wdT" in self.dbg_names:
            d = self.dout("dbg_wdT", (HT, 128, D), BF16)
            self.p.op("pool", lambda h, d=d: h.dma_start(
                out=d[:, :, :], in_=self.dram["wdT_scr"][:, :, :]), dma="dbg")
        if "mask" in self.dbg_names:
            d = self.dout("dbg_mask", (128, NT * NBLK), BF16)
            self.p.op("pool", lambda h, d=d: h.dma_start(
                out=d[:, :], in_=sb["mask_sb"][:]),
                reads=[self.r("mask_sb")], dma="dbg")
        for nm in ("ssb0", "wsb0", "pbf0", "pwb0", "ptb0", "ptw0", "tmps0"):
            if nm in self.dbg_names:
                hnd = self.sb[nm]
                shp = (128, hnd.shape[1])
                dt = hnd.dtype if hasattr(hnd, "dtype") else None
                d = self.dout("dbg_" + nm, shp, dt)
                self.p.op("pool", lambda h, d=d, hnd=hnd: h.dma_start(
                    out=d[:, :], in_=hnd[:]),
                    reads=[self.r(nm)], dma="dbg")
        if "qhatT" in self.dbg_names:
            d = self.dout("dbg_qhatT", (128, 16 * 512), BF16)
            self.p.op("pool", lambda h, d=d: h.dma_start(
                out=d[:, :], in_=self.sb["qhatT"][:]),
                reads=[self.r("qhatT")], dma="dbg")
        if "attn_bf" in self.dbg_names:
            d = self.dout("dbg_attn", (128, 16 * 512), BF16)
            self.p.op("pool", lambda h, d=d: h.dma_start(
                out=d[:, :], in_=self.sb["attn_bf"][:]),
                reads=[self.r("attn_bf")], dma="dbg")
        if "s" in self.dbg_names:
            d = self.dout("dbg_s", (NT, 128, D), BF16)
            self.p.op("pool", lambda h, d=d: h.dma_start(
                out=d[:, :, :], in_=self.dram["s_scr"][:, :, :]), dma="dbg")

    def build(self):
        p = self.p
        self.p0_globals()
        with ExitStack() as cA:
            self.p0_allocA(cA)
            with ExitStack() as cAB:
                self.sbuf("qiT", (128, 2 * T), BF16, cAB)
                self.sbuf("wih", (128, NT * 4), F32, cAB)
                with ExitStack() as cB:
                    self.p0_weights(cB)
                    self.p1_alloc(cB)
                    p.barrier()
                    for u in range(NSUP):
                        self.p1(u)
                    self.p1_post()
                    p.barrier()
                with ExitStack() as cC:
                    self.p2_alloc(cC)
                    self.p05_alloc(cC)
                    for st in range(NT):
                        self.p2(st)
                        if st < DT:
                            self.p2_fwconv(st)
                        self.p05_chunk(5)
                    self.p05_chunk(10 ** 9)
                    p.barrier()
            with ExitStack() as cD:
                self.p3_alloc(cD)
                for u in range(NSUP):
                    self.p3(u)
                p.barrier()
            self.emit_debug()
            p.barrier()
        with ExitStack() as cE:
            self.p4_alloc(cE)
            for v in range(T // 1024):
                self.p4(v)
            p.barrier()
        block = self.ctx.enter_context(self.nc.Block())
        p.replay(block)
        self.ctx.close()
        return self.nc


_BUILD_CACHE = {}


def build_nc(dbg_names=()):
    key = tuple(sorted(dbg_names))
    if key not in _BUILD_CACHE:
        b = Builder(dbg_names)
        _BUILD_CACHE[key] = (b.build(), b)
    return _BUILD_CACHE[key]


def kernel(**inputs):
    shared, per_core = host_prep(inputs)
    nc, b = build_nc()
    in_maps = []
    for c in range(8):
        m = dict(shared)
        m.update(per_core[c])
        in_maps.append(m)
    res = run_bass_kernel_spmd(nc, in_maps, core_ids=list(range(8)))
    out = np.stack([res.results[c]["y"] for c in range(8)], axis=0)
    return out.astype(np.float32)


# version 08 (final)
# speedup vs baseline: 1.4883x; 1.0509x over previous
import math
from contextlib import ExitStack

import numpy as np
import ml_dtypes

import concourse.bass as bass
import concourse.mybir as mybir
from concourse.bass_utils import run_bass_kernel_spmd

F32 = mybir.dt.float32
BF16 = mybir.dt.bfloat16
I32 = mybir.dt.int32
AF = mybir.ActivationFunctionType
ALU = mybir.AluOpType
AX = mybir.AxisListType

BF = ml_dtypes.bfloat16

T = 4096
D = 2048
H = 16
HD = 128
QC = 512
NBLK = 256
MBLK = 16
CI = 64
NIH = 4
WIN = 256
HID = 5632
NEG = -1.0e30
SENT = -2.0e30
SCALE = 1.0 / math.sqrt(128.0)

NT = T // 128
NSUP = T // 512
HT = HID // 128
DT = D // 128

ENGINES = ("pe", "dve", "act", "pool", "sp")
DISABLE = set()
ROPE_CAP = 6


class Res:
    __slots__ = ("w", "rs")

    def __init__(self):
        self.w = None
        self.rs = []

    def for_read(self):
        return [self.w] if self.w is not None else []

    def for_write(self):
        out = list(self.rs)
        if self.w is not None:
            out.append(self.w)
        return out


class Prog:

    def __init__(self, nc, ctx):
        self.nc = nc
        self.ctx = ctx
        self.ops = {e: [] for e in ENGINES}
        self.sems = {}
        self.cnt = {}
        for name in ENGINES:
            self.sems[name] = ctx.enter_context(nc.semaphore("sem_" + name))
            self.cnt[name] = 0
        self.dma_sems = []
        self.last_tok = {e: None for e in ENGINES}
        self.floor = {e: {} for e in ENGINES}

    def _dma_sem(self, name):
        key = "dma_" + name
        if key not in self.sems:
            self.sems[key] = self.ctx.enter_context(
                self.nc.semaphore("sem_" + key))
            self.cnt[key] = 0
            self.dma_sems.append(key)
        return key

    def op(self, eng, fn, reads=(), writes=(), waits=(), dma=None, strict=False):
        wl = {}
        fl = self.floor[eng]

        def add(tok):
            if tok is None:
                return
            sem, val, teng = tok
            if fl.get(sem, -1) >= val:
                return
            if wl.get(sem, -1) < val:
                wl[sem] = val
        for rres in reads:
            for tok in rres.for_read():
                add(tok)
        for wres in writes:
            for tok in wres.for_write():
                add(tok)
        for tok in waits:
            add(tok)
        for sem, val in wl.items():
            assert val <= self.cnt[sem], (sem, val, self.cnt[sem])
            fl[sem] = val
        if dma is None:
            sem, k = eng, 1
        else:
            sem, k = self._dma_sem(dma), 16
        self.cnt[sem] += k
        tok = (sem, self.cnt[sem], None if dma is not None else eng)
        self.ops[eng].append((fn, sorted(wl.items()), sem, k))
        if dma is None:
            self.last_tok[eng] = tok
        for rres in reads:
            rres.rs.append(tok)
        for wres in writes:
            wres.w = tok
            wres.rs = []
        return tok

    def barrier(self):
        toks = [t for t in self.last_tok.values() if t is not None]
        for key in self.dma_sems:
            if self.cnt[key]:
                toks.append((key, self.cnt[key], None))
        for e in ENGINES:
            self.op(e, lambda h: h.nop(), waits=toks)

    def replay(self, block):
        sems = self.sems

        def mk_body(eng_name):
            def run(h):
                for fn, waits, sem, k in self.ops[eng_name]:
                    for wsem, val in waits:
                        h.wait_ge(sems[wsem], val)
                    instr = fn(h)
                    if instr is not None:
                        instr.then_inc(sems[sem], k)
                    else:
                        h.nop().then_inc(sems[sem], k)
            return run
        block.tensor(mk_body("pe"))
        block.vector(mk_body("dve"))
        block.scalar(mk_body("act"))
        block.gpsimd(mk_body("pool"))
        block.sync(mk_body("sp"))


def host_constants():
    c = {}
    c["ident"] = np.eye(128, dtype=BF)
    inv = 1.0 / 10000.0 ** (np.arange(0, 64, 2, dtype=np.float64) / 64.0)
    ang = np.arange(T, dtype=np.float64)[:, None] * inv
    cos = np.cos(ang).astype(np.float32)
    sin = np.sin(ang).astype(np.float32)
    c["cosp"] = np.ascontiguousarray(
        cos.reshape(NT, 128, 32).transpose(1, 0, 2).reshape(128, NT * 32))
    c["sinp"] = np.ascontiguousarray(
        sin.reshape(NT, 128, 32).transpose(1, 0, 2).reshape(128, NT * 32))
    tvec = np.arange(T)
    nvec = np.arange(NBLK)
    causal = nvec[None, :] < (tvec[:, None] // MBLK)
    ca = np.where(causal, 0.0, NEG).astype(np.float32)
    c["causal"] = np.ascontiguousarray(
        ca.reshape(NT, 128, NBLK).transpose(1, 0, 2).reshape(128, NT * NBLK)
    ).astype(BF)
    wms = []
    for blk0 in (0, 1):
        for e in (0, 1):
            pvec = np.arange(128)[:, None]
            cvec = np.arange(384)[None, :]
            diff = cvec - pvec
            ok = (diff > 0) & (diff <= WIN)
            if blk0:
                ok = ok & (cvec + e * 128 >= 256)
            wms.append(np.where(ok, 0.0, NEG).astype(np.float32))
    c["winmask"] = np.ascontiguousarray(
        np.concatenate(wms, axis=1)).astype(BF)
    return c


def host_prep(inputs):
    g = lambda k: np.ascontiguousarray(np.asarray(inputs[k]))
    shared = dict(host_constants())
    shared["w_qc"] = g("L0_attn_w_qc")
    shared["w_qup"] = g("L0_attn_w_qup")
    shared["W1"] = np.ascontiguousarray(np.concatenate(
        [g("L0_attn_k_proj_w"), g("L0_attn_v_proj_w"), g("L0_attn_w_w")], axis=1))
    shared["W2"] = np.ascontiguousarray(np.concatenate(
        [g("L0_attn_kvc_w"), g("L0_attn_kvc_wz"), g("L0_attn_idx_c_w"),
         g("L0_attn_idx_c_wz"), g("L0_attn_w_dq")], axis=1))
    shared["w_iuq"] = g("L0_attn_w_iuq")
    shared["biasTkv"] = np.ascontiguousarray(g("L0_attn_kvc_bias").T)
    shared["biasTidx"] = np.ascontiguousarray(g("L0_attn_idx_c_bias").T)
    qn = g("L0_attn_q_norm_w").reshape(-1)
    kn = g("L0_attn_k_norm_w").reshape(-1)
    shared["qn_col"] = np.ascontiguousarray(qn[:, None])
    shared["kn_col"] = np.ascontiguousarray(kn[:, None])
    shared["qn_rep"] = np.ascontiguousarray(np.broadcast_to(qn[None, :], (128, 128)))
    shared["kn_rep"] = np.ascontiguousarray(np.broadcast_to(kn[None, :], (128, 128)))
    shared["group_w"] = g("L0_attn_group_w")
    shared["final_w"] = g("L0_attn_final_w")
    shared["wq_gate"] = g("L0_ffn_wq_gate")
    shared["wq_up"] = g("L0_ffn_wq_up")
    shared["wq_down"] = g("L0_ffn_wq_down")
    shared["sgate"] = np.ascontiguousarray(g("L0_ffn_s_gate").reshape(HT, 128).T)
    shared["sup"] = np.ascontiguousarray(g("L0_ffn_s_up").reshape(HT, 128).T)
    shared["sdown_rep"] = np.ascontiguousarray(
        np.broadcast_to(g("L0_ffn_s_down").reshape(-1)[None, :], (128, D)))
    x = np.asarray(inputs["L0_attn_h"])
    per_core = [{"x": np.ascontiguousarray(x[b])} for b in range(8)]
    return shared, per_core


class Builder:
    def __init__(self, dbg_names=()):
        self.nc = bass.Bass("TRN2", target_bir_lowering=False, debug=False,
                            num_swdge_queues=4)
        self.ctx = ExitStack()
        self.p = Prog(self.nc, self.ctx)
        self.dbg_names = set(dbg_names)
        self.dram = {}
        self.sb = {}
        self.res = {}

    def din(self, name, shape, dtype):
        self.dram[name] = self.nc.dram_tensor(name, list(shape), dtype,
                                              kind="ExternalInput").ap()
        return self.dram[name]

    def dout(self, name, shape, dtype):
        self.dram[name] = self.nc.dram_tensor(name, list(shape), dtype,
                                              kind="ExternalOutput").ap()
        return self.dram[name]

    def dscr(self, name, shape, dtype):
        self.dram[name] = self.nc.dram_tensor(name, list(shape), dtype,
                                              kind="Internal").ap()
        return self.dram[name]

    def sbuf(self, name, shape, dtype, ctx=None):
        h = (ctx or self.ctx).enter_context(
            self.nc.sbuf_tensor("sb_" + name, list(shape), dtype))
        self.sb[name] = h
        self.res[name] = Res()
        return h

    def psum(self, name, ctx=None):
        h = (ctx or self.ctx).enter_context(
            self.nc.psum_tensor("ps_" + name, [128, 512], F32))
        self.sb[name] = h
        self.res[name] = Res()
        return h

    def r(self, name):
        return self.res[name]

    def dbg(self, name, ap_fn, shape, dtype, reads):
        if name not in self.dbg_names:
            return
        d = self.dout("dbg_" + name, shape, dtype)
        self.p.op("pool", lambda h, d=d, f=ap_fn: h.dma_start(out=d, in_=f()),
                  reads=[self.r(n) for n in reads], dma="dbg")

    def p0_globals(self):
        p, dr = self.p, self.dram
        for i in range(8):
            self.psum("pb%d" % i)
        self.din("x", (T, D), F32)
        for nm, sh, dt in [
            ("ident", (128, 128), BF16), ("cosp", (128, NT * 32), F32),
            ("sinp", (128, NT * 32), F32), ("causal", (128, NT * NBLK), BF16),
            ("winmask", (128, 4 * 384), BF16),
            ("w_qc", (D, QC), F32), ("w_qup", (QC, D), F32),
            ("W1", (D, 260), F32), ("W2", (D, 448), F32),
            ("w_iuq", (64, 256), F32),
            ("biasTkv", (128, 16), F32), ("biasTidx", (64, 16), F32),
            ("qn_col", (128, 1), F32), ("kn_col", (128, 1), F32),
            ("qn_rep", (128, 128), F32), ("kn_rep", (128, 128), F32),
            ("group_w", (4, 512, 512), F32), ("final_w", (D, D), F32),
            ("wq_gate", (HID, D), I32), ("wq_up", (HID, D), I32),
            ("wq_down", (D, HID), I32),
            ("sgate", (128, HT), F32), ("sup", (128, HT), F32),
            ("sdown_rep", (128, D), F32),
        ]:
            self.din(nm, sh, dt)
        self.dout("y", (T, D), F32)
        self.dscr("xnT_scr", (DT, 128, T), BF16)
        self.dscr("s_scr", (NT, 128, D), BF16)
        self.dscr("wgT_scr", (HT, 128, DT * 128), BF16)
        self.dscr("wuT_scr", (HT, 128, DT * 128), BF16)
        self.dscr("wdT_scr", (HT, 128, D), BF16)
        self.dscr("fwT_scr", (DT, 128, D), BF16)

        sb = self.sbuf
        sb("ident", (128, 128), BF16)
        sb("constp", (128, 8), F32)
        sb("sdown", (128, D), F32)
        sb("qn_col", (128, 1), F32)
        sb("kn_col", (128, 1), F32)
        sb("gains_col", (128, 1), F32)
        sb("ones_col", (128, 1), F32)
        sb("ones_row", (1, 128), F32)

        for nm in ("ident", "qn_col", "kn_col"):
            p.op("sp", lambda h, d=self.sb[nm], s=dr[nm]: h.dma_start(
                out=d[:], in_=s[:, :]), writes=[self.r(nm)], dma=nm)
        p.op("sp", lambda h: h.dma_start(out=self.sb["sdown"][:],
                                         in_=dr["sdown_rep"][:, :]),
             writes=[self.r("sdown")], dma="sdown")
        cp = self.sb["constp"]
        for slot, val in [(0, 1e-6), (1, 1e-5), (2, -0.5), (3, 1.0 / D),
                          (4, 1.0 / 128.0), (5, 0.0), (6, 1e30), (7, -1e30)]:
            p.op("dve", lambda h, s=slot, v=val: h.memset(cp[:, s:s + 1], v),
                 writes=[self.r("constp")])
        p.op("dve", lambda h: h.memset(self.sb["ones_col"][:], 1.0),
             writes=[self.r("ones_col")])
        p.op("dve", lambda h: h.memset(self.sb["ones_row"][:], 1.0),
             writes=[self.r("ones_row")])
        p.op("dve", lambda h: h.tensor_tensor(
            out=self.sb["gains_col"][:], in0=self.sb["qn_col"][:],
            in1=self.sb["kn_col"][:], op=ALU.mult),
            reads=[self.r("qn_col"), self.r("kn_col")],
            writes=[self.r("gains_col")])
        p.op("dve", lambda h: h.tensor_scalar(
            out=self.sb["gains_col"][:], in0=self.sb["gains_col"][:],
            scalar1=SCALE, scalar2=None, op0=ALU.mult),
            writes=[self.r("gains_col")], strict=True)

    def p0_allocA(self, ctx):
        p, dr = self.p, self.dram
        sb = self.sbuf
        sb("cos", (128, NT * 32), F32, ctx)
        sb("sin", (128, NT * 32), F32, ctx)
        sb("q_cT", (128, 4 * T), BF16, ctx)
        sb("khatT", (128, T), BF16, ctx)
        sb("v_sb", (128, NT * 128), BF16, ctx)
        sb("ckvT", (128, NBLK), F32, ctx)
        sb("kcompT", (64, NBLK), F32, ctx)
        sb("kcomp_bf", (128, NBLK), BF16, ctx)
        sb("skv_ss", (128, NBLK), BF16, ctx)
        sb("skv_pv", (128, 2 * 128), BF16, ctx)
        sb("mask_sb", (128, NT * NBLK), BF16, ctx)
        for nm, srcn in (("cos", "cosp"), ("sin", "sinp")):
            p.op("sp", lambda h, d=self.sb[nm], s=dr[srcn]: h.dma_start(
                out=d[:], in_=s[:, :]), writes=[self.r(nm)], dma=nm)

    def p0_weights(self, ctx):
        p, dr = self.p, self.dram
        sb = self.sbuf
        sb("wqs_rep", (128, 128), F32, ctx)
        sb("wk_rep", (128, 128), F32, ctx)
        sb("biasTkv", (128, 16), F32, ctx)
        sb("biasTidx", (64, 16), F32, ctx)
        sb("w_iuq", (64, 256), BF16, ctx)
        sb("wstage", (128, 512), F32, ctx)
        sb("w_qc_bf", (128, DT * QC), BF16, ctx)
        sb("W1_bf", (128, DT * 260), BF16, ctx)
        sb("W2_bf", (128, DT * 448), BF16, ctx)

        for nm in ("biasTkv", "biasTidx"):
            p.op("sp", lambda h, d=self.sb[nm], s=dr[nm]: h.dma_start(
                out=d[:], in_=s[:, :]), writes=[self.r(nm)], dma=nm)

        def conv(dstname, col, src_ap, ncols):
            st = self.sb["wstage"]
            p.op("sp", lambda h, s=src_ap, n=ncols: h.dma_start(
                out=st[:, :n], in_=s), writes=[self.r("wstage")], dma="wstage")
            p.op("dve", lambda h, d=self.sb[dstname], c=col, n=ncols:
                 h.tensor_copy(out=d[:, c:c + n], in_=st[:, :n]),
                 reads=[self.r("wstage")], writes=[self.r(dstname)])

        for i in range(DT):
            conv("w_qc_bf", i * QC, dr["w_qc"][i * 128:(i + 1) * 128, :], QC)
            conv("W1_bf", i * 260, dr["W1"][i * 128:(i + 1) * 128, :], 260)
            conv("W2_bf", i * 448, dr["W2"][i * 128:(i + 1) * 128, :], 448)
        st = self.sb["wstage"]
        p.op("sp", lambda h: h.dma_start(out=st[:64, :256], in_=dr["w_iuq"][:, :]),
             writes=[self.r("wstage")], dma="wstage")
        p.op("dve", lambda h: h.tensor_copy(out=self.sb["w_iuq"][:, :],
                                            in_=st[:64, :256]),
             reads=[self.r("wstage")], writes=[self.r("w_iuq")])
        p.op("sp", lambda h: h.dma_start(out=self.sb["wqs_rep"][:],
                                         in_=dr["qn_rep"][:, :]),
             writes=[self.r("wqs_rep")], dma="wqs_rep")
        p.op("dve", lambda h: h.tensor_scalar(
            out=self.sb["wqs_rep"][:], in0=self.sb["wqs_rep"][:],
            scalar1=SCALE, scalar2=None, op0=ALU.mult),
            reads=[self.r("wqs_rep")], writes=[self.r("wqs_rep")])
        p.op("sp", lambda h: h.dma_start(out=self.sb["wk_rep"][:],
                                         in_=dr["kn_rep"][:, :]),
             writes=[self.r("wk_rep")], dma="wk_rep")

    def p1_alloc(self, ctx):
        sb = self.sbuf
        sb("xstage", (128, D), F32, ctx)
        sb("lnstat", (128, 16), F32, ctx)
        sb("xn_tm", (128, 2 * D), BF16, ctx)
        sb("xnT", (128, DT * 512), BF16, ctx)
        sb("kstage", (128, 4 * 128), F32, ctx)
        sb("khat_tm", (128, 4 * 128), BF16, ctx)
        sb("dq_bf", (64, 512), BF16, ctx)
        sb("czkv", (128, 1024), F32, ctx)
        sb("czidx", (64, 1024), F32, ctx)
        sb("cmp_scr", (128, 512), F32, ctx)
        sb("cmp_red", (128, 64), F32, ctx)
        sb("rope_scr", (128, 128), F32, ctx)
        sb("skv_stg", (128, 256), BF16, ctx)

    def p1(self, u):
        p, sb, dr, r = self.p, self.sb, self.dram, self.r
        t0 = u * 512
        cp = sb["constp"]
        xst = sb["xstage"]

        for s in range(4):
            p.op("pool", lambda h, s=s: h.dma_start(
                out=xst[:],
                in_=dr["x"][t0 + s * 128:t0 + (s + 1) * 128, :]),
                writes=[r("xstage")], dma="xstage")
            xs = xst[:, 0:D]
            c_mean = sb["lnstat"][:, 4 * s:4 * s + 1]
            c_rv = sb["lnstat"][:, 4 * s + 1:4 * s + 2]
            c_t = sb["lnstat"][:, 4 * s + 2:4 * s + 3]
            p.op("dve", lambda h, xs=xs, c=c_mean: h.reduce_sum(
                out=c, in_=xs, axis=AX.X),
                reads=[r("xstage")], writes=[r("lnstat")])
            p.op("act", lambda h, xs=xs, s=s, c=c_rv: h.activation(
                out=sb["xn_tm"][:, (s % 2) * D:(s % 2 + 1) * D], in_=xs,
                func=AF.Square, accum_out=c),
                reads=[r("xstage")], writes=[r("xn_tm"), r("lnstat")])
            p.op("dve", lambda h, c=c_mean: h.tensor_scalar(
                out=c, in0=c, scalar1=1.0 / D, scalar2=None, op0=ALU.mult),
                reads=[r("lnstat")], writes=[r("lnstat")], strict=True)
            p.op("dve", lambda h, c=c_t, m=c_mean: h.tensor_tensor(
                out=c, in0=m, in1=m, op=ALU.mult),
                reads=[r("lnstat")], writes=[r("lnstat")], strict=True)
            p.op("dve", lambda h, c=c_rv, t_=c_t: h.scalar_tensor_tensor(
                out=c, in0=c, scalar=1.0 / D, in1=t_, op0=ALU.mult,
                op1=ALU.subtract),
                reads=[r("lnstat")], writes=[r("lnstat")], strict=True)
            p.op("act", lambda h, c=c_rv: h.activation(
                out=c, in_=c, func=AF.Ln, bias=cp[:, 1:2], scale=1.0),
                reads=[r("lnstat"), r("constp")], writes=[r("lnstat")])
            p.op("act", lambda h, c=c_rv: h.activation(
                out=c, in_=c, func=AF.Exp, bias=0.0, scale=cp[:, 2:3]),
                reads=[r("lnstat"), r("constp")], writes=[r("lnstat")],
                strict=True)
            p.op("dve", lambda h, c=c_t, m=c_mean, v=c_rv: h.scalar_tensor_tensor(
                out=c, in0=m, scalar=-1.0, in1=v, op0=ALU.mult, op1=ALU.mult),
                reads=[r("lnstat")], writes=[r("lnstat")])
            p.op("act", lambda h, xs=xs, s=s, v=c_rv, b=c_t: h.activation(
                out=sb["xn_tm"][:, (s % 2) * D:(s % 2 + 1) * D], in_=xs,
                func=AF.Identity, bias=b, scale=v),
                reads=[r("lnstat"), r("xstage")], writes=[r("xn_tm")])
            if s % 2 == 1:
                pair = s // 2
                for dt in range(DT):
                    bn = "pb%d" % (dt % 2)
                    bank = sb[bn]

                    def tmm(h, dt=dt, bank=bank, pair=pair):
                        last = None
                        for q2 in range(2):
                            last = h.matmul(
                                bank[:, (pair * 2 + q2) * 128:
                                     (pair * 2 + q2 + 1) * 128],
                                sb["xn_tm"][:, q2 * D + dt * 128:
                                            q2 * D + (dt + 1) * 128],
                                sb["ident"][:], start=True, stop=True)
                        return last
                    p.op("pe", tmm, reads=[r("xn_tm"), r("ident")],
                         writes=[r(bn)])
                    p.op("dve", lambda h, dt=dt, bank=bank, pair=pair:
                         h.tensor_copy(
                             out=sb["xnT"][:, dt * 512 + pair * 256:
                                           dt * 512 + (pair + 1) * 256],
                             in_=bank[:, pair * 256:(pair + 1) * 256]),
                         reads=[r(bn)], writes=[r("xnT")])
        p.op("pool", lambda h: h.dma_start(
            out=dr["xnT_scr"][:, :, t0:t0 + 512].rearrange("dt p t -> p dt t"),
            in_=sb["xnT"][:].rearrange("p (dt t) -> p dt t", dt=DT)),
            reads=[r("xnT")], dma="xnT_sp")

        for mt in range(4):
            bank = sb["pb2"]

            def qcmm(h, mt=mt, bank=bank):
                last = None
                for kt in range(DT):
                    last = h.matmul(
                        bank[:],
                        sb["w_qc_bf"][:, kt * QC + mt * 128:
                                      kt * QC + (mt + 1) * 128],
                        sb["xnT"][:, kt * 512:(kt + 1) * 512],
                        start=(kt == 0), stop=(kt == DT - 1))
                return last
            p.op("pe", qcmm, reads=[r("w_qc_bf"), r("xnT")], writes=[r("pb2")])
            p.op("dve", lambda h, mt=mt, bank=bank: h.tensor_copy(
                out=sb["q_cT"][:, mt * T + t0: mt * T + t0 + 512], in_=bank[:]),
                reads=[r("pb2")], writes=[r("q_cT")])

        for s in range(4):
            bank = sb["pb3"]
            st_idx = u * 4 + s

            def w1mm(h, s=s, bank=bank):
                last = None
                for kt in range(DT):
                    last = h.matmul(
                        bank[:, :260],
                        sb["xnT"][:, kt * 512 + s * 128:kt * 512 + (s + 1) * 128],
                        sb["W1_bf"][:, kt * 260:(kt + 1) * 260],
                        start=(kt == 0), stop=(kt == DT - 1))
                return last
            p.op("pe", w1mm, reads=[r("W1_bf"), r("xnT")], writes=[r("pb3")])
            p.op("act", lambda h, s=s, bank=bank: h.copy(
                out=sb["kstage"][:, s * 128:(s + 1) * 128], in_=bank[:, 0:128]),
                reads=[r("pb3")], writes=[r("kstage")])
            p.op("act", lambda h, bank=bank, st=st_idx: h.copy(
                out=sb["v_sb"][:, st * 128:(st + 1) * 128], in_=bank[:, 128:256]),
                reads=[r("pb3")], writes=[r("v_sb")])
            p.op("act", lambda h, bank=bank, st=st_idx: h.copy(
                out=sb["wih"][:, st * 4:(st + 1) * 4], in_=bank[:, 256:260]),
                reads=[r("pb3")], writes=[r("wih")])

        for s in range(4):
            st_idx = u * 4 + s
            ks = sb["kstage"][:, s * 128:(s + 1) * 128]
            c_rs = sb["lnstat"][:, s:s + 1]
            p.op("act", lambda h, ks=ks, c=c_rs: h.activation(
                out=sb["xn_tm"][:, 0:128], in_=ks, func=AF.Square, accum_out=c),
                reads=[r("kstage")], writes=[r("xn_tm"), r("lnstat")])
            p.op("act", lambda h, c=c_rs: h.activation(
                out=c, in_=c, func=AF.Ln, bias=cp[:, 0:1], scale=cp[:, 4:5]),
                reads=[r("lnstat"), r("constp")], writes=[r("lnstat")],
                strict=True)
            p.op("act", lambda h, c=c_rs: h.activation(
                out=c, in_=c, func=AF.Exp, bias=0.0, scale=cp[:, 2:3]),
                reads=[r("lnstat"), r("constp")], writes=[r("lnstat")],
                strict=True)
            p.op("act", lambda h, ks=ks, c=c_rs: h.activation(
                out=ks, in_=ks, func=AF.Identity, bias=0.0, scale=c),
                reads=[r("lnstat")], writes=[r("kstage")], strict=True)
            p.op("dve", lambda h, ks=ks: h.tensor_tensor(
                out=ks, in0=ks, in1=sb["wk_rep"][:], op=ALU.mult),
                reads=[r("wk_rep"), r("kstage")], writes=[r("kstage")],
                strict=True)
            co = sb["cos"][:, st_idx * 32:(st_idx + 1) * 32]
            si = sb["sin"][:, st_idx * 32:(st_idx + 1) * 32]
            x1 = ks[:, 0:32]
            x2 = ks[:, 32:64]
            rs_ = sb["rope_scr"]
            p.op("dve", lambda h, x1=x1, co=co: h.tensor_tensor(
                out=rs_[:, 0:32], in0=x1, in1=co, op=ALU.mult),
                reads=[r("kstage"), r("cos")], writes=[r("rope_scr")],
                strict=True)
            p.op("dve", lambda h, x2=x2, co=co: h.tensor_tensor(
                out=rs_[:, 32:64], in0=x2, in1=co, op=ALU.mult),
                reads=[r("kstage"), r("cos")], writes=[r("rope_scr")],
                strict=True)
            p.op("dve", lambda h, x2=x2, si=si: h.scalar_tensor_tensor(
                out=rs_[:, 64:96], in0=x2, scalar=-1.0, in1=si,
                op0=ALU.mult, op1=ALU.mult),
                reads=[r("kstage"), r("sin")], writes=[r("rope_scr")],
                strict=True)
            p.op("dve", lambda h, x1=x1, si=si: h.tensor_tensor(
                out=rs_[:, 96:128], in0=x1, in1=si, op=ALU.mult),
                reads=[r("kstage"), r("sin")], writes=[r("rope_scr")],
                strict=True)
            p.op("dve", lambda h, x1=x1: h.tensor_tensor(
                out=x1, in0=rs_[:, 0:32], in1=rs_[:, 64:96], op=ALU.add),
                reads=[r("rope_scr")], writes=[r("kstage")], strict=True)
            p.op("dve", lambda h, x2=x2: h.tensor_tensor(
                out=x2, in0=rs_[:, 96:128], in1=rs_[:, 32:64], op=ALU.add),
                reads=[r("rope_scr")], writes=[r("kstage")], strict=True)
            p.op("dve", lambda h, ks=ks, s=s: h.tensor_tensor(
                out=sb["khat_tm"][:, s * 128:(s + 1) * 128], in0=ks,
                in1=sb["wqs_rep"][:], op=ALU.mult),
                reads=[r("kstage"), r("wqs_rep")], writes=[r("khat_tm")],
                strict=True)
        bank = sb["pb4"]

        def ktr(h, bank=bank):
            last = None
            for s in range(4):
                last = h.matmul(bank[:, s * 128:(s + 1) * 128],
                                sb["khat_tm"][:, s * 128:(s + 1) * 128],
                                sb["ident"][:], start=True, stop=True)
            return last
        p.op("pe", ktr, reads=[r("khat_tm"), r("ident")], writes=[r("pb4")])
        p.op("dve", lambda h, bank=bank: h.tensor_copy(
            out=sb["khatT"][:, t0:t0 + 512], in_=bank[:]),
            reads=[r("pb4")], writes=[r("khatT")])

        for mt, (c0, cn) in enumerate([(0, 128), (128, 128), (256, 128),
                                       (384, 64)]):
            bn = "pb%d" % (4 + mt % 2)
            bank = sb[bn]

            def w2mm(h, c0=c0, cn=cn, bank=bank):
                last = None
                for kt in range(DT):
                    last = h.matmul(
                        bank[:cn, :512],
                        sb["W2_bf"][:, kt * 448 + c0:kt * 448 + c0 + cn],
                        sb["xnT"][:, kt * 512:(kt + 1) * 512],
                        start=(kt == 0), stop=(kt == DT - 1))
                return last
            p.op("pe", w2mm, reads=[r("W2_bf"), r("xnT")], writes=[r(bn)])
            if mt == 0:
                p.op("act", lambda h, bank=bank: h.copy(
                    out=sb["czkv"][:, 0:512], in_=bank[:, :512]),
                    reads=[r(bn)], writes=[r("czkv")])
            elif mt == 1:
                p.op("dve", lambda h, bank=bank: h.tensor_tensor(
                    out=sb["czkv"][:, 512:1024].rearrange("p (b m) -> p b m",
                                                          m=16),
                    in0=bank[:, :512].rearrange("p (b m) -> p b m", m=16),
                    in1=sb["biasTkv"][:].unsqueeze(1).to_broadcast([128, 32, 16]),
                    op=ALU.add),
                    reads=[r(bn), r("biasTkv")], writes=[r("czkv")])
            elif mt == 2:
                p.op("dve", lambda h, bank=bank: h.tensor_copy(
                    out=sb["czidx"][:, 0:512], in_=bank[0:64, :512]),
                    reads=[r(bn)], writes=[r("czidx")])
                p.op("dve", lambda h, bank=bank: h.tensor_tensor(
                    out=sb["czidx"][:, 512:1024].rearrange("p (b m) -> p b m",
                                                           m=16),
                    in0=bank[64:128, :512].rearrange("p (b m) -> p b m", m=16),
                    in1=sb["biasTidx"][:].unsqueeze(1).to_broadcast([64, 32, 16]),
                    op=ALU.add),
                    reads=[r(bn), r("biasTidx")], writes=[r("czidx")])
            else:
                p.op("dve", lambda h, bank=bank: h.tensor_copy(
                    out=sb["dq_bf"][:], in_=bank[0:64, :512]),
                    reads=[r(bn)], writes=[r("dq_bf")])
                for mt2 in range(2):
                    bank2 = sb["pb3"]
                    p.op("pe", lambda h, mt2=mt2, bank2=bank2: h.matmul(
                        bank2[:, :512],
                        sb["w_iuq"][:, mt2 * 128:(mt2 + 1) * 128],
                        sb["dq_bf"][:], start=True, stop=True),
                        reads=[r("w_iuq"), r("dq_bf")], writes=[r("pb3")])
                    p.op("dve", lambda h, mt2=mt2, bank2=bank2: h.tensor_copy(
                        out=sb["qiT"][:, mt2 * T + t0:mt2 * T + t0 + 512],
                        in_=bank2[:, :512]),
                        reads=[r("pb3")], writes=[r("qiT")])

        def compress(cz_name, parts, dstname):
            cz = sb[cz_name]
            czr = r(cz_name)
            scr = sb["cmp_scr"]
            red = sb["cmp_red"]
            c_ap = cz[:parts, 0:512]
            z_ap = cz[:parts, 512:1024]
            p.op("dve", lambda h: h.reduce_max(
                out=red[:parts, 0:32],
                in_=z_ap.rearrange("p (b m) -> p b m", m=16),
                axis=AX.X, negate=True),
                reads=[czr], writes=[r("cmp_red")])
            p.op("dve", lambda h: h.tensor_tensor(
                out=scr[:parts, 0:512].rearrange("p (b m) -> p b m", m=16),
                in0=z_ap.rearrange("p (b m) -> p b m", m=16),
                in1=red[:parts, 0:32].unsqueeze(2).to_broadcast([parts, 32, 16]),
                op=ALU.add),
                reads=[czr, r("cmp_red")], writes=[r("cmp_scr")], strict=True)
            p.op("act", lambda h: h.activation(
                out=scr[:parts, 0:512], in_=scr[:parts, 0:512], func=AF.Exp),
                reads=[r("cmp_scr")], writes=[r("cmp_scr")])
            p.op("dve", lambda h: h.reduce_sum(
                out=red[:parts, 32:64],
                in_=scr[:parts, 0:512].rearrange("p (b m) -> p b m", m=16),
                axis=AX.X),
                reads=[r("cmp_scr")], writes=[r("cmp_red")])
            p.op("dve", lambda h: h.tensor_tensor(
                out=scr[:parts, 0:512], in0=scr[:parts, 0:512], in1=c_ap,
                op=ALU.mult),
                reads=[r("cmp_scr"), czr], writes=[r("cmp_scr")], strict=True)
            p.op("dve", lambda h: h.reduce_sum(
                out=red[:parts, 0:32],
                in_=scr[:parts, 0:512].rearrange("p (b m) -> p b m", m=16),
                axis=AX.X),
                reads=[r("cmp_scr")], writes=[r("cmp_red")], strict=True)
            p.op("dve", lambda h: h.reciprocal(
                out=red[:parts, 32:64], in_=red[:parts, 32:64]),
                reads=[r("cmp_red")], writes=[r("cmp_red")], strict=True)
            p.op("dve", lambda h: h.tensor_tensor(
                out=sb[dstname][:parts, u * 32:(u + 1) * 32],
                in0=red[:parts, 0:32], in1=red[:parts, 32:64], op=ALU.mult),
                reads=[r("cmp_red")], writes=[r(dstname)], strict=True)
        compress("czkv", 128, "ckvT")
        compress("czidx", 64, "kcompT")

    def p1_post(self):
        p, sb, r = self.p, self.sb, self.r
        cp = sb["constp"]
        scr = sb["cmp_scr"]
        p.op("dve", lambda h: h.tensor_tensor(
            out=scr[:, 0:256], in0=sb["ckvT"][:], in1=sb["ckvT"][:],
            op=ALU.mult),
            reads=[r("ckvT")], writes=[r("cmp_scr")])
        bank = sb["pb0"]
        p.op("pe", lambda h: h.matmul(bank[:1, 0:256], sb["ones_col"][:],
                                      scr[:, 0:256], start=True, stop=True),
             reads=[r("ones_col"), r("cmp_scr")], writes=[r("pb0")])
        p.op("act", lambda h: h.activation(
            out=scr[:1, 256:512], in_=bank[:1, 0:256], func=AF.Ln,
            bias=cp[:1, 0:1], scale=cp[:1, 4:5]),
            reads=[r("pb0"), r("constp")], writes=[r("cmp_scr")])
        p.op("act", lambda h: h.activation(
            out=scr[:1, 256:512], in_=scr[:1, 256:512], func=AF.Exp,
            bias=0.0, scale=cp[:1, 2:3]),
            reads=[r("cmp_scr"), r("constp")], writes=[r("cmp_scr")],
            strict=True)
        bank1 = sb["pb1"]
        p.op("pe", lambda h: h.matmul(bank1[:, 0:256], sb["ones_row"][:],
                                      scr[:1, 256:512], start=True, stop=True),
             reads=[r("ones_row"), r("cmp_scr")], writes=[r("pb1")])
        p.op("dve", lambda h: h.tensor_tensor(
            out=scr[:, 0:256], in0=sb["ckvT"][:], in1=bank1[:, 0:256],
            op=ALU.mult),
            reads=[r("ckvT"), r("pb1")], writes=[r("cmp_scr")])
        p.op("dve", lambda h: h.tensor_scalar(
            out=sb["skv_ss"][:], in0=scr[:, 0:256],
            scalar1=sb["gains_col"][:, 0:1], scalar2=None, op0=ALU.mult),
            reads=[r("cmp_scr"), r("gains_col")], writes=[r("skv_ss")])
        p.op("dve", lambda h: h.tensor_copy(
            out=sb["skv_stg"][:], in_=scr[:, 0:256]),
            reads=[r("cmp_scr")], writes=[r("skv_stg")])
        bank2 = sb["pb2"]

        def tr2(h, bank2=bank2):
            last = None
            for i in range(2):
                last = h.matmul(
                    bank2[:, i * 128:(i + 1) * 128],
                    sb["skv_stg"][:, i * 128:(i + 1) * 128],
                    sb["ident"][:], start=True, stop=True)
            return last
        p.op("pe", tr2, reads=[r("skv_stg"), r("ident")], writes=[r("pb2")])
        p.op("dve", lambda h, bank2=bank2: h.tensor_copy(
            out=sb["skv_pv"][:], in_=bank2[:, 0:256]),
            reads=[r("pb2")], writes=[r("skv_pv")])
        p.op("dve", lambda h: h.tensor_tensor(
            out=sb["skv_pv"][:].rearrange("p (i c) -> p i c", i=2),
            in0=sb["skv_pv"][:].rearrange("p (i c) -> p i c", i=2),
            in1=sb["wk_rep"][:].unsqueeze(1).to_broadcast([128, 2, 128]),
            op=ALU.mult),
            reads=[r("wk_rep")], writes=[r("skv_pv")], strict=True)
        p.op("dve", lambda h: h.tensor_copy(out=sb["kcomp_bf"][0:64, :],
                                            in_=sb["kcompT"][:]),
             reads=[r("kcompT")], writes=[r("kcomp_bf")])
        p.op("pool", lambda h: h.dma_start(out=sb["kcomp_bf"][64:128, :],
                                           in_=sb["kcomp_bf"][0:64, :]),
             reads=[r("kcomp_bf")], writes=[r("kcomp_bf")], dma="kcompdup")

    def p2_alloc(self, ctx):
        p, dr = self.p, self.dram
        sb = self.sbuf
        sb("causal", (128, NT * NBLK), BF16, ctx)
        sb("fw_f32", (128, D), F32, ctx)
        sb("fw_bf", (128, D), BF16, ctx)
        sb("iscc", (128, NBLK), F32, ctx)
        sb("isc2", (128, NBLK), F32, ctx)
        sb("m8", (128, 8), F32, ctx)
        p.op("sp", lambda h: h.dma_start(out=self.sb["causal"][:],
                                         in_=dr["causal"][:, :]),
             writes=[self.r("causal")], dma="causal")

    def p2_fwconv(self, kt):
        p, sb, dr, r = self.p, self.sb, self.dram, self.r
        p.op("sp", lambda h, kt=kt: h.dma_start(
            out=sb["fw_f32"][:], in_=dr["final_w"][kt * 128:(kt + 1) * 128, :]),
            writes=[r("fw_f32")], dma="fw_f32")
        p.op("dve", lambda h: h.tensor_copy(out=sb["fw_bf"][:],
                                            in_=sb["fw_f32"][:]),
             reads=[r("fw_f32")], writes=[r("fw_bf")])
        p.op("pool", lambda h, kt=kt: h.dma_start(
            out=self.dram["fwT_scr"][kt, :, :], in_=sb["fw_bf"][:]),
            reads=[r("fw_bf")], dma="fwbf_st")

    def p2(self, st):
        p, sb, r = self.p, self.sb, self.r
        t0 = st * 128
        nvb = min(8 * st + 8, NBLK)
        cp = sb["constp"]
        for ih in range(NIH):
            bn = "pb%d" % (ih % 2)
            bank = sb[bn]
            ptile = ih // 2
            row0 = (ih % 2) * 64
            p.op("pe", lambda h, bank=bank, ptile=ptile, row0=row0, nvb=nvb:
                 h.matmul(
                     bank[:, 0:nvb],
                     sb["qiT"][row0:row0 + 64, ptile * T + t0:ptile * T + t0 + 128],
                     sb["kcomp_bf"][row0:row0 + 64, 0:nvb],
                     start=True, stop=True),
                 reads=[r("qiT"), r("kcomp_bf")], writes=[r(bn)])
            wcol = sb["wih"][:, st * 4 + ih:st * 4 + ih + 1]
            if ih == 0:
                p.op("dve", lambda h, bank=bank, wcol=wcol, nvb=nvb:
                     h.scalar_tensor_tensor(
                         out=sb["iscc"][:, 0:nvb], in0=bank[:, 0:nvb], scalar=0.0,
                         in1=wcol.to_broadcast([128, nvb]), op0=ALU.max,
                         op1=ALU.mult),
                     reads=[r(bn), r("wih")], writes=[r("iscc")])
            else:
                p.op("dve", lambda h, bank=bank, wcol=wcol, nvb=nvb:
                     h.scalar_tensor_tensor(
                         out=sb["isc2"][:, 0:nvb], in0=bank[:, 0:nvb], scalar=0.0,
                         in1=wcol.to_broadcast([128, nvb]), op0=ALU.max,
                         op1=ALU.mult),
                     reads=[r(bn), r("wih")], writes=[r("isc2")])
                p.op("dve", lambda h, nvb=nvb: h.tensor_tensor(
                    out=sb["iscc"][:, 0:nvb], in0=sb["iscc"][:, 0:nvb],
                    in1=sb["isc2"][:, 0:nvb], op=ALU.add),
                    reads=[r("isc2")], writes=[r("iscc")], strict=True)
        p.op("dve", lambda h, st=st, nvb=nvb: h.tensor_tensor(
            out=sb["iscc"][:, 0:nvb], in0=sb["iscc"][:, 0:nvb],
            in1=sb["causal"][:, st * NBLK:st * NBLK + nvb], op=ALU.add),
            reads=[r("causal")], writes=[r("iscc")], strict=True)
        for rnd in range(8):
            p.op("dve", lambda h, nvb=nvb: h.max(out=sb["m8"][:],
                                                 in_=sb["iscc"][:, 0:nvb]),
                 reads=[r("iscc")], writes=[r("m8")], strict=True)
            p.op("dve", lambda h, nvb=nvb: h.match_replace(
                out=sb["iscc"][:, 0:nvb], in_to_replace=sb["m8"][:],
                in_values=sb["iscc"][:, 0:nvb], imm_value=SENT),
                reads=[r("m8")], writes=[r("iscc")], strict=True)
        p.op("dve", lambda h, nvb=nvb: h.tensor_scalar(
            out=sb["isc2"][:, 0:nvb], in0=sb["iscc"][:, 0:nvb], scalar1=SENT,
            scalar2=None, op0=ALU.is_equal),
            reads=[r("iscc")], writes=[r("isc2")], strict=True)
        p.op("act", lambda h, nvb=nvb: h.activation(
            out=sb["isc2"][:, 0:nvb], in_=sb["isc2"][:, 0:nvb],
            func=AF.Identity, bias=cp[:, 7:8], scale=cp[:, 6:7]),
            reads=[r("isc2"), r("constp")], writes=[r("isc2")])
        p.op("dve", lambda h, st=st, nvb=nvb: h.tensor_tensor(
            out=sb["mask_sb"][:, st * NBLK:st * NBLK + nvb],
            in0=sb["isc2"][:, 0:nvb],
            in1=sb["causal"][:, st * NBLK:st * NBLK + nvb], op=ALU.add),
            reads=[r("isc2"), r("causal")], writes=[r("mask_sb")])

    def p3_alloc(self, ctx):
        p, dr = self.p, self.dram
        sb = self.sbuf
        sb("winmask", (128, 4 * 384), BF16, ctx)
        sb("w_qup_bf", (128, 4 * D), BF16, ctx)
        sb("groupw_bf", (128, 16 * 512), BF16, ctx)
        sb("wstage3", (128, 512), F32, ctx)
        for s in range(2):
            sb("qpre%d" % s, (128, 512), F32, ctx)
            sb("qrt%d" % s, (128, 512), F32, ctx)
            sb("qhat_tm%d" % s, (128, 512), BF16, ctx)
            sb("qstat%d" % s, (128, 4), F32, ctx)
        for s in range(4):
            sb("ssb%d" % s, (128, 256), F32, ctx)
            sb("wsb%d" % s, (128, 384), F32, ctx)
            sb("nd%d" % s, (128, 4), F32, ctx)
            sb("pbf%d" % s, (128, 256), BF16, ctx)
            sb("pwb%d" % s, (128, 384), BF16, ctx)
            sb("ptb%d" % s, (128, 256), BF16, ctx)
            sb("ptw%d" % s, (128, 384), BF16, ctx)
        sb("qhatT", (128, 16 * 512), BF16, ctx)
        sb("attn_bf", (128, 16 * 512), BF16, ctx)
        sb("sstage", (128, D), BF16, ctx)
        sb("fwstage", (128, 2 * D), BF16, ctx)
        self.res["fwstage0"] = Res()
        self.res["fwstage1"] = Res()
        p.op("sp", lambda h: h.dma_start(out=self.sb["winmask"][:],
                                         in_=dr["winmask"][:, :]),
             writes=[self.r("winmask")], dma="winmask")

        def conv(dstname, col, src_ap, ncols):
            st = self.sb["wstage3"]
            p.op("sp", lambda h, s=src_ap, n=ncols: h.dma_start(
                out=st[:, :n], in_=s), writes=[self.r("wstage3")], dma="wstage3")
            p.op("dve", lambda h, d=self.sb[dstname], c=col, n=ncols:
                 h.tensor_copy(out=d[:, c:c + n], in_=st[:, :n]),
                 reads=[self.r("wstage3")], writes=[self.r(dstname)])
        for i in range(4):
            for c4 in range(4):
                conv("w_qup_bf", i * D + c4 * 512,
                     dr["w_qup"][i * 128:(i + 1) * 128, c4 * 512:(c4 + 1) * 512],
                     512)
        for gi in range(4):
            for isub in range(4):
                conv("groupw_bf", (gi * 4 + isub) * 512,
                     dr["group_w"][gi, isub * 128:(isub + 1) * 128, :], 512)

    def p3(self, u):
        p, sb, dr, r = self.p, self.sb, self.dram, self.r
        t0 = u * 512
        cp = sb["constp"]

        for it in range(16) if "qhat" not in DISABLE else []:
            tsub, hg = it // 4, it % 4
            gt0 = t0 + tsub * 128
            par = it % 2
            qpre = sb["qpre%d" % par]
            qrt = sb["qrt%d" % par]
            qstat = sb["qstat%d" % par]
            qhat_tm = sb["qhat_tm%d" % par]
            rpre, rrt, rst, rtm = (r("qpre%d" % par), r("qrt%d" % par),
                                   r("qstat%d" % par), r("qhat_tm%d" % par))
            bn0 = "pb%d" % (par * 2)
            bn1 = "pb%d" % (par * 2 + 1)
            bank = sb[bn0]

            def qmm(h, gt0=gt0, hg=hg, bank=bank):
                last = None
                for mt in range(4):
                    last = h.matmul(
                        bank[:],
                        sb["q_cT"][:, mt * T + gt0:mt * T + gt0 + 128],
                        sb["w_qup_bf"][:, mt * D + hg * 512:
                                       mt * D + (hg + 1) * 512],
                        start=(mt == 0), stop=(mt == 3))
                return last
            p.op("pe", qmm, reads=[r("q_cT"), r("w_qup_bf")], writes=[r(bn0)])
            p.op("act", lambda h, bank=bank, qpre=qpre: h.copy(
                out=qpre[:], in_=bank[:]),
                reads=[r(bn0)], writes=[rpre])
            st_idx = gt0 // 128
            co = sb["cos"][:, st_idx * 32:(st_idx + 1) * 32]
            si = sb["sin"][:, st_idx * 32:(st_idx + 1) * 32]
            cob = co.unsqueeze(1).to_broadcast([128, 4, 32])
            sib = si.unsqueeze(1).to_broadcast([128, 4, 32])
            x1 = qpre[:].rearrange("p (h c) -> p h c", h=4)[:, :, 0:32]
            x2 = qpre[:].rearrange("p (h c) -> p h c", h=4)[:, :, 32:64]
            a_ = qrt[:, 0:128].rearrange("p (h c) -> p h c", h=4)
            b_ = qrt[:, 128:256].rearrange("p (h c) -> p h c", h=4)
            c_ = qrt[:, 256:384].rearrange("p (h c) -> p h c", h=4)
            d_ = qrt[:, 384:512].rearrange("p (h c) -> p h c", h=4)
            for dst, s0, s1 in ((a_, x1, cob), (b_, x2, sib),
                                (c_, x1, sib), (d_, x2, cob)):
                p.op("dve", lambda h, dst=dst, s0=s0, s1=s1: h.tensor_tensor(
                    out=dst, in0=s0, in1=s1, op=ALU.mult),
                    reads=[rpre, r("cos"), r("sin")], writes=[rrt])
            p.op("dve", lambda h, a_=a_, b_=b_, x1=x1: h.tensor_tensor(
                out=x1, in0=a_, in1=b_, op=ALU.subtract),
                reads=[rrt], writes=[rpre])
            p.op("dve", lambda h, c_=c_, d_=d_, x2=x2: h.tensor_tensor(
                out=x2, in0=c_, in1=d_, op=ALU.add),
                reads=[rrt], writes=[rpre])
            for hh4 in range(4):
                p.op("act", lambda h, hh4=hh4, qpre=qpre, qrt=qrt, qstat=qstat:
                     h.activation(
                         out=qrt[:, 0:128],
                         in_=qpre[:, hh4 * 128:(hh4 + 1) * 128],
                         func=AF.Square, accum_out=qstat[:, hh4:hh4 + 1]),
                     reads=[rpre], writes=[rrt, rst])
            p.op("act", lambda h, qstat=qstat: h.activation(
                out=qstat[:], in_=qstat[:], func=AF.Ln,
                bias=cp[:, 0:1], scale=cp[:, 4:5]),
                reads=[rst, r("constp")], writes=[rst])
            p.op("act", lambda h, qstat=qstat: h.activation(
                out=qstat[:], in_=qstat[:], func=AF.Exp,
                bias=0.0, scale=cp[:, 2:3]),
                reads=[rst, r("constp")], writes=[rst])
            p.op("dve", lambda h, qpre=qpre, qstat=qstat, qhat_tm=qhat_tm:
                 h.tensor_tensor(
                     out=qhat_tm[:],
                     in0=qpre[:].rearrange("p (h c) -> p h c", h=4),
                     in1=qstat[:].unsqueeze(2).to_broadcast([128, 4, 128]),
                     op=ALU.mult),
                 reads=[rst, rpre], writes=[rtm])
            bank1 = sb[bn1]

            def qtr(h, bank1=bank1, qhat_tm=qhat_tm):
                last = None
                for hh4 in range(4):
                    last = h.matmul(
                        bank1[:, hh4 * 128:(hh4 + 1) * 128],
                        qhat_tm[:, hh4 * 128:(hh4 + 1) * 128],
                        sb["ident"][:], start=True, stop=True)
                return last
            p.op("pe", qtr, reads=[rtm, r("ident")], writes=[r(bn1)])
            p.op("dve", lambda h, bank1=bank1, hg=hg, tsub=tsub: h.tensor_copy(
                out=sb["qhatT"][:].rearrange(
                    "p (hd t) -> p hd t", hd=16)[:, hg * 4:hg * 4 + 4,
                                                 tsub * 128:(tsub + 1) * 128],
                in_=bank1[:].rearrange("p (hh t) -> p hh t", hh=4)),
                reads=[r(bn1)], writes=[r("qhatT")])

        for tsub in range(4) if "attn" not in DISABLE else []:
            st_idx = (t0 + tsub * 128) // 128
            nblk = st_idx // 2
            parity = st_idx % 2
            sel = parity + (2 if nblk == 0 else 0)
            nvb = min(8 * st_idx + 8, NBLK)
            ntile = 1 if nvb <= 128 else 2
            kw0 = (nblk - 1) * 256 + parity * 128
            ks_lo = max(0, -(kw0 // 128))
            for wave in range(4):
                ctxs = []
                for i in range(4):
                    hh = wave * 4 + i
                    s = i
                    ctxs.append(dict(
                        hh=hh,
                        qsl=sb["qhatT"][:, hh * 512 + tsub * 128:
                                        hh * 512 + (tsub + 1) * 128],
                        bX="pb%d" % s, bY="pb%d" % (4 + s),
                        ssb=sb["ssb%d" % s], wsb=sb["wsb%d" % s],
                        nd=sb["nd%d" % s], pbf=sb["pbf%d" % s],
                        pwb=sb["pwb%d" % s], ptb=sb["ptb%d" % s],
                        ptw=sb["ptw%d" % s],
                        rs=r("ssb%d" % s), rw=r("wsb%d" % s),
                        rnd=r("nd%d" % s), rpb=r("pbf%d" % s),
                        rpw=r("pwb%d" % s), rtb=r("ptb%d" % s),
                        rtw=r("ptw%d" % s)))
                for c in ctxs:
                    p.op("pe", lambda h, c=c, nvb=nvb: h.matmul(
                        sb[c["bX"]][:, 0:nvb], c["qsl"], sb["skv_ss"][:, 0:nvb],
                        start=True, stop=True),
                        reads=[r("qhatT"), r("skv_ss")], writes=[r(c["bX"])])
                for c in ctxs:
                    if nblk == 0:
                        ndum = 256 - parity * 128

                        def ws0(h, c=c, ndum=ndum):
                            h.matmul(sb[c["bY"]][:, 0:ndum], c["qsl"],
                                     sb["khatT"][:, 0:ndum],
                                     start=True, stop=True)
                            return h.matmul(sb[c["bY"]][:, ndum:384], c["qsl"],
                                            sb["khatT"][:, 0:384 - ndum],
                                            start=True, stop=True)
                        p.op("pe", ws0, reads=[r("qhatT"), r("khatT")],
                             writes=[r(c["bY"])])
                    else:
                        p.op("pe", lambda h, c=c, kw0=kw0: h.matmul(
                            sb[c["bY"]][:, 0:384], c["qsl"],
                            sb["khatT"][:, kw0:kw0 + 384],
                            start=True, stop=True),
                            reads=[r("qhatT"), r("khatT")], writes=[r(c["bY"])])
                for c in ctxs:
                    p.op("dve", lambda h, c=c, st_idx=st_idx, nvb=nvb:
                         h.tensor_tensor(
                             out=c["ssb"][:, 0:nvb], in0=sb[c["bX"]][:, 0:nvb],
                             in1=sb["mask_sb"][:, st_idx * NBLK:
                                               st_idx * NBLK + nvb],
                             op=ALU.add),
                         reads=[r(c["bX"]), r("mask_sb")], writes=[c["rs"]])
                for c in ctxs:
                    p.op("dve", lambda h, c=c, nvb=nvb: h.reduce_max(
                        out=c["nd"][:, 0:1], in_=c["ssb"][:, 0:nvb], axis=AX.X,
                        negate=True),
                        reads=[c["rs"]], writes=[c["rnd"]])
                for c in ctxs:
                    p.op("act", lambda h, c=c, nvb=nvb: h.activation(
                        out=c["ssb"][:, 0:nvb], in_=c["ssb"][:, 0:nvb],
                        func=AF.Exp, bias=c["nd"][:, 0:1], scale=1.0,
                        accum_out=c["nd"][:, 1:2]),
                        reads=[c["rs"], c["rnd"]], writes=[c["rs"], c["rnd"]])
                for c in ctxs:
                    p.op("dve", lambda h, c=c, sel=sel: h.tensor_tensor(
                        out=c["wsb"][:, 0:384], in0=sb[c["bY"]][:, 0:384],
                        in1=sb["winmask"][:, sel * 384:(sel + 1) * 384],
                        op=ALU.add),
                        reads=[r(c["bY"]), r("winmask")], writes=[c["rw"]])
                for c in ctxs:
                    p.op("dve", lambda h, c=c: h.reduce_max(
                        out=c["nd"][:, 2:3], in_=c["wsb"][:, 0:384], axis=AX.X,
                        negate=True),
                        reads=[c["rw"]], writes=[c["rnd"]])
                for c in ctxs:
                    p.op("act", lambda h, c=c: h.activation(
                        out=c["wsb"][:, 0:384], in_=c["wsb"][:, 0:384],
                        func=AF.Exp, bias=c["nd"][:, 2:3], scale=1.0,
                        accum_out=c["nd"][:, 3:4]),
                        reads=[c["rw"], c["rnd"]], writes=[c["rw"], c["rnd"]])
                for c in ctxs:
                    p.op("dve", lambda h, c=c: h.reciprocal(
                        out=c["nd"][:, 1:2], in_=c["nd"][:, 1:2]),
                        reads=[c["rnd"]], writes=[c["rnd"]])
                    p.op("dve", lambda h, c=c, nvb=nvb: h.tensor_scalar(
                        out=c["pbf"][:, 0:nvb], in0=c["ssb"][:, 0:nvb],
                        scalar1=c["nd"][:, 1:2], scalar2=None, op0=ALU.mult),
                        reads=[c["rs"], c["rnd"]], writes=[c["rpb"]])
                    if st_idx == 0:
                        p.op("dve", lambda h, c=c: h.memset(
                            c["pbf"][0:16, :], 0.0),
                            writes=[c["rpb"]], strict=True)
                for c in ctxs:
                    def ptr(h, c=c, nvb=nvb, ntile=ntile):
                        last = None
                        for i in range(ntile):
                            m = min(128, nvb - i * 128)
                            last = h.matmul(
                                sb[c["bX"]][0:m, 256 + i * 128:256 + (i + 1) * 128],
                                c["pbf"][:, i * 128:i * 128 + m],
                                sb["ident"][:], start=True, stop=True)
                        return last
                    p.op("pe", ptr, reads=[c["rpb"], r("ident")],
                         writes=[r(c["bX"])])
                for c in ctxs:
                    for i in range(ntile):
                        m = min(128, nvb - i * 128)
                        p.op("act", lambda h, c=c, i=i, m=m: h.copy(
                            out=c["ptb"][0:m, i * 128:(i + 1) * 128],
                            in_=sb[c["bX"]][0:m, 256 + i * 128:256 + (i + 1) * 128]),
                            reads=[r(c["bX"])], writes=[c["rtb"]])
                for c in ctxs:
                    p.op("dve", lambda h, c=c: h.reciprocal(
                        out=c["nd"][:, 3:4], in_=c["nd"][:, 3:4]),
                        reads=[c["rnd"]], writes=[c["rnd"]])
                    p.op("dve", lambda h, c=c: h.tensor_scalar(
                        out=c["pwb"][:, 0:384], in0=c["wsb"][:, 0:384],
                        scalar1=c["nd"][:, 3:4], scalar2=None, op0=ALU.mult),
                        reads=[c["rw"], c["rnd"]], writes=[c["rpw"]])
                for c in ctxs:
                    def pvs(h, c=c, nvb=nvb, ntile=ntile):
                        last = None
                        for i in range(ntile):
                            cnt = min(128, nvb - i * 128)
                            last = h.matmul(sb[c["bX"]][:, 256:384],
                                            sb["skv_pv"][0:cnt,
                                                         i * 128:(i + 1) * 128],
                                            c["ptb"][0:cnt,
                                                     i * 128:(i + 1) * 128],
                                            start=(i == 0), stop=False)
                        return last
                    p.op("pe", pvs, reads=[r("skv_pv"), c["rtb"]],
                         writes=[r(c["bX"])])
                for c in ctxs:
                    def wtr(h, c=c):
                        last = None
                        for ks in range(3):
                            last = h.matmul(sb[c["bY"]][:, ks * 128:(ks + 1) * 128],
                                            c["pwb"][:, ks * 128:(ks + 1) * 128],
                                            sb["ident"][:], start=True, stop=True)
                        return last
                    p.op("pe", wtr, reads=[c["rpw"], r("ident")],
                         writes=[r(c["bY"])])
                for c in ctxs:
                    p.op("act", lambda h, c=c: h.copy(
                        out=c["ptw"][:, 0:384], in_=sb[c["bY"]][:, 0:384]),
                        reads=[r(c["bY"])], writes=[c["rtw"]])
                for c in ctxs:
                    def pvw(h, c=c, ks_lo=ks_lo, kw0=kw0):
                        last = None
                        for ks in range(ks_lo, 3):
                            ktile = kw0 // 128 + ks
                            last = h.matmul(sb[c["bX"]][:, 256:384],
                                            sb["v_sb"][:, ktile * 128:(ktile + 1) * 128],
                                            c["ptw"][:, ks * 128:(ks + 1) * 128],
                                            start=False, stop=(ks == 2))
                        return last
                    p.op("pe", pvw, reads=[r("v_sb"), c["rtw"]],
                         writes=[r(c["bX"])])
                for c in ctxs:
                    p.op("act", lambda h, c=c, tsub=tsub: h.copy(
                        out=sb["attn_bf"][:, c["hh"] * 512 + tsub * 128:
                                          c["hh"] * 512 + (tsub + 1) * 128],
                        in_=sb[c["bX"]][:, 256:384]),
                        reads=[r(c["bX"])], writes=[r("attn_bf")])

        for g in range(4) if "group" not in DISABLE else []:
            for osub in range(4):
                bn = "pb%d" % (4 + osub)
                bank = sb[bn]

                def gmm(h, g=g, osub=osub, bank=bank):
                    last = None
                    for isub in range(4):
                        last = h.matmul(
                            bank[:],
                            sb["groupw_bf"][:, (g * 4 + isub) * 512 + osub * 128:
                                            (g * 4 + isub) * 512 + (osub + 1) * 128],
                            sb["attn_bf"][:, (g * 4 + isub) * 512:
                                          (g * 4 + isub + 1) * 512],
                            start=(isub == 0), stop=(isub == 3))
                    return last
                p.op("pe", gmm, reads=[r("groupw_bf"), r("attn_bf")],
                     writes=[r(bn)])
            for osub in range(4):
                bn = "pb%d" % (4 + osub)
                bank = sb[bn]
                p.op("dve", lambda h, g=g, osub=osub, bank=bank: h.tensor_copy(
                    out=sb["attn_bf"][:, (g * 4 + osub) * 512:
                                      (g * 4 + osub + 1) * 512], in_=bank[:]),
                    reads=[r(bn)], writes=[r("attn_bf")])

        for pair in range(2) if "final" not in DISABLE else []:
            for kt in range(DT):
                slot = kt % 2
                fw = sb["fwstage"][:, slot * D:(slot + 1) * D]
                p.op("sp", lambda h, fw=fw, kt=kt: h.dma_start(
                    out=fw, in_=dr["fwT_scr"][kt, :, :]),
                    writes=[r("fwstage%d" % slot)], dma="fwstage%d" % slot)
                for ts2 in range(2):
                    tsub = pair * 2 + ts2
                    bn = "pb%d" % (ts2 * 4)
                    for oc in range(4):
                        bno = "pb%d" % (ts2 * 4 + oc)
                        p.op("pe", lambda h, kt=kt, tsub=tsub, oc=oc, fw=fw,
                             bno=bno: h.matmul(
                                 sb[bno][:],
                                 sb["attn_bf"][:, kt * 512 + tsub * 128:
                                               kt * 512 + (tsub + 1) * 128],
                                 fw[:, oc * 512:(oc + 1) * 512],
                                 start=(kt == 0), stop=(kt == DT - 1)),
                             reads=[r("attn_bf"), r("fwstage%d" % (kt % 2))],
                             writes=[r(bno)])
            for ts2 in range(2):
                tsub = pair * 2 + ts2
                st_idx = (t0 + tsub * 128) // 128
                for oc in range(4):
                    bno = "pb%d" % (ts2 * 4 + oc)
                    p.op("dve", lambda h, oc=oc, bno=bno: h.tensor_copy(
                        out=sb["sstage"][:, oc * 512:(oc + 1) * 512],
                        in_=sb[bno][:]),
                        reads=[r(bno)], writes=[r("sstage")])
                p.op("pool", lambda h, st=st_idx: h.dma_start(
                    out=dr["s_scr"][st, :, :], in_=sb["sstage"][:]),
                    reads=[r("sstage")], dma="s_sp")

    def p05_alloc(self, ctx):
        sb = self.sbuf
        sb("wq_i32", (128, 2 * 2048), I32, ctx)
        sb("wq_bf", (128, 2 * 2048), BF16, ctx)
        sb("tev", (128, 2 * 1024), BF16, ctx)
        for nm in ("wq_i32", "wq_bf", "tev"):
            self.res[nm + "_1"] = Res()
        self._units = ([("g", j, 0) for j in range(HT)]
                       + [("u", j, 0) for j in range(HT)]
                       + [("d", ot, part) for ot in range(DT) for part in range(3)])
        self._ui = 0

    def p05_chunk(self, n_units):
        p, sb, dr, r = self.p, self.sb, self.dram, self.r
        for _ in range(n_units):
            if self._ui >= len(self._units):
                return
            kind, j, part = self._units[self._ui]
            usl = self._ui % 2
            self._ui += 1
            i32 = sb["wq_i32"][:, usl * 2048:(usl + 1) * 2048]
            wbf = sb["wq_bf"][:, usl * 2048:(usl + 1) * 2048]
            tev = sb["tev"][:, usl * 1024:(usl + 1) * 1024]
            ri32 = r("wq_i32") if usl == 0 else r("wq_i32_1")
            rbf = r("wq_bf") if usl == 0 else r("wq_bf_1")
            rtv = r("tev") if usl == 0 else r("tev_1")
            if kind == "g" or kind == "u":
                srcd = dr["wq_gate"] if kind == "g" else dr["wq_up"]
                srcap = srcd[j * 128:(j + 1) * 128, :]
                ncol = 2048
            else:
                ncol = 2048 if part < 2 else 1536
                srcap = dr["wq_down"][j * 128:(j + 1) * 128,
                                      part * 2048:part * 2048 + ncol]
            p.op("sp", lambda h, s=srcap, i32=i32, ncol=ncol: h.dma_start(
                out=i32[:, :ncol], in_=s), writes=[ri32], dma="wq_ld%d" % usl)
            p.op("dve", lambda h, i32=i32, wbf=wbf, ncol=ncol: h.tensor_scalar(
                out=wbf[:, :ncol], in0=i32[:, :ncol], scalar1=-128.0,
                scalar2=None, op0=ALU.add),
                reads=[ri32], writes=[rbf])
            ngrp = ncol // 512
            for grp in range(ngrp):
                bn = "pb%d" % (6 + grp % 2)
                bank = sb[bn]

                def tmm(h, wbf=wbf, grp=grp, bank=bank):
                    last = None
                    for s4 in range(4):
                        dt = grp * 4 + s4
                        last = h.matmul(bank[:, s4 * 128:(s4 + 1) * 128],
                                        wbf[:, dt * 128:(dt + 1) * 128],
                                        sb["ident"][:], start=True, stop=True)
                    return last
                p.op("pe", tmm, reads=[rbf, r("ident")], writes=[r(bn)])
                half = grp % 2
                tv = tev[:, half * 512:(half + 1) * 512]
                if kind == "d":
                    sdsl = sb["sdown"][:, j * 128:(j + 1) * 128]
                    p.op("dve", lambda h, tv=tv, bank=bank, sdsl=sdsl:
                         h.tensor_tensor(
                             out=tv.rearrange("p (s o) -> p s o", s=4),
                             in0=bank[:].rearrange("p (s o) -> p s o", s=4),
                             in1=sdsl.unsqueeze(1).to_broadcast([128, 4, 128]),
                             op=ALU.mult),
                         reads=[r(bn), r("sdown")], writes=[rtv])
                else:
                    p.op("act", lambda h, tv=tv, bank=bank: h.copy(
                        out=tv, in_=bank[:]),
                        reads=[r(bn)], writes=[rtv])
                if half == 1 or grp == ngrp - 1:
                    nh = half + 1
                    g0 = grp - half
                    if kind == "g" or kind == "u":
                        dst = dr["wgT_scr" if kind == "g" else "wuT_scr"]
                        p.op("pool", lambda h, tev=tev, dst=dst, j=j, g0=g0,
                             nh=nh: h.dma_start(
                                 out=dst[j, :, g0 * 512:(g0 + nh) * 512],
                                 in_=tev[:, :nh * 512]),
                             reads=[rtv], dma="tev_st%d" % usl)
                    else:
                        base_ht = part * 16 + g0 * 4
                        p.op("pool", lambda h, tev=tev, j=j, b=base_ht, nh=nh:
                             h.dma_start(
                                 out=dr["wdT_scr"][b:b + nh * 4, :,
                                                   j * 128:(j + 1) * 128].rearrange(
                                     "ht p o -> p ht o"),
                                 in_=tev[:, :nh * 512].rearrange(
                                     "p (ht o) -> p ht o", ht=nh * 4)),
                             reads=[rtv], dma="tev_st%d" % usl)

    def p4_alloc(self, ctx):
        p, dr = self.p, self.dram
        sb = self.sbuf
        sb("sgate", (128, HT), F32, ctx)
        sb("sup", (128, HT), F32, ctx)
        sb("xnTs", (128, DT * 1024), BF16, ctx)
        for s in range(3):
            sb("wg_t%d" % s, (128, DT * 128), BF16, ctx)
            sb("wu_t%d" % s, (128, DT * 128), BF16, ctx)
            sb("wd_t%d" % s, (128, D), BF16, ctx)
        sb("hact", (128, HT * 1024), BF16, ctx)
        sb("silu_s", (128, 2 * 512), BF16, ctx)
        self.res["silu_s1"] = Res()
        sb("xres", (128, D), F32, ctx)
        sb("sres", (128, D), BF16, ctx)
        sb("gbuf", (128, D), BF16, ctx)
        sb("obuf", (128, D), F32, ctx)
        for nm in ("sgate", "sup"):
            p.op("sp", lambda h, d=self.sb[nm], s=dr[nm]: h.dma_start(
                out=d[:], in_=s[:, :]), writes=[self.r(nm)], dma=nm)

    def p4(self, v):
        p, sb, dr, r = self.p, self.sb, self.dram, self.r
        t0 = v * 1024
        p.op("pool", lambda h: h.dma_start(
            out=sb["xnTs"][:].rearrange("p (dt t) -> p dt t", dt=DT),
            in_=dr["xnT_scr"][:, :, t0:t0 + 1024].rearrange("dt p t -> p dt t")),
            writes=[r("xnTs")], dma="xnTs")

        for j in range(HT):
            sl = j % 3
            wg = sb["wg_t%d" % sl]
            wu = sb["wu_t%d" % sl]
            p.op("sp", lambda h, wg=wg, j=j: h.dma_start(
                out=wg[:], in_=dr["wgT_scr"][j, :, :]),
                writes=[r("wg_t%d" % sl)], dma="wg_t%d" % sl)
            p.op("sp", lambda h, wu=wu, j=j: h.dma_start(
                out=wu[:], in_=dr["wuT_scr"][j, :, :]),
                writes=[r("wu_t%d" % sl)], dma="wu_t%d" % sl)
            bg = "pb%d" % (2 * (j % 2))
            bu = "pb%d" % (2 * (j % 2) + 1)
            silu_res = "silu_s" if j % 2 == 0 else "silu_s1"
            ssl = sb["silu_s"][:, (j % 2) * 512:(j % 2 + 1) * 512]
            for c in range(2):
                for which, wt, bn in (("g", wg, bg), ("u", wu, bu)):
                    bank = sb[bn]

                    def gmm(h, wt=wt, c=c, bank=bank):
                        last = None
                        for dt in range(DT):
                            last = h.matmul(
                                bank[:],
                                wt[:, dt * 128:(dt + 1) * 128],
                                sb["xnTs"][:, dt * 1024 + c * 512:
                                           dt * 1024 + c * 512 + 512],
                                start=(dt == 0), stop=(dt == DT - 1))
                        return last
                    p.op("pe", gmm, reads=[r("wg_t%d" % sl) if which == "g"
                                           else r("wu_t%d" % sl), r("xnTs")],
                         writes=[r(bn)])
                p.op("act", lambda h, j=j, bg=bg, ssl=ssl: h.activation(
                    out=ssl, in_=sb[bg][:], func=AF.Silu,
                    bias=0.0, scale=sb["sgate"][:, j:j + 1]),
                    reads=[r(bg), r("sgate")], writes=[r(silu_res)])
                p.op("dve", lambda h, j=j, c=c, bu=bu, ssl=ssl:
                     h.scalar_tensor_tensor(
                         out=sb["hact"][:, j * 1024 + c * 512:
                                        j * 1024 + c * 512 + 512],
                         in0=sb[bu][:], scalar=sb["sup"][:, j:j + 1],
                         in1=ssl, op0=ALU.mult, op1=ALU.mult),
                     reads=[r(bu), r("sup"), r(silu_res)], writes=[r("hact")])

        for tg in range(4):
            for j in range(HT):
                sl = j % 3
                wd = sb["wd_t%d" % sl]
                p.op("sp", lambda h, wd=wd, j=j: h.dma_start(
                    out=wd[:], in_=dr["wdT_scr"][j, :, :]),
                    writes=[r("wd_t%d" % sl)], dma="wd_t%d" % sl)
                for ts2 in range(2):
                    for oc in range(4):
                        bno = "pb%d" % (ts2 * 4 + oc)
                        p.op("pe", lambda h, j=j, tg=tg, ts2=ts2, oc=oc, wd=wd,
                             bno=bno: h.matmul(
                                 sb[bno][:],
                                 sb["hact"][:, j * 1024 + tg * 256 + ts2 * 128:
                                            j * 1024 + tg * 256 + (ts2 + 1) * 128],
                                 wd[:, oc * 512:(oc + 1) * 512],
                                 start=(j == 0), stop=(j == HT - 1)),
                             reads=[r("hact"), r("wd_t%d" % sl)],
                             writes=[r(bno)])
            for ts2 in range(2):
                trow = t0 + tg * 256 + ts2 * 128
                st_idx = trow // 128
                for oc in range(4):
                    bno = "pb%d" % (ts2 * 4 + oc)
                    p.op("act", lambda h, oc=oc, bno=bno: h.activation(
                        out=sb["gbuf"][:, oc * 512:(oc + 1) * 512],
                        in_=sb[bno][:], func=AF.Silu),
                        reads=[r(bno)], writes=[r("gbuf")])
                p.op("pool", lambda h, trow=trow: h.dma_start(
                    out=sb["xres"][:], in_=dr["x"][trow:trow + 128, :]),
                    writes=[r("xres")], dma="xres")
                p.op("pool", lambda h, st=st_idx: h.dma_start(
                    out=sb["sres"][:], in_=dr["s_scr"][st, :, :]),
                    writes=[r("sres")], dma="sres")
                p.op("dve", lambda h: h.tensor_tensor(
                    out=sb["obuf"][:], in0=sb["sres"][:], in1=sb["gbuf"][:],
                    op=ALU.mult),
                    reads=[r("sres"), r("gbuf")], writes=[r("obuf")])
                p.op("dve", lambda h: h.tensor_tensor(
                    out=sb["obuf"][:], in0=sb["obuf"][:], in1=sb["xres"][:],
                    op=ALU.add),
                    reads=[r("xres")], writes=[r("obuf")], strict=True)
                p.op("pool", lambda h, trow=trow: h.dma_start(
                    out=dr["y"][trow:trow + 128, :], in_=sb["obuf"][:]),
                    reads=[r("obuf")], dma="y_st")

    def emit_debug(self):
        sb = self.sb
        self.dbg("q_cT", lambda: sb["q_cT"][:], (128, 4 * T), BF16, ["q_cT"])
        self.dbg("khatT", lambda: sb["khatT"][:], (128, T), BF16, ["khatT"])
        self.dbg("v_sb", lambda: sb["v_sb"][:], (128, NT * 128), BF16, ["v_sb"])
        self.dbg("qiT", lambda: sb["qiT"][:], (128, 2 * T), BF16, ["qiT"])
        self.dbg("wih", lambda: sb["wih"][:], (128, NT * 4), F32, ["wih"])
        self.dbg("ckvT", lambda: sb["ckvT"][:], (128, NBLK), F32, ["ckvT"])
        self.dbg("kcompT", lambda: sb["kcompT"][:64, :], (64, NBLK), F32,
                 ["kcompT"])
        self.dbg("skv_ss", lambda: sb["skv_ss"][:], (128, NBLK), BF16,
                 ["skv_ss"])
        self.dbg("skv_pv", lambda: sb["skv_pv"][:], (128, 2 * 128), BF16,
                 ["skv_pv"])
        if "xnT" in self.dbg_names:
            d = self.dout("dbg_xnT", (DT, 128, T), BF16)
            self.p.op("pool", lambda h, d=d: h.dma_start(
                out=d[:, :, :], in_=self.dram["xnT_scr"][:, :, :]), dma="dbg")
        if "wgT" in self.dbg_names:
            d = self.dout("dbg_wgT", (HT, 128, DT * 128), BF16)
            self.p.op("pool", lambda h, d=d: h.dma_start(
                out=d[:, :, :], in_=self.dram["wgT_scr"][:, :, :]),
                dma="dbg")
        if "wdT" in self.dbg_names:
            d = self.dout("dbg_wdT", (HT, 128, D), BF16)
            self.p.op("pool", lambda h, d=d: h.dma_start(
                out=d[:, :, :], in_=self.dram["wdT_scr"][:, :, :]), dma="dbg")
        if "mask" in self.dbg_names:
            d = self.dout("dbg_mask", (128, NT * NBLK), BF16)
            self.p.op("pool", lambda h, d=d: h.dma_start(
                out=d[:, :], in_=sb["mask_sb"][:]),
                reads=[self.r("mask_sb")], dma="dbg")
        for nm in ("ssb0", "wsb0", "pbf0", "pwb0", "ptb0", "ptw0", "tmps0"):
            if nm in self.dbg_names:
                hnd = self.sb[nm]
                shp = (128, hnd.shape[1])
                dt = hnd.dtype if hasattr(hnd, "dtype") else None
                d = self.dout("dbg_" + nm, shp, dt)
                self.p.op("pool", lambda h, d=d, hnd=hnd: h.dma_start(
                    out=d[:, :], in_=hnd[:]),
                    reads=[self.r(nm)], dma="dbg")
        if "qhatT" in self.dbg_names:
            d = self.dout("dbg_qhatT", (128, 16 * 512), BF16)
            self.p.op("pool", lambda h, d=d: h.dma_start(
                out=d[:, :], in_=self.sb["qhatT"][:]),
                reads=[self.r("qhatT")], dma="dbg")
        if "attn_bf" in self.dbg_names:
            d = self.dout("dbg_attn", (128, 16 * 512), BF16)
            self.p.op("pool", lambda h, d=d: h.dma_start(
                out=d[:, :], in_=self.sb["attn_bf"][:]),
                reads=[self.r("attn_bf")], dma="dbg")
        if "s" in self.dbg_names:
            d = self.dout("dbg_s", (NT, 128, D), BF16)
            self.p.op("pool", lambda h, d=d: h.dma_start(
                out=d[:, :, :], in_=self.dram["s_scr"][:, :, :]), dma="dbg")

    def build(self):
        p = self.p
        self.p0_globals()
        with ExitStack() as cA:
            self.p0_allocA(cA)
            with ExitStack() as cAB:
                self.sbuf("qiT", (128, 2 * T), BF16, cAB)
                self.sbuf("wih", (128, NT * 4), F32, cAB)
                with ExitStack() as cB:
                    self.p0_weights(cB)
                    self.p1_alloc(cB)
                    p.barrier()
                    for u in range(NSUP):
                        self.p1(u)
                    self.p1_post()
                    p.barrier()
                with ExitStack() as cC:
                    self.p2_alloc(cC)
                    self.p05_alloc(cC)
                    for st in range(NT):
                        self.p2(st)
                        if st < DT:
                            self.p2_fwconv(st)
                        self.p05_chunk(5)
                    self.p05_chunk(10 ** 9)
                    p.barrier()
            with ExitStack() as cD:
                self.p3_alloc(cD)
                for u in range(NSUP):
                    self.p3(u)
                p.barrier()
            self.emit_debug()
            p.barrier()
        with ExitStack() as cE:
            self.p4_alloc(cE)
            for v in range(T // 1024):
                self.p4(v)
            p.barrier()
        block = self.ctx.enter_context(self.nc.Block())
        p.replay(block)
        self.ctx.close()
        return self.nc


_BUILD_CACHE = {}


def build_nc(dbg_names=()):
    key = tuple(sorted(dbg_names))
    if key not in _BUILD_CACHE:
        b = Builder(dbg_names)
        _BUILD_CACHE[key] = (b.build(), b)
    return _BUILD_CACHE[key]


def kernel(**inputs):
    shared, per_core = host_prep(inputs)
    nc, b = build_nc()
    in_maps = []
    for c in range(8):
        m = dict(shared)
        m.update(per_core[c])
        in_maps.append(m)
    res = run_bass_kernel_spmd(nc, in_maps, core_ids=list(range(8)))
    out = np.stack([res.results[c]["y"] for c in range(8)], axis=0)
    return out.astype(np.float32)
